# version 01
# baseline (speedup 1.0000x reference)
import numpy as np
import ml_dtypes
import concourse.bass as bass
import concourse.mybir as mybir
from concourse.bass_utils import run_bass_kernel_spmd

F32 = mybir.dt.float32
BF16 = mybir.dt.bfloat16
AF = mybir.ActivationFunctionType
ALU = mybir.AluOpType
AX = mybir.AxisListType
BF = ml_dtypes.bfloat16

D = 1024
NCHUNK = 8
TOK = 2048
TOKH = TOK + 128
NH = 16
HD = 64
NPAIR = 8
WIN = 64


class Ctr:
    def __init__(self, nc, stack, name, step=1):
        self.sem = stack.enter_context(nc.semaphore(name))
        self.n = 0
        self.step = step

    def tick(self, ins, k=None):
        k = self.step if k is None else k
        ins.then_inc(self.sem, k)
        self.n += k
        return self.n

    def wait(self, eng, n=None):
        eng.wait_ge(self.sem, self.n if n is None else n)


def fence(engines, ctrs):
    for e in engines:
        for c in ctrs:
            if c.n > 0:
                c.wait(e)



def emit_ln_group(env, ysb, ysq, outsb, out_dram, off, out_w=512):
    (G, T, V, A, Y) = env["eng"]
    (cG, cY, cP, cT, cV, cA) = env["ctr"]
    PF = env["PF"]
    onescol_f, ones1 = env["onescol_f"], env["ones1"]
    st_sb, mu, m2, var, rstd, nmur = env["st_sb"], env["mu"], env["m2"], env["var"], env["rstd"], env["nmur"]
    rstd_bf, nmur_bf, bcln = env["rstd_bf"], env["nmur_bf"], env["bcln"]
    AFL, ALUL, w = AF, ALU, out_w
    fence([T, A], [cV])
    for dc in range(8):
        ins = T.matmul(PF[2][0:1, 0:w], onescol_f[:, 0:1], ysb[:, dc, 0:w],
                       start=(dc == 0), stop=(dc == 7))
    cT.tick(ins)
    for dc in range(8):
        a_sq = cA.tick(A.activation(ysq[:, 0:w], ysb[:, dc, 0:w], AFL.Square))
        T.wait_ge(cA.sem, a_sq)
        ins = T.matmul(PF[3][0:1, 0:w], onescol_f[:, 0:1], ysq[:, 0:w],
                       start=(dc == 0), stop=(dc == 7))
        t_s2 = cT.tick(ins)
        if dc < 7:
            A.wait_ge(cT.sem, t_s2)
    V.wait_ge(cT.sem, cT.n)
    cV.tick(V.tensor_copy(st_sb[0:1, 0:w], PF[2][0:1, 0:w]))
    cV.tick(V.tensor_copy(st_sb[0:1, 512:512 + w], PF[3][0:1, 0:w]))
    cV.wait(V)
    cV.tick(V.tensor_single_scalar(mu[0:1, 0:w], st_sb[0:1, 0:w], 1.0 / 1024, ALUL.mult))
    cV.tick(V.tensor_single_scalar(m2[0:1, 0:w], st_sb[0:1, 512:512 + w], 1.0 / 1024, ALUL.mult))
    cV.wait(V)
    cV.tick(V.tensor_mul(var[0:1, 0:w], mu[0:1, 0:w], mu[0:1, 0:w]))
    cV.wait(V)
    cV.tick(V.tensor_sub(var[0:1, 0:w], m2[0:1, 0:w], var[0:1, 0:w]))
    cV.wait(V)
    cV.tick(V.tensor_single_scalar(var[0:1, 0:w], var[0:1, 0:w], 1e-5, ALUL.add))
    cV.wait(A)
    cA.tick(A.activation(var[0:1, 0:w], var[0:1, 0:w], AFL.Ln))
    cA.wait(A)
    a_r = cA.tick(A.activation(rstd[0:1, 0:w], var[0:1, 0:w], AFL.Exp, scale=-0.5))
    V.wait_ge(cA.sem, a_r)
    cV.tick(V.tensor_mul(nmur[0:1, 0:w], mu[0:1, 0:w], rstd[0:1, 0:w]))
    cV.wait(V)
    cV.tick(V.tensor_single_scalar(nmur[0:1, 0:w], nmur[0:1, 0:w], -1.0, ALUL.mult))
    cV.tick(V.tensor_copy(rstd_bf[0:1, 0:w], rstd[0:1, 0:w]))
    cV.wait(V)
    cV.tick(V.tensor_copy(nmur_bf[0:1, 0:w], nmur[0:1, 0:w]))
    cV.wait(T)
    cT.tick(T.matmul(PF[4][:, 0:w], ones1[0:1, :], rstd_bf[0:1, 0:w], start=True, stop=True))
    t_b2 = cT.tick(T.matmul(PF[5][:, 0:w], ones1[0:1, :], nmur_bf[0:1, 0:w], start=True, stop=True))
    A.wait_ge(cT.sem, t_b2)
    cA.tick(A.activation(bcln[:, 0, 0:w], PF[4][:, 0:w], AFL.Copy))
    a_bc = cA.tick(A.activation(bcln[:, 1, 0:w], PF[5][:, 0:w], AFL.Copy))
    V.wait_ge(cA.sem, a_bc)
    for dc in range(8):
        cV.tick(V.tensor_mul(ysq[:, 0:w], ysb[:, dc, 0:w], bcln[:, 0, 0:w]))
        V.wait_ge(cV.sem, cV.n)
        v_ap = cV.tick(V.tensor_add(outsb[:, dc, 0:w], ysq[:, 0:w], bcln[:, 1, 0:w]))
        Y.wait_ge(cV.sem, v_ap)
        cY.tick(Y.dma_start(out=out_dram[128 * dc:128 * (dc + 1), off:off + w],
                            in_=outsb[:, dc, 0:w]))
    fence([T, A], [cV, cY])



def build_launch_A(nc, stack):
    G, T, V, A, Y = nc.gpsimd, nc.tensor, nc.vector, nc.scalar, nc.sync
    x0T = nc.dram_tensor("x0T", [D, TOKH], F32, kind="ExternalInput").ap()
    validT = nc.dram_tensor("validT", [128, 17], F32, kind="ExternalInput").ap()
    Wq = nc.dram_tensor("Wq", [D, D], F32, kind="ExternalInput").ap()
    Wk = nc.dram_tensor("Wk", [D, D], F32, kind="ExternalInput").ap()
    Wv = nc.dram_tensor("Wv", [D, D], F32, kind="ExternalInput").ap()
    Wo = nc.dram_tensor("Wo", [D, D], F32, kind="ExternalInput").ap()
    y1 = nc.dram_tensor("y1", [D, TOK], BF16, kind="ExternalOutput").ap()

    sb = lambda name, shape, dt=BF16: stack.enter_context(nc.sbuf_tensor(name, shape, dt))
    ps = lambda name, shape, dt=F32: stack.enter_context(nc.psum_tensor(name, shape, dt))

    x0b = sb("x0b", [128, NCHUNK, TOKH])
    vld = sb("vld", [128, 17], F32)
    wraw = sb("wraw", [128, D])
    wT = {w: sb(f"w{w}T", [128, 8, 128]) for w in "qkv"}
    qT = sb("qT", [128, TOK])
    kT = sb("kT", [128, TOKH])
    vT = sb("vT", [128, TOKH])
    vt = sb("vt", [128, 2, 17, 65])
    sq = sb("sq", [128, TOKH])
    pt = sb("pt", [128, 2, 256])
    iot = sb("iot", [128, 128], F32)
    icol = sb("icol", [128, 128], F32)
    irow = sb("irow", [128, 1], F32)
    tmpm = sb("tmpm", [128, 2, 128], F32)
    maskAB = sb("maskAB", [128, 256])
    id128 = sb("id128", [128, 128])
    id64d = sb("id64d", [128, 64])
    irow64 = sb("irow64", [128, 1], F32)
    ones1 = sb("ones1", [1, 128])
    onescol_f = sb("onescol_f", [128, 1], F32)
    onescol_b = sb("onescol_b", [128, 1])
    nmax = sb("nmax", [1, 32], F32)
    bnd = sb("bnd", [1, 8], F32)
    bndbf = sb("bndbf", [1, 1])
    boundc = sb("boundc", [128, 2], F32)
    lr = sb("lr", [1, 256], F32)
    rbf = sb("rbf", [1, 256])
    bc_sb = sb("bc_sb", [128, 256], F32)
    attnT = sb("attnT", [128, NPAIR, TOK])
    woT = sb("woT", [128, 8, 8, 128])
    ysb = sb("ysb", [128, NCHUNK, 512], F32)
    ysq = sb("ysq", [128, 512], F32)
    st_sb = sb("st_sb", [1, 1024], F32)
    mu = sb("mu", [1, 512], F32)
    m2 = sb("m2", [1, 512], F32)
    var = sb("var", [1, 512], F32)
    rstd = sb("rstd", [1, 512], F32)
    nmur = sb("nmur", [1, 512], F32)
    rstd_bf = sb("rstd_bf", [1, 512])
    nmur_bf = sb("nmur_bf", [1, 512])
    bcln = sb("bcln", [128, 2, 512])
    y1sb = sb("y1sb", [128, NCHUNK, 512])

    PB0 = ps("PB0", [128, 1024], BF16)
    PB1 = ps("PB1", [128, 1024], BF16)
    PF = [ps(f"PF{i}", [128, 512], F32) for i in range(6)]

    cG = Ctr(nc, stack, "cG", 16)
    cY = Ctr(nc, stack, "cY", 16)
    cP = Ctr(nc, stack, "cP")
    cT = Ctr(nc, stack, "cT")
    cV = Ctr(nc, stack, "cV")
    cA = Ctr(nc, stack, "cA")
    ALL = [cG, cY, cP, cT, cV, cA]
    ENG = [G, T, V, A, Y]

    MDL = dict(max_dma_last_dim=8192)
    env = {"eng": (G, T, V, A, Y), "ctr": (cG, cY, cP, cT, cV, cA), "PF": PF,
           "onescol_f": onescol_f, "ones1": ones1, "st_sb": st_sb, "mu": mu,
           "m2": m2, "var": var, "rstd": rstd, "nmur": nmur, "rstd_bf": rstd_bf,
           "nmur_bf": nmur_bf, "bcln": bcln}

    for c in range(NCHUNK):
        cG.tick(G.dma_start(out=x0b[:, c, :], in_=x0T[128 * c:128 * (c + 1), :], **MDL))
    cG.tick(G.dma_start(out=vld[:], in_=validT[:, :]))
    cP.tick(G.iota(iot[:], pattern=[[-1, 128]], base=0, channel_multiplier=1,
                   allow_small_or_imprecise_dtypes=True))
    cP.tick(G.iota(icol[:], pattern=[[1, 128]], base=0, channel_multiplier=0,
                   allow_small_or_imprecise_dtypes=True))
    cP.tick(G.iota(irow[:], pattern=[[0, 1]], base=0, channel_multiplier=1,
                   allow_small_or_imprecise_dtypes=True))
    cP.tick(G.memset(ones1[:], 1.0))
    cP.tick(G.memset(onescol_f[:], 1.0))
    cP.tick(G.memset(onescol_b[:], 1.0))
    cP.wait(V)
    cV.tick(V.tensor_scalar(id128[:], icol[:], irow[:, 0:1], None, ALU.is_equal))
    cV.tick(V.tensor_single_scalar(irow64[:], irow[:], 64.0, ALU.is_ge))
    cV.wait(V)
    cV.tick(V.scalar_tensor_tensor(irow64[:], irow64[:], -64.0, irow[:], ALU.mult, ALU.add))
    cV.wait(V)
    cV.tick(V.tensor_scalar(id64d[:], icol[:, 0:64], irow64[:, 0:1], None, ALU.is_equal))
    cV.tick(V.tensor_single_scalar(tmpm[:, 0, :], iot[:], 65.0, ALU.is_ge))
    cV.tick(V.tensor_single_scalar(tmpm[:, 1, :], iot[:], 128.0, ALU.is_le))
    cV.wait(V)
    cV.tick(V.tensor_mul(maskAB[:, 0:128], tmpm[:, 0, :], tmpm[:, 1, :]))
    cV.tick(V.tensor_single_scalar(tmpm[:, 0, :], iot[:], -63.0, ALU.is_ge))
    cV.tick(V.tensor_single_scalar(tmpm[:, 1, :], iot[:], 0.0, ALU.is_le))
    cV.wait(V)
    cV.tick(V.tensor_mul(maskAB[:, 128:256], tmpm[:, 0, :], tmpm[:, 1, :]))
    fence(ENG, ALL)

    kv_groups = [(0, 512), (512, 512), (1024, 512), (1536, 512), (2048, 128)]
    q_groups = [(0, 512), (512, 512), (1024, 512), (1536, 512)]

    for p in range(NPAIR):
        for wname, wdram in (("q", Wq), ("k", Wk), ("v", Wv)):
            fence([G], [cT])
            g_ld = cG.tick(G.dma_start(out=wraw[:], in_=wdram[128 * p:128 * (p + 1), :], **MDL))
            T.wait_ge(cG.sem, g_ld)
            for half in range(2):
                pb = (PB0, PB1)[half]
                fence([T], [cV])
                for k4 in range(4):
                    k = half * 4 + k4
                    ins = T.transpose(pb[:, 128 * k4:128 * (k4 + 1)],
                                      wraw[:, 128 * k:128 * (k + 1)], id128[:])
                cT.tick(ins)
                cT.wait(V)
                cV.tick(V.tensor_copy(wT[wname][:, 4 * half:4 * half + 4, :],
                                      pb[:, 0:512].rearrange("p (a b) -> p a b", b=128)))
        fence([T], [cV, cG])
        for wname, dst, groups, coloff in (("q", qT, q_groups, 128),
                                           ("k", kT, kv_groups, 0),
                                           ("v", vT, kv_groups, 0)):
            for gi, (off, gw) in enumerate(groups):
                pf = PF[gi % 2]
                fence([T], [cA])
                for k in range(8):
                    ins = T.matmul(pf[:, 0:gw], wT[wname][:, k, :],
                                   x0b[:, k, coloff + off:coloff + off + gw],
                                   start=(k == 0), stop=(k == 7))
                cT.tick(ins)
                cT.wait(A)
                cA.tick(A.activation(dst[:, off:off + gw], pf[:, 0:gw], AF.Copy))
        fence([T, V], [cA])
        for h2 in range(2):
            for c0 in range(0, 17, 4):
                cnt = min(4, 17 - c0)
                pb = (PB0, PB1)[(c0 // 4) % 2]
                fence([T], [cV])
                for cc in range(cnt):
                    ins = T.transpose(pb[:, 64 * cc:64 * (cc + 1)],
                                      vT[64 * h2:64 * (h2 + 1), 128 * (c0 + cc):128 * (c0 + cc + 1)],
                                      id64d[64 * h2:64 * (h2 + 1), 0:64])
                cT.tick(ins)
                cT.wait(V)
                cV.tick(V.tensor_copy(vt[:, h2, c0:c0 + cnt, 0:64],
                                      pb[:, 0:64 * cnt].rearrange("p (a b) -> p a b", b=64)))
            cV.tick(V.tensor_copy(vt[:, h2, :, 64], vld[:, 0:17]))
        fence([A], [cT, cV])
        for src, groups, gbase in ((qT, q_groups, 0), (kT, kv_groups, 16)):
            a_sq = cA.tick(A.activation(sq[:, 0:src.shape[-1]], src[:], AF.Square))
            T.wait_ge(cA.sem, a_sq)
            for h2 in range(2):
                for gi, (off, gw) in enumerate(groups):
                    cV.wait(T)
                    ins = T.matmul(PF[5][0:1, 0:gw],
                                   onescol_b[64 * h2:64 * (h2 + 1), 0:1],
                                   sq[64 * h2:64 * (h2 + 1), off:off + gw],
                                   start=True, stop=True)
                    t_mm = cT.tick(ins)
                    V.wait_ge(cT.sem, t_mm)
                    cV.tick(V.reduce_max(nmax[0:1, gbase + 8 * h2 + gi:gbase + 8 * h2 + gi + 1],
                                         PF[5][0:1, 0:gw], axis=AX.X))
            fence([A], [cT])
        for h2 in range(2):
            cV.wait(V)
            cV.tick(V.reduce_max(bnd[0:1, 0:1], nmax[0:1, 8 * h2:8 * h2 + 4], axis=AX.X))
            cV.tick(V.reduce_max(bnd[0:1, 1:2], nmax[0:1, 16 + 8 * h2:16 + 8 * h2 + 5], axis=AX.X))
            cV.wait(V)
            cV.tick(V.tensor_mul(bnd[0:1, 2:3], bnd[0:1, 0:1], bnd[0:1, 1:2]))
            cV.wait(A)
            cA.tick(A.activation(bnd[0:1, 3:4], bnd[0:1, 2:3], AF.Sqrt))
            cA.wait(V)
            cV.tick(V.tensor_single_scalar(bnd[0:1, 4:5], bnd[0:1, 3:4], -0.125, ALU.mult))
            cV.wait(V)
            cV.tick(V.tensor_copy(bndbf[0:1, 0:1], bnd[0:1, 4:5]))
            cV.wait(T)
            cA.wait(T)
            t_mm = cT.tick(T.matmul(PF[5][:, 0:1], ones1[0:1, :], bndbf[0:1, 0:1],
                                    start=True, stop=True))
            A.wait_ge(cT.sem, t_mm)
            cA.tick(A.activation(boundc[:, h2:h2 + 1], PF[5][:, 0:1], AF.Copy))
        fence(ENG, [cT, cV, cA])
        for i in range(16):
            for h2 in range(2):
                spf = PF[2 + h2]
                for ci, cc in enumerate((i, i + 1)):
                    ins = T.matmul(spf[:, 128 * ci:128 * (ci + 1)],
                                   kT[64 * h2:64 * (h2 + 1), 128 * cc:128 * (cc + 1)],
                                   qT[64 * h2:64 * (h2 + 1), 128 * i:128 * (i + 1)],
                                   start=True, stop=True)
                t_s = cT.tick(ins)
                A.wait_ge(cT.sem, t_s)
                cA.tick(A.activation(pt[:, h2, :], spf[:, 0:256], AF.Exp,
                                     bias=boundc[:, h2:h2 + 1], scale=0.125))
            cA.wait(V)
            for h2 in range(2):
                cV.tick(V.tensor_mul(pt[:, h2, :], pt[:, h2, :], maskAB[:]))
            cV.wait(T)
            for h2 in range(2):
                T.matmul(PF[4][64 * h2:64 * (h2 + 1), 0:128], vt[:, h2, i, 0:64],
                         pt[:, h2, 0:128], start=True, stop=False,
                         tile_position=(0, 64 * h2))
                ins = T.matmul(PF[4][64 * h2:64 * (h2 + 1), 0:128], vt[:, h2, i + 1, 0:64],
                               pt[:, h2, 128:256], start=False, stop=True,
                               tile_position=(0, 64 * h2))
                cT.tick(ins)
                T.matmul(PF[5][0:1, 128 * h2:128 * (h2 + 1)], vt[:, h2, i, 64:65],
                         pt[:, h2, 0:128], start=True, stop=False)
                ins = T.matmul(PF[5][0:1, 128 * h2:128 * (h2 + 1)], vt[:, h2, i + 1, 64:65],
                               pt[:, h2, 128:256], start=False, stop=True)
                t_pv = cT.tick(ins)
            V.wait_ge(cT.sem, t_pv)
            cV.tick(V.tensor_copy(lr[0:1, 0:256], PF[5][0:1, 0:256]))
            cV.wait(A)
            cA.tick(A.activation(lr[0:1, :], lr[0:1, :], AF.Ln))
            cA.wait(A)
            a_r = cA.tick(A.activation(rbf[0:1, :], lr[0:1, :], AF.Exp, scale=-1.0))
            T.wait_ge(cA.sem, a_r)
            for h2 in range(2):
                ins = T.matmul(PF[0][:, 128 * h2:128 * (h2 + 1)], ones1[0:1, :],
                               rbf[0:1, 128 * h2:128 * (h2 + 1)], start=True, stop=True)
            t_bc = cT.tick(ins)
            A.wait_ge(cT.sem, t_bc)
            a_bc = cA.tick(A.activation(bc_sb[:, 0:256], PF[0][:, 0:256], AF.Copy))
            V.wait_ge(cA.sem, a_bc)
            for h2 in range(2):
                cV.tick(V.tensor_mul(attnT[64 * h2:64 * (h2 + 1), p, 128 * i:128 * (i + 1)],
                                     PF[4][64 * h2:64 * (h2 + 1), 0:128],
                                     bc_sb[64 * h2:64 * (h2 + 1), 128 * h2:128 * (h2 + 1)]))
            fence([T], [cA, cV])
        fence(ENG, ALL)

    for dc in range(8):
        fence([G], [cT])
        g_ld = cG.tick(G.dma_start(out=wraw[:], in_=Wo[128 * dc:128 * (dc + 1), :], **MDL))
        T.wait_ge(cG.sem, g_ld)
        for half in range(2):
            pb = (PB0, PB1)[half]
            fence([T], [cV])
            for k4 in range(4):
                k = half * 4 + k4
                ins = T.transpose(pb[:, 128 * k4:128 * (k4 + 1)],
                                  wraw[:, 128 * k:128 * (k + 1)], id128[:])
            cT.tick(ins)
            cT.wait(V)
            cV.tick(V.tensor_copy(woT[:, dc, 4 * half:4 * half + 4, :],
                                  pb[:, 0:512].rearrange("p (a b) -> p a b", b=128)))
    fence(ENG, ALL)

    for g in range(4):
        off = 512 * g
        for dc in range(8):
            pf = PF[dc % 2]
            fence([T], [cV])
            for k in range(8):
                ins = T.matmul(pf[:], woT[:, dc, k, :], attnT[:, k, off:off + 512],
                               start=(k == 0), stop=(k == 7))
            t_mm = cT.tick(ins)
            V.wait_ge(cT.sem, t_mm)
            cV.tick(V.tensor_add(ysb[:, dc, :], pf[:], x0b[:, dc, 128 + off:128 + off + 512]))
        emit_ln_group(env, ysb, ysq, y1sb, y1, off)
    Y.wait_ge(cY.sem, cY.n)
    return nc



def build_launch_B(nc, stack, debug=False):
    G, T, V, A, Y = nc.gpsimd, nc.tensor, nc.vector, nc.scalar, nc.sync
    y1 = nc.dram_tensor("y1", [D, TOK], BF16, kind="ExternalInput").ap()
    w_red = nc.dram_tensor("w_red", [D, 16], F32, kind="ExternalInput").ap()
    wg = nc.dram_tensor("wg", [8, 16], F32, kind="ExternalInput").ap()
    W1 = nc.dram_tensor("W1", [8, D, 2048], F32, kind="ExternalInput").ap()
    b1s = nc.dram_tensor("b1s", [8, 128, 16], F32, kind="ExternalInput").ap()
    W2 = nc.dram_tensor("W2", [8, 2048, D], F32, kind="ExternalInput").ap()
    b2 = nc.dram_tensor("b2", [8, D], F32, kind="ExternalInput").ap()
    y2 = nc.dram_tensor("y2", [D, TOK], BF16, kind="ExternalOutput").ap()
    if debug:
        dbg_m8 = nc.dram_tensor("dbg_m8", [8, TOK], F32, kind="ExternalOutput").ap()
        dbg_mrows = nc.dram_tensor("dbg_mrows", [8, TOK], F32, kind="ExternalOutput").ap()
        dbg_out32 = nc.dram_tensor("dbg_out32", [D, 1024], F32, kind="ExternalOutput").ap()
        dbg_lgt = nc.dram_tensor("dbg_lgt", [128, 4, 8], F32, kind="ExternalOutput").ap()
        dbg_r = nc.dram_tensor("dbg_r", [16, 520], F32, kind="ExternalOutput").ap()

    sb = lambda name, shape, dt=BF16: stack.enter_context(nc.sbuf_tensor(name, shape, dt))
    ps = lambda name, shape, dt=F32: stack.enter_context(nc.psum_tensor(name, shape, dt))

    ysrc = sb("ysrc", [128, NCHUNK, TOK])
    wred_sb = sb("wred_sb", [128, 8, 16])
    wg_sb = sb("wg_sb", [8, 16], F32)
    wgsq = sb("wgsq", [8, 16], F32)
    wgn = sb("wgn", [8, 2], F32)
    wgnb = sb("wgnb", [8, 16])
    wgnT = sb("wgnT", [16, 8])
    xr_bf = sb("xr_bf", [16, 512])
    lg_sb = sb("lg_sb", [8, 512])
    lgt = sb("lgt", [128, 4, 8], F32)
    nm1 = sb("nm1", [128, 1], F32)
    eq8 = sb("eq8", [128, 8], F32)
    e8 = sb("e8", [128, 8], F32)
    ssum = sb("ssum", [128, 1], F32)
    rs = sb("rs", [128, 1], F32)
    mtok = sb("mtok", [128, 8])
    m8 = sb("m8", [8, TOK])
    mrows = sb("mrows", [1, 8, TOK])
    b1c = sb("b1c", [128, 16], F32)
    b2_sb = sb("b2_sb", [8, 8, 128])
    w1t = sb("w1t", [128, 2, 8, 128])
    w2t = sb("w2t", [128, 2, 16, 128])
    hh = sb("hh", [128, 16, 1024])
    bcm = sb("bcm", [128, 512])
    out32 = sb("out32", [128, 8, 1024], F32)
    id128 = sb("id128", [128, 128])
    icol = sb("icol", [128, 128], F32)
    irow = sb("irow", [128, 1], F32)
    ones1 = sb("ones1", [1, 128])
    onescol_f = sb("onescol_f", [128, 1], F32)
    ysq = sb("ysq", [128, 512], F32)
    st_sb = sb("st_sb", [1, 1024], F32)
    mu = sb("mu", [1, 512], F32)
    m2 = sb("m2", [1, 512], F32)
    var = sb("var", [1, 512], F32)
    rstd = sb("rstd", [1, 512], F32)
    nmur = sb("nmur", [1, 512], F32)
    rstd_bf = sb("rstd_bf", [1, 512])
    nmur_bf = sb("nmur_bf", [1, 512])
    bcln = sb("bcln", [128, 2, 512])
    y2sb = sb("y2sb", [128, NCHUNK, 512])

    PB0 = ps("PB0", [128, 1024], BF16)
    PB1 = ps("PB1", [128, 1024], BF16)
    PF = [ps(f"PF{i}", [128, 512], F32) for i in range(6)]

    cG = Ctr(nc, stack, "cG", 16)
    cY = Ctr(nc, stack, "cY", 16)
    cP = Ctr(nc, stack, "cP")
    cT = Ctr(nc, stack, "cT")
    cV = Ctr(nc, stack, "cV")
    cA = Ctr(nc, stack, "cA")
    ALL = [cG, cY, cP, cT, cV, cA]
    ENG = [G, T, V, A, Y]
    MDL = dict(max_dma_last_dim=8192)
    env = {"eng": (G, T, V, A, Y), "ctr": (cG, cY, cP, cT, cV, cA), "PF": PF,
           "onescol_f": onescol_f, "ones1": ones1, "st_sb": st_sb, "mu": mu,
           "m2": m2, "var": var, "rstd": rstd, "nmur": nmur, "rstd_bf": rstd_bf,
           "nmur_bf": nmur_bf, "bcln": bcln}

    for c in range(NCHUNK):
        cY.tick(Y.dma_start(out=ysrc[:, c, :], in_=y1[128 * c:128 * (c + 1), :]))
    cG.tick(G.dma_start(out=wred_sb[:],
                        in_=w_red.rearrange("(kc p) g -> p kc g", p=128)))
    cG.tick(G.dma_start(out=wg_sb[:], in_=wg[:, :]))
    cG.tick(G.dma_start(out=b2_sb[:],
                        in_=b2.rearrange("e (dc m) -> e dc m", m=128)))
    cP.tick(G.iota(icol[:], pattern=[[1, 128]], base=0, channel_multiplier=0,
                   allow_small_or_imprecise_dtypes=True))
    cP.tick(G.iota(irow[:], pattern=[[0, 1]], base=0, channel_multiplier=1,
                   allow_small_or_imprecise_dtypes=True))
    cP.tick(G.memset(ones1[:], 1.0))
    cP.tick(G.memset(onescol_f[:], 1.0))
    cP.wait(V)
    cV.tick(V.tensor_scalar(id128[:], icol[:], irow[:, 0:1], None, ALU.is_equal))
    fence(ENG, ALL)

    cA.tick(A.activation(wgsq[:], wg_sb[:], AF.Square))
    cA.wait(V)
    cV.tick(V.reduce_sum(wgn[:, 0:1], wgsq[:], axis=AX.X))
    cV.wait(A)
    cA.tick(A.activation(wgn[:, 1:2], wgn[:, 0:1], AF.Sqrt))
    cA.wait(V)
    cV.tick(V.tensor_scalar_max(wgn[:, 1:2], wgn[:, 1:2], 1e-4))
    cV.wait(V)
    cV.tick(V.reciprocal(wgn[:, 0:1], wgn[:, 1:2]))
    cV.wait(V)
    cV.tick(V.tensor_scalar(wgnb[:], wg_sb[:], wgn[:, 0:1], None, ALU.mult))
    cV.wait(T)
    cT.tick(T.transpose(PB0[0:16, 0:8], wgnb[:], id128[0:8, 0:8]))
    cT.wait(V)
    cV.tick(V.tensor_copy(wgnT[:], PB0[0:16, 0:8]))
    fence(ENG, ALL)

    if debug:
        fence(ENG, ALL)
        cG.tick(G.dma_start(out=dbg_r[0:8, 0:2], in_=wgn[:]))
        cG.tick(G.dma_start(out=dbg_r[0:8, 2:18], in_=wgnb[:]))
        cG.tick(G.dma_start(out=dbg_r[0:16, 18:26], in_=wgnT[:]))
        cG.tick(G.dma_start(out=dbg_r[0:8, 26:42], in_=wg_sb[:]))
    for g in range(4):
        off = 512 * g
        fence([T], [cA, cV])
        for kc in range(8):
            ins = T.matmul(PF[0][0:16, :], wred_sb[:, kc, :], ysrc[:, kc, off:off + 512],
                           start=(kc == 0), stop=(kc == 7))
        t_xr = cT.tick(ins)
        A.wait_ge(cT.sem, t_xr)
        a_xr = cA.tick(A.activation(xr_bf[:], PF[0][0:16, :], AF.Copy))
        T.wait_ge(cA.sem, a_xr)
        t_lg = cT.tick(T.matmul(PF[1][0:8, :], wgnT[:], xr_bf[:], start=True, stop=True))
        V.wait_ge(cT.sem, t_lg)
        cV.tick(V.tensor_copy(lg_sb[:], PF[1][0:8, :]))
        if debug and g == 0:
            v_d = cV.n
            G.wait_ge(cV.sem, v_d)
            cG.tick(G.dma_start(out=dbg_r[0:16, 42:42 + 128], in_=xr_bf[:, 0:128]))
            cG.tick(G.dma_start(out=dbg_r[0:8, 256:256 + 256], in_=lg_sb[:, 0:256]))
        cV.wait(T)
        for t4 in range(4):
            ins = T.transpose(PB0[0:128, 8 * t4:8 * (t4 + 1)],
                              lg_sb[:, 128 * t4:128 * (t4 + 1)], id128[0:8, 0:8])
        t_tr = cT.tick(ins)
        V.wait_ge(cT.sem, t_tr)
        cV.tick(V.tensor_copy(lgt[:], PB0[0:128, 0:32].rearrange("p (a b) -> p a b", b=8)))
        for t4 in range(4):
            cV.wait(V)
            cV.tick(V.reduce_max(nm1[:], lgt[:, t4, :], axis=AX.X, negate=True))
            cV.wait(V)
            cV.tick(V.tensor_scalar(eq8[:], lgt[:, t4, :], nm1[:, 0:1], 0.0,
                                    ALU.add, ALU.is_equal))
            cV.wait(A)
            cA.tick(A.activation(e8[:], lgt[:, t4, :], AF.Exp, bias=nm1[:, 0:1],
                                 accum_out=ssum[:, 0:1]))
            cA.wait(V)
            cV.tick(V.reciprocal(rs[:], ssum[:]))
            cV.wait(V)
            cV.tick(V.tensor_scalar(mtok[:], eq8[:], rs[:, 0:1], None, ALU.mult))
            cV.wait(T)
            t_f = cT.tick(T.transpose(PB1[0:8, 128 * t4:128 * (t4 + 1)], mtok[:], id128[:]))
            for e in range(8):
                ins = T.transpose(PB0[0:1, 128 * e:128 * (e + 1)],
                                  mtok[:, e:e + 1], id128[:])
            t_r = cT.tick(ins)
            V.wait_ge(cT.sem, t_r)
            cV.tick(V.tensor_copy(mrows[0:1, :, off + 128 * t4:off + 128 * (t4 + 1)],
                                  PB0[0:1, 0:1024].rearrange("p (a b) -> p a b", b=128)))
        V.wait_ge(cT.sem, cT.n)
        cV.tick(V.tensor_copy(m8[:, off:off + 512], PB1[0:8, 0:512]))
        fence([T], [cV])
        if debug and g == 0:
            cY.tick(Y.dma_start(out=dbg_lgt[:, :, :], in_=lgt[:]))
    fence(ENG, ALL)
    if debug:
        cG.tick(G.dma_start(out=dbg_m8[:, :], in_=m8[:]))
        for e in range(8):
            cG.tick(G.dma_start(out=dbg_mrows[e:e + 1, :], in_=mrows[0:1, e, :]))

    for gh in range(2):
        goff = 1024 * gh
        fence([T], [cV, cA])
        for dc in range(8):
            for g2 in range(2):
                pf = PF[(dc * 2 + g2) % 2]
                fence([T], [cV])
                t_b = cT.tick(T.matmul(pf[:], b2_sb[:, dc, :],
                                       m8[:, goff + 512 * g2:goff + 512 * (g2 + 1)],
                                       start=True, stop=True))
                V.wait_ge(cT.sem, t_b)
                cV.tick(V.tensor_copy(out32[:, dc, 512 * g2:512 * (g2 + 1)], pf[:]))
        fence(ENG, ALL)
        for e in range(8):
            g_b1 = cG.tick(G.dma_start(out=b1c[:], in_=b1s[e, :, :]))
            A.wait_ge(cG.sem, g_b1)
            for mc in range(16):
                buf = mc % 2
                fence([G], [cT])
                g_w = cG.tick(G.dma_start(
                    out=w1t[:, buf, :, :],
                    in_=W1[e].rearrange("(kc p) hh -> p kc hh", p=128)[:, :, 128 * mc:128 * (mc + 1)],
                    **MDL))
                T.wait_ge(cG.sem, g_w)
                for g2 in range(2):
                    pf = PF[2 + (mc * 2 + g2) % 2]
                    fence([T], [cA])
                    for kc in range(8):
                        ins = T.matmul(pf[:], w1t[:, buf, kc, :],
                                       ysrc[:, kc, goff + 512 * g2:goff + 512 * (g2 + 1)],
                                       start=(kc == 0), stop=(kc == 7))
                    t_h = cT.tick(ins)
                    A.wait_ge(cT.sem, t_h)
                    cA.tick(A.activation(hh[:, mc, 512 * g2:512 * (g2 + 1)], pf[:],
                                         AF.Gelu, bias=b1c[:, mc:mc + 1]))
            for g2 in range(2):
                fence([T], [cA])
                t_bc = cT.tick(T.matmul(PF[4][:], ones1[0:1, :],
                                        mrows[0:1, e, goff + 512 * g2:goff + 512 * (g2 + 1)],
                                        start=True, stop=True))
                A.wait_ge(cT.sem, t_bc)
                cV.wait(A)
                a_bc = cA.tick(A.activation(bcm[:], PF[4][:], AF.Copy))
                V.wait_ge(cA.sem, a_bc)
                V.wait_ge(cA.sem, cA.n)
                for mc in range(16):
                    cV.tick(V.tensor_mul(hh[:, mc, 512 * g2:512 * (g2 + 1)],
                                         hh[:, mc, 512 * g2:512 * (g2 + 1)], bcm[:]))
            for dc in range(8):
                buf = dc % 2
                fence([G], [cT])
                fence([T], [cA])
                g_w = cG.tick(G.dma_start(
                    out=w2t[:, buf, :, :],
                    in_=W2[e].rearrange("(kc p) d -> p kc d", p=128)[:, :, 128 * dc:128 * (dc + 1)],
                    **MDL))
                T.wait_ge(cG.sem, g_w)
                cV.wait(T)
                for g2 in range(2):
                    pf = PF[4 + (dc * 2 + g2) % 2]
                    fence([T], [cV])
                    for kc in range(16):
                        ins = T.matmul(pf[:], w2t[:, buf, kc, :],
                                       hh[:, kc, 512 * g2:512 * (g2 + 1)],
                                       start=(kc == 0), stop=(kc == 15))
                    t_o = cT.tick(ins)
                    V.wait_ge(cT.sem, t_o)
                    cV.tick(V.tensor_tensor(out32[:, dc, 512 * g2:512 * (g2 + 1)],
                                            out32[:, dc, 512 * g2:512 * (g2 + 1)],
                                            pf[:], ALU.add))
            fence(ENG, ALL)
        if debug and gh == 0:
            for dc in range(8):
                cY.tick(Y.dma_start(out=dbg_out32[128 * dc:128 * (dc + 1), :],
                                    in_=out32[:, dc, :]))
            Y.wait_ge(cY.sem, cY.n)
        for g2 in range(2):
            off = goff + 512 * g2
            for dc in range(8):
                cV.wait(V)
                cV.tick(V.tensor_add(out32[:, dc, 512 * g2:512 * (g2 + 1)],
                                     out32[:, dc, 512 * g2:512 * (g2 + 1)],
                                     ysrc[:, dc, off:off + 512]))
            emit_ln_group(env, out32[:, :, 512 * g2:512 * (g2 + 1)], ysq, y2sb, y2, off)
        fence(ENG, ALL)
    Y.wait_ge(cY.sem, cY.n)
    return nc


def _stage_B(inputs, y1_shards):
    maps = []
    b1s = np.ascontiguousarray(
        inputs["L0_ffn_b1"].reshape(8, 16, 128).transpose(0, 2, 1))
    for c in range(8):
        maps.append({
            "y1": np.asarray(y1_shards[c]),
            "w_red": inputs["L0_ffn_w_red"],
            "wg": inputs["L0_ffn_wg"],
            "W1": inputs["L0_ffn_W1"],
            "b1s": b1s,
            "W2": inputs["L0_ffn_W2"],
            "b2": inputs["L0_ffn_b2"],
        })
    return maps


def run_launch_B(inputs, y1_shards, debug=False):
    from contextlib import ExitStack
    nc = bass.Bass("TRN2", target_bir_lowering=False)
    with ExitStack() as stack:
        build_launch_B(nc, stack, debug=debug)
    res = run_bass_kernel_spmd(nc, _stage_B(inputs, y1_shards), core_ids=list(range(8)))
    if debug:
        return res.results
    return [r["y2"] for r in res.results]



def build_launch_C(nc, stack):
    G, T, V, A, Y = nc.gpsimd, nc.tensor, nc.vector, nc.scalar, nc.sync
    S = 4096
    kvT = nc.dram_tensor("kvT", [D, S], BF16, kind="ExternalInput").ap()
    qinT = nc.dram_tensor("qinT", [D, TOK], BF16, kind="ExternalInput").ap()
    resT = nc.dram_tensor("resT", [D, TOK], BF16, kind="ExternalInput").ap()
    h32 = nc.dram_tensor("h32", [128, 1], F32, kind="ExternalInput").ap()
    Wqkv = nc.dram_tensor("Wqkv", [NH, D, 192], F32, kind="ExternalInput").ap()
    Wout = nc.dram_tensor("Wout", [D, D], F32, kind="ExternalInput").ap()
    y3 = nc.dram_tensor("y3", [D, TOK], BF16, kind="ExternalOutput").ap()
    o_dram = nc.dram_tensor("o_dram", [NH, TOK, HD], BF16, kind="Internal").ap()

    sb = lambda name, shape, dt=BF16: stack.enter_context(nc.sbuf_tensor(name, shape, dt))
    ps = lambda name, shape, dt=F32: stack.enter_context(nc.psum_tensor(name, shape, dt))

    kvc = sb("kvc", [128, 2, 512])
    qc = sb("qc", [128, 2, 512])
    wqkv_sb = sb("wqkv_sb", [128, 8, 2, 192])
    KT = sb("KT", [128, S])
    vstage = sb("vstage", [128, S])
    vt = sb("vt", [128, 2, 32, 65])
    QT = sb("QT", [128, TOK])
    sq = sb("sq", [128, S])
    ptile = sb("ptile", [128, 512])
    h32c = sb("h32c", [128, 1], F32)
    io2 = sb("io2", [128, 128], F32)
    icol = sb("icol", [128, 128], F32)
    irow = sb("irow", [128, 1], F32)
    irow64 = sb("irow64", [128, 1], F32)
    m1t = sb("m1t", [128, 128])
    m2t = sb("m2t", [128, 128])
    id128 = sb("id128", [128, 128])
    id64d = sb("id64d", [128, 64])
    ones1 = sb("ones1", [1, 128])
    onescol_f = sb("onescol_f", [128, 1], F32)
    onescol_b = sb("onescol_b", [128, 1])
    nmax = sb("nmax", [1, 48], F32)
    bnd = sb("bnd", [1, 8], F32)
    bndbf = sb("bndbf", [1, 1])
    boundc = sb("boundc", [128, 2], F32)
    lr = sb("lr", [1, 2, 512], F32)
    r_bf = sb("r_bf", [1, 2, 512])
    rcol = sb("rcol", [128, 2, 4], F32)
    oevac = sb("oevac", [128, 512])
    osb = sb("osb", [128, 2, 4, 64])
    wout_c = sb("wout_c", [128, 2, D])
    wsum_sb = sb("wsum_sb", [1, D], F32)
    wsum_bf = sb("wsum_bf", [1, D])
    wsumT = sb("wsumT", [128, 8], F32)
    att = sb("att", [128, 512])
    resc = sb("resc", [128, NCHUNK, 512])
    ysb = sb("ysb", [128, NCHUNK, 512], F32)
    ysq = sb("ysq", [128, 512], F32)
    st_sb = sb("st_sb", [1, 1024], F32)
    mu = sb("mu", [1, 512], F32)
    m2s = sb("m2s", [1, 512], F32)
    var = sb("var", [1, 512], F32)
    rstd = sb("rstd", [1, 512], F32)
    nmur = sb("nmur", [1, 512], F32)
    rstd_bf = sb("rstd_bf", [1, 512])
    nmur_bf = sb("nmur_bf", [1, 512])
    bcln = sb("bcln", [128, 2, 512])
    y3sb = sb("y3sb", [128, NCHUNK, 512])

    PB0 = ps("PB0", [128, 1024], BF16)
    PB1 = ps("PB1", [128, 1024], BF16)
    PF = [ps(f"PF{i}", [128, 512], F32) for i in range(6)]

    cG = Ctr(nc, stack, "cG", 16)
    cY = Ctr(nc, stack, "cY", 16)
    cP = Ctr(nc, stack, "cP")
    cT = Ctr(nc, stack, "cT")
    cV = Ctr(nc, stack, "cV")
    cA = Ctr(nc, stack, "cA")
    ALL = [cG, cY, cP, cT, cV, cA]
    ENG = [G, T, V, A, Y]
    env = {"eng": (G, T, V, A, Y), "ctr": (cG, cY, cP, cT, cV, cA), "PF": PF,
           "onescol_f": onescol_f, "ones1": ones1, "st_sb": st_sb, "mu": mu,
           "m2": m2s, "var": var, "rstd": rstd, "nmur": nmur, "rstd_bf": rstd_bf,
           "nmur_bf": nmur_bf, "bcln": bcln}

    cY.tick(Y.dma_start(out=h32c[:], in_=h32[:, :]))
    cP.tick(G.iota(io2[:], pattern=[[64, 4], [1, 32]], base=0, channel_multiplier=-1,
                   allow_small_or_imprecise_dtypes=True))
    cP.tick(G.iota(icol[:], pattern=[[1, 128]], base=0, channel_multiplier=0,
                   allow_small_or_imprecise_dtypes=True))
    cP.tick(G.iota(irow[:], pattern=[[0, 1]], base=0, channel_multiplier=1,
                   allow_small_or_imprecise_dtypes=True))
    cP.tick(G.memset(ones1[:], 1.0))
    cP.tick(G.memset(onescol_f[:], 1.0))
    cP.tick(G.memset(onescol_b[:], 1.0))
    cP.tick(G.memset(vt[:, :, :, 64], 1.0))
    cP.wait(V)
    cY.wait(V)
    cV.tick(V.tensor_scalar(id128[:], icol[:], irow[:, 0:1], None, ALU.is_equal))
    cV.tick(V.tensor_single_scalar(irow64[:], irow[:], 64.0, ALU.is_ge))
    cV.wait(V)
    cV.tick(V.scalar_tensor_tensor(irow64[:], irow64[:], -64.0, irow[:], ALU.mult, ALU.add))
    cV.wait(V)
    cV.tick(V.tensor_scalar(id64d[:], icol[:, 0:64], irow64[:, 0:1], None, ALU.is_equal))
    cV.tick(V.tensor_scalar(m1t[:], io2[:], h32c[:, 0:1], 0.0, ALU.add, ALU.is_ge))
    cV.tick(V.tensor_scalar(m2t[:], io2[:], h32c[:, 0:1], 128.0, ALU.add, ALU.is_ge))
    for kc in range(8):
        buf = kc % 2
        fence([G], [cT])
        g_w = cG.tick(G.dma_start(out=wout_c[:, buf, :],
                                  in_=Wout[128 * kc:128 * (kc + 1), :], max_dma_last_dim=8192))
        T.wait_ge(cG.sem, g_w)
        for half in range(2):
            ins = T.matmul(PF[half][0:1, :], onescol_b[:, 0:1],
                           wout_c[:, buf, 512 * half:512 * (half + 1)],
                           start=(kc == 0), stop=(kc == 7))
        cT.tick(ins)
    cT.wait(V)
    cV.tick(V.tensor_copy(wsum_sb[0:1, 0:512], PF[0][0:1, :]))
    cV.tick(V.tensor_copy(wsum_sb[0:1, 512:1024], PF[1][0:1, :]))
    cV.wait(V)
    cV.tick(V.tensor_copy(wsum_bf[0:1, :], wsum_sb[0:1, :]))
    cV.wait(T)
    for kc in range(8):
        ins = T.transpose(PB0[0:128, 2 * kc:2 * kc + 1], wsum_bf[0:1, 128 * kc:128 * (kc + 1)],
                          id128[0:1, 0:1])
    t_ws = cT.tick(ins)
    A.wait_ge(cT.sem, t_ws)
    cA.tick(A.activation(wsumT[:], PB0[0:128, 0:16].rearrange("p (a b) -> p a b", b=2)[:, :, 0],
                         AF.Copy))
    fence(ENG, ALL)

    for p in range(NPAIR):
        fence([G], [cT])
        for kc in range(8):
            g_w = cG.tick(G.dma_start(
                out=wqkv_sb[:, kc, :, :],
                in_=Wqkv[2 * p:2 * p + 2, 128 * kc:128 * (kc + 1), :].rearrange("h p w -> p h w")))
        T.wait_ge(cG.sem, cG.n)
        for g in range(8):
            fence([T], [cA, cV])
            for kc in range(8):
                buf = kc % 2
                fence([Y], [cT])
                y_kv = cY.tick(Y.dma_start(out=kvc[:, buf, 0:512],
                                           in_=kvT[128 * kc:128 * (kc + 1), 512 * g:512 * (g + 1)]))
                T.wait_ge(cY.sem, y_kv)
                for h2 in range(2):
                    T.matmul(PF[0][64 * h2:64 * (h2 + 1), :], wqkv_sb[:, kc, h2, 64:128],
                             kvc[:, buf, 0:512], start=(kc == 0), stop=(kc == 7),
                             tile_position=(0, 64 * h2), skip_group_check=True)
                    ins = T.matmul(PF[1][64 * h2:64 * (h2 + 1), :], wqkv_sb[:, kc, h2, 128:192],
                                   kvc[:, buf, 0:512], start=(kc == 0), stop=(kc == 7),
                                   tile_position=(0, 64 * h2), skip_group_check=True)
                cT.tick(ins)
            cT.wait(A)
            cT.wait(V)
            cA.tick(A.activation(KT[:, 512 * g:512 * (g + 1)], PF[0][:], AF.Copy))
            cV.tick(V.tensor_copy(vstage[:, 512 * g:512 * (g + 1)], PF[1][:]))
        for g in range(4):
            fence([T], [cA])
            for kc in range(8):
                buf = kc % 2
                fence([Y], [cT])
                y_q = cY.tick(Y.dma_start(out=qc[:, buf, 0:512],
                                          in_=qinT[128 * kc:128 * (kc + 1), 512 * g:512 * (g + 1)]))
                T.wait_ge(cY.sem, y_q)
                for h2 in range(2):
                    ins = T.matmul(PF[0][64 * h2:64 * (h2 + 1), :], wqkv_sb[:, kc, h2, 0:64],
                                   qc[:, buf, 0:512], start=(kc == 0), stop=(kc == 7),
                                   tile_position=(0, 64 * h2), skip_group_check=True)
                cT.tick(ins)
            cT.wait(A)
            cA.tick(A.activation(QT[:, 512 * g:512 * (g + 1)], PF[0][:], AF.Copy))
        fence(ENG, [cT, cV, cA])
        for h2 in range(2):
            for c0 in range(0, 32, 4):
                pb = (PB0, PB1)[(c0 // 4) % 2]
                fence([T], [cV])
                for cc in range(4):
                    ins = T.transpose(pb[:, 64 * cc:64 * (cc + 1)],
                                      vstage[64 * h2:64 * (h2 + 1), 128 * (c0 + cc):128 * (c0 + cc + 1)],
                                      id64d[64 * h2:64 * (h2 + 1), 0:64])
                cT.tick(ins)
                cT.wait(V)
                cV.tick(V.tensor_copy(vt[:, h2, c0:c0 + 4, 0:64],
                                      pb[:, 0:256].rearrange("p (a b) -> p a b", b=64)))
        fence([A], [cT, cV])
        for src, width, ng, gbase in ((QT, TOK, 4, 0), (KT, S, 8, 16)):
            a_sq = cA.tick(A.activation(sq[:, 0:width], src[:, 0:width], AF.Square))
            T.wait_ge(cA.sem, a_sq)
            for h2 in range(2):
                for gi in range(ng):
                    cV.wait(T)
                    ins = T.matmul(PF[5][0:1, 0:512],
                                   onescol_b[64 * h2:64 * (h2 + 1), 0:1],
                                   sq[64 * h2:64 * (h2 + 1), 512 * gi:512 * (gi + 1)],
                                   start=True, stop=True)
                    t_mm = cT.tick(ins)
                    V.wait_ge(cT.sem, t_mm)
                    cV.tick(V.reduce_max(nmax[0:1, gbase + 8 * h2 + gi:gbase + 8 * h2 + gi + 1],
                                         PF[5][0:1, 0:512], axis=AX.X))
            fence([A], [cT])
        for h2 in range(2):
            cV.wait(V)
            cV.tick(V.reduce_max(bnd[0:1, 0:1], nmax[0:1, 8 * h2:8 * h2 + 4], axis=AX.X))
            cV.tick(V.reduce_max(bnd[0:1, 1:2], nmax[0:1, 16 + 8 * h2:16 + 8 * h2 + 8], axis=AX.X))
            cV.wait(V)
            cV.tick(V.tensor_mul(bnd[0:1, 2:3], bnd[0:1, 0:1], bnd[0:1, 1:2]))
            cV.wait(A)
            cA.tick(A.activation(bnd[0:1, 3:4], bnd[0:1, 2:3], AF.Sqrt))
            cA.wait(V)
            cV.tick(V.tensor_single_scalar(bnd[0:1, 4:5], bnd[0:1, 3:4], -0.125, ALU.mult))
            cV.wait(V)
            cV.tick(V.tensor_copy(bndbf[0:1, 0:1], bnd[0:1, 4:5]))
            cV.wait(T)
            cA.wait(T)
            t_mm = cT.tick(T.matmul(PF[5][:, 0:1], ones1[0:1, :], bndbf[0:1, 0:1],
                                    start=True, stop=True))
            A.wait_ge(cT.sem, t_mm)
            cA.tick(A.activation(boundc[:, h2:h2 + 1], PF[5][:, 0:1], AF.Copy))
        fence(ENG, [cT, cV, cA])
        for g4 in range(4):
            nchunks = 8 * g4 + 8
            for h2 in range(2):
                ltgt = PF[5] if h2 == 0 else PF[0]
                for c in range(nchunks):
                    j0 = c // 2
                    qoff = 128 * max(0, j0 - 4 * g4)
                    aw = 512 - qoff
                    spf = PF[2 + h2]
                    t_s = cT.tick(T.matmul(spf[:, 0:aw],
                                           KT[64 * h2:64 * (h2 + 1), 128 * c:128 * (c + 1)],
                                           QT[64 * h2:64 * (h2 + 1), 512 * g4 + qoff:512 * (g4 + 1)],
                                           start=True, stop=True))
                    A.wait_ge(cT.sem, t_s)
                    fence([A], [cT])
                    a_e = cA.tick(A.activation(ptile[:, 0:aw], spf[:, 0:aw], AF.Exp,
                                               bias=boundc[:, h2:h2 + 1], scale=0.125))
                    if c >= 8 * g4:
                        V.wait_ge(cA.sem, a_e)
                        cV.tick(V.tensor_mul(ptile[:, 0:128], ptile[:, 0:128],
                                             (m1t if c % 2 == 0 else m2t)[:]))
                        T.wait_ge(cV.sem, cV.n)
                    else:
                        T.wait_ge(cA.sem, a_e)
                    T.matmul(PF[4][64 * h2:64 * (h2 + 1), qoff:512], vt[:, h2, c, 0:64],
                             ptile[:, 0:aw], start=(c == 0), stop=(c == nchunks - 1),
                             tile_position=(0, 64 * h2), skip_group_check=True)
                    ins = T.matmul(ltgt[0:1, qoff:512], vt[:, h2, c, 64:65],
                                   ptile[:, 0:aw], start=(c == 0), stop=(c == nchunks - 1),
                                   skip_group_check=True)
                    cT.tick(ins)
            V.wait_ge(cT.sem, cT.n)
            cV.tick(V.tensor_copy(lr[0:1, 0, :], PF[5][0:1, 0:512]))
            cV.tick(V.tensor_copy(lr[0:1, 1, :], PF[0][0:1, 0:512]))
            cV.wait(A)
            cA.tick(A.activation(lr[0:1, :, :], lr[0:1, :, :], AF.Ln))
            cA.wait(A)
            a_r = cA.tick(A.activation(r_bf[0:1, :, :], lr[0:1, :, :], AF.Exp, scale=-1.0))
            A.wait_ge(cA.sem, a_r)
            a_o = cA.tick(A.activation(oevac[:], PF[4][:], AF.Copy))
            T.wait_ge(cA.sem, a_o)
            for h2 in range(2):
                fence([T], [cA])
                for t4 in range(4):
                    ins = T.transpose(PB0[0:128, 2 * t4:2 * t4 + 1],
                                      r_bf[0:1, h2, 128 * t4:128 * (t4 + 1)], id128[0:1, 0:1])
                cT.tick(ins)
                cT.wait(A)
                cA.tick(A.activation(rcol[:, h2, :],
                                     PB0[0:128, 0:8].rearrange("p (a b) -> p a b", b=2)[:, :, 0],
                                     AF.Copy))
            for rnd in range(2):
                h2 = rnd
                fence([T], [cV])
                for t4 in range(4):
                    ins = T.transpose(PB1[:, 64 * t4:64 * (t4 + 1)],
                                      oevac[64 * h2:64 * (h2 + 1), 128 * t4:128 * (t4 + 1)],
                                      id64d[64 * h2:64 * (h2 + 1), 0:64])
                cT.tick(ins)
                cT.wait(V)
                cA.wait(V)
                for t4 in range(4):
                    cV.tick(V.tensor_scalar(osb[:, h2, t4, :], PB1[:, 64 * t4:64 * (t4 + 1)],
                                            rcol[:, h2, t4:t4 + 1], None, ALU.mult))
            v_done = cV.n
            Y.wait_ge(cV.sem, v_done)
            for h2 in range(2):
                cY.tick(Y.dma_start(
                    out=o_dram[2 * p + h2, 512 * g4:512 * (g4 + 1), :].rearrange(
                        "(t4 r) e -> r t4 e", t4=4),
                    in_=osb[:, h2, :, :]))
            fence([T], [cA, cV, cY])
        fence(ENG, ALL)

    o_flat = o_dram.rearrange("h q e -> (h q e)")
    for g in range(4):
        off = 512 * g
        for dc in range(8):
            fence([Y], [cV])
            y_r = cY.tick(Y.dma_start(out=resc[:, dc, :],
                                      in_=resT[128 * dc:128 * (dc + 1), off:off + 512]))
            y_a = cY.tick(Y.dma_start(
                out=att[:, 0:512],
                in_=o_flat[dc * 262144:(dc + 1) * 262144].rearrange(
                    "(r c) -> r c", r=128)[:, off:off + 512]))
            V.wait_ge(cY.sem, y_a)
            cV.wait(V)
            cV.tick(V.scalar_tensor_tensor(ysb[:, dc, :], att[:, 0:512],
                                           wsumT[:, dc:dc + 1], resc[:, dc, :],
                                           ALU.mult, ALU.add))
        emit_ln_group(env, ysb, ysq, y3sb, y3, off)
    Y.wait_ge(cY.sem, cY.n)
    return nc


def _stage_C(inputs, y2_shards):
    wqkv = inputs["L1_attn_W_qkv"]
    wout = np.ascontiguousarray(inputs["L1_attn_W_out"].reshape(D, D))
    maps = []
    y2f = []
    for b in range(4):
        y2f.append(np.concatenate([np.asarray(y2_shards[2 * b]),
                                   np.asarray(y2_shards[2 * b + 1])], axis=1))
    for c in range(8):
        b, h = c // 2, c % 2
        full = y2f[b]
        qcols = (np.arange(4096).reshape(64, 64)[:, 32 * h:32 * h + 32]).reshape(-1)
        maps.append({
            "kvT": full,
            "qinT": np.ascontiguousarray(full[:, qcols]),
            "resT": np.ascontiguousarray(full[:, 2048 * h:2048 * (h + 1)]),
            "h32": np.full((128, 1), 32.0 * h, np.float32),
            "Wqkv": wqkv,
            "Wout": wout,
        })
    return maps


def run_launch_C(inputs, y2_shards):
    from contextlib import ExitStack
    nc = bass.Bass("TRN2", target_bir_lowering=False)
    with ExitStack() as stack:
        build_launch_C(nc, stack)
    res = run_bass_kernel_spmd(nc, _stage_C(inputs, y2_shards), core_ids=list(range(8)))
    return [r["y3"] for r in res.results]



def build_launch_D(nc, stack):
    G, T, V, A, Y = nc.gpsimd, nc.tensor, nc.vector, nc.scalar, nc.sync
    H1 = 4096
    y3 = nc.dram_tensor("y3", [D, TOK], BF16, kind="ExternalInput").ap()
    rw1 = nc.dram_tensor("rw1", [D, D], F32, kind="ExternalInput").ap()
    rb1s = nc.dram_tensor("rb1s", [128, 8], F32, kind="ExternalInput").ap()
    rw2 = nc.dram_tensor("rw2", [D, 8], F32, kind="ExternalInput").ap()
    rb2 = nc.dram_tensor("rb2", [8, 1], F32, kind="ExternalInput").ap()
    we1 = nc.dram_tensor("we1", [8, D, H1], F32, kind="ExternalInput").ap()
    be1s = nc.dram_tensor("be1s", [8, 128, 32], F32, kind="ExternalInput").ap()
    we2 = nc.dram_tensor("we2", [8, H1, D], F32, kind="ExternalInput").ap()
    be2 = nc.dram_tensor("be2", [8, D], F32, kind="ExternalInput").ap()
    yout = nc.dram_tensor("yout", [D, TOK], F32, kind="ExternalOutput").ap()

    sb = lambda name, shape, dt=BF16: stack.enter_context(nc.sbuf_tensor(name, shape, dt))
    ps = lambda name, shape, dt=F32: stack.enter_context(nc.psum_tensor(name, shape, dt))

    ysrc = sb("ysrc", [128, NCHUNK, TOK])
    big = sb("big", [128, 32768])
    t_v = big[:].rearrange("p (a b) -> p a b", b=TOK)
    hh_v = big[:].rearrange("p (a b) -> p a b", b=1024)
    rw1t = sb("rw1t", [128, 2, 8, 128])
    rw2_sb = sb("rw2_sb", [128, 8, 8])
    rb1c = sb("rb1c", [128, 8], F32)
    rb2c = sb("rb2c", [8, 1], F32)
    lg_sb = sb("lg_sb", [8, 512])
    lgt = sb("lgt", [128, 4, 8], F32)
    nm1 = sb("nm1", [128, 1], F32)
    nm2 = sb("nm2", [128, 1], F32)
    eq8 = sb("eq8", [128, 8], F32)
    l2 = sb("l2", [128, 8], F32)
    msk8 = sb("msk8", [128, 8], F32)
    e8 = sb("e8", [128, 8], F32)
    wr8 = sb("wr8", [128, 8], F32)
    ssum = sb("ssum", [128, 1], F32)
    rs = sb("rs", [128, 1], F32)
    wtok = sb("wtok", [128, 8])
    w8 = sb("w8", [8, TOK])
    wrows = sb("wrows", [1, 8, TOK])
    be1c = sb("be1c", [128, 32], F32)
    be2_sb = sb("be2_sb", [8, 8, 128])
    w1t = sb("w1t", [128, 2, 8, 128])
    w2t = sb("w2t", [128, 32, 128])
    bcm = sb("bcm", [128, 512])
    out32 = sb("out32", [128, 8, 1024], F32)
    id128 = sb("id128", [128, 128])
    icol = sb("icol", [128, 128], F32)
    irow = sb("irow", [128, 1], F32)
    ones1 = sb("ones1", [1, 128])
    onescol_f = sb("onescol_f", [128, 1], F32)
    ysq = sb("ysq", [128, 512], F32)
    st_sb = sb("st_sb", [1, 1024], F32)
    mu = sb("mu", [1, 512], F32)
    m2s = sb("m2s", [1, 512], F32)
    var = sb("var", [1, 512], F32)
    rstd = sb("rstd", [1, 512], F32)
    nmur = sb("nmur", [1, 512], F32)
    rstd_bf = sb("rstd_bf", [1, 512])
    nmur_bf = sb("nmur_bf", [1, 512])
    bcln = sb("bcln", [128, 2, 512])
    y4sb = big[:, 16384:24576].bitcast(F32).rearrange("p (a b) -> p a b", b=512)

    PB0 = ps("PB0", [128, 1024], BF16)
    PB1 = ps("PB1", [128, 1024], BF16)
    PF = [ps(f"PF{i}", [128, 512], F32) for i in range(6)]

    cG = Ctr(nc, stack, "cG", 16)
    cY = Ctr(nc, stack, "cY", 16)
    cP = Ctr(nc, stack, "cP")
    cT = Ctr(nc, stack, "cT")
    cV = Ctr(nc, stack, "cV")
    cA = Ctr(nc, stack, "cA")
    ALL = [cG, cY, cP, cT, cV, cA]
    ENG = [G, T, V, A, Y]
    MDL = dict(max_dma_last_dim=8192)
    env = {"eng": (G, T, V, A, Y), "ctr": (cG, cY, cP, cT, cV, cA), "PF": PF,
           "onescol_f": onescol_f, "ones1": ones1, "st_sb": st_sb, "mu": mu,
           "m2": m2s, "var": var, "rstd": rstd, "nmur": nmur, "rstd_bf": rstd_bf,
           "nmur_bf": nmur_bf, "bcln": bcln}

    for c in range(NCHUNK):
        cY.tick(Y.dma_start(out=ysrc[:, c, :], in_=y3[128 * c:128 * (c + 1), :]))
    cG.tick(G.dma_start(out=rw2_sb[:],
                        in_=rw2.rearrange("(kc p) e -> p kc e", p=128)))
    cG.tick(G.dma_start(out=rb1c[:], in_=rb1s[:, :]))
    cG.tick(G.dma_start(out=rb2c[:], in_=rb2[:, :]))
    cG.tick(G.dma_start(out=be2_sb[:],
                        in_=be2.rearrange("e (dc m) -> e dc m", m=128)))
    cP.tick(G.iota(icol[:], pattern=[[1, 128]], base=0, channel_multiplier=0,
                   allow_small_or_imprecise_dtypes=True))
    cP.tick(G.iota(irow[:], pattern=[[0, 1]], base=0, channel_multiplier=1,
                   allow_small_or_imprecise_dtypes=True))
    cP.tick(G.memset(ones1[:], 1.0))
    cP.tick(G.memset(onescol_f[:], 1.0))
    cP.wait(V)
    cV.tick(V.tensor_scalar(id128[:], icol[:], irow[:, 0:1], None, ALU.is_equal))
    fence(ENG, ALL)

    for mc in range(8):
        buf = mc % 2
        fence([G], [cT])
        g_w = cG.tick(G.dma_start(
            out=rw1t[:, buf, :, :],
            in_=rw1.rearrange("(kc p) d -> p kc d", p=128)[:, :, 128 * mc:128 * (mc + 1)],
            **MDL))
        T.wait_ge(cG.sem, g_w)
        for g in range(4):
            pf = PF[(mc * 4 + g) % 2]
            fence([T], [cA])
            for kc in range(8):
                ins = T.matmul(pf[:], rw1t[:, buf, kc, :],
                               ysrc[:, kc, 512 * g:512 * (g + 1)],
                               start=(kc == 0), stop=(kc == 7))
            t_h = cT.tick(ins)
            A.wait_ge(cT.sem, t_h)
            cA.tick(A.activation(t_v[:, mc, 512 * g:512 * (g + 1)], pf[:],
                                 AF.Silu, bias=rb1c[:, mc:mc + 1]))
    fence(ENG, ALL)
    for g in range(4):
        off = 512 * g
        fence([T], [cA, cV])
        for kc in range(8):
            ins = T.matmul(PF[1][0:8, :], rw2_sb[:, kc, :], t_v[:, kc, off:off + 512],
                           start=(kc == 0), stop=(kc == 7))
        t_lg = cT.tick(ins)
        V.wait_ge(cT.sem, t_lg)
        cV.tick(V.tensor_scalar(lg_sb[:], PF[1][0:8, :], rb2c[:, 0:1], None, ALU.add))
        cV.wait(T)
        for t4 in range(4):
            ins = T.transpose(PB0[0:128, 8 * t4:8 * (t4 + 1)],
                              lg_sb[:, 128 * t4:128 * (t4 + 1)], id128[0:8, 0:8])
        t_tr = cT.tick(ins)
        V.wait_ge(cT.sem, t_tr)
        cV.tick(V.tensor_copy(lgt[:], PB0[0:128, 0:32].rearrange("p (a b) -> p a b", b=8)))
        for t4 in range(4):
            cV.wait(V)
            cV.tick(V.reduce_max(nm1[:], lgt[:, t4, :], axis=AX.X, negate=True))
            cV.wait(V)
            cV.tick(V.tensor_scalar(eq8[:], lgt[:, t4, :], nm1[:, 0:1], 0.0,
                                    ALU.add, ALU.is_equal))
            cV.wait(V)
            cV.tick(V.scalar_tensor_tensor(l2[:], eq8[:], -1e9, lgt[:, t4, :],
                                           ALU.mult, ALU.add))
            cV.wait(V)
            cV.tick(V.reduce_max(nm2[:], l2[:], axis=AX.X, negate=True))
            cV.wait(V)
            cV.tick(V.tensor_scalar(msk8[:], lgt[:, t4, :], nm2[:, 0:1], 0.0,
                                    ALU.add, ALU.is_ge))
            cV.wait(A)
            cA.tick(A.activation(e8[:], lgt[:, t4, :], AF.Exp, bias=nm1[:, 0:1]))
            cA.wait(V)
            cV.tick(V.tensor_mul(wr8[:], e8[:], msk8[:]))
            cV.wait(V)
            cV.tick(V.reduce_sum(ssum[:], wr8[:], axis=AX.X))
            cV.wait(V)
            cV.tick(V.reciprocal(rs[:], ssum[:]))
            cV.wait(V)
            cV.tick(V.tensor_scalar(wtok[:], wr8[:], rs[:, 0:1], None, ALU.mult))
            cV.wait(T)
            t_f = cT.tick(T.transpose(PB1[0:8, 128 * t4:128 * (t4 + 1)], wtok[:], id128[:]))
            for e in range(8):
                ins = T.transpose(PB0[0:1, 128 * e:128 * (e + 1)],
                                  wtok[:, e:e + 1], id128[:])
            t_r = cT.tick(ins)
            V.wait_ge(cT.sem, t_r)
            cV.tick(V.tensor_copy(wrows[0:1, :, off + 128 * t4:off + 128 * (t4 + 1)],
                                  PB0[0:1, 0:1024].rearrange("p (a b) -> p a b", b=128)))
        V.wait_ge(cT.sem, cT.n)
        cV.tick(V.tensor_copy(w8[:, off:off + 512], PB1[0:8, 0:512]))
        fence([T], [cV])
    fence(ENG, ALL)

    for gh in range(2):
        goff = 1024 * gh
        fence([T], [cV, cA])
        for dc in range(8):
            for g2 in range(2):
                pf = PF[(dc * 2 + g2) % 2]
                fence([T], [cV])
                t_b = cT.tick(T.matmul(pf[:], be2_sb[:, dc, :],
                                       w8[:, goff + 512 * g2:goff + 512 * (g2 + 1)],
                                       start=True, stop=True))
                V.wait_ge(cT.sem, t_b)
                cV.tick(V.tensor_copy(out32[:, dc, 512 * g2:512 * (g2 + 1)], pf[:]))
        fence(ENG, ALL)
        for e in range(8):
            g_b1 = cG.tick(G.dma_start(out=be1c[:], in_=be1s[e, :, :]))
            A.wait_ge(cG.sem, g_b1)
            for mc in range(32):
                buf = mc % 2
                fence([G], [cT])
                g_w = cG.tick(G.dma_start(
                    out=w1t[:, buf, :, :],
                    in_=we1[e].rearrange("(kc p) hh -> p kc hh", p=128)[:, :, 128 * mc:128 * (mc + 1)],
                    **MDL))
                T.wait_ge(cG.sem, g_w)
                for g2 in range(2):
                    pf = PF[2 + (mc * 2 + g2) % 2]
                    fence([T], [cA])
                    for kc in range(8):
                        ins = T.matmul(pf[:], w1t[:, buf, kc, :],
                                       ysrc[:, kc, goff + 512 * g2:goff + 512 * (g2 + 1)],
                                       start=(kc == 0), stop=(kc == 7))
                    t_h = cT.tick(ins)
                    A.wait_ge(cT.sem, t_h)
                    cA.tick(A.activation(hh_v[:, mc, 512 * g2:512 * (g2 + 1)], pf[:],
                                         AF.Silu, bias=be1c[:, mc:mc + 1]))
            for g2 in range(2):
                fence([T], [cA])
                t_bc = cT.tick(T.matmul(PF[4][:], ones1[0:1, :],
                                        wrows[0:1, e, goff + 512 * g2:goff + 512 * (g2 + 1)],
                                        start=True, stop=True))
                A.wait_ge(cT.sem, t_bc)
                cV.wait(A)
                a_bc = cA.tick(A.activation(bcm[:], PF[4][:], AF.Copy))
                V.wait_ge(cA.sem, a_bc)
                V.wait_ge(cA.sem, cA.n)
                for mc in range(32):
                    cV.tick(V.tensor_mul(hh_v[:, mc, 512 * g2:512 * (g2 + 1)],
                                         hh_v[:, mc, 512 * g2:512 * (g2 + 1)], bcm[:]))
            for dc in range(8):
                fence([G], [cT])
                fence([T], [cA])
                g_w = cG.tick(G.dma_start(
                    out=w2t[:, :, :],
                    in_=we2[e].rearrange("(kc p) d -> p kc d", p=128)[:, :, 128 * dc:128 * (dc + 1)],
                    **MDL))
                T.wait_ge(cG.sem, g_w)
                cV.wait(T)
                for g2 in range(2):
                    pf = PF[4 + (dc * 2 + g2) % 2]
                    fence([T], [cV])
                    for kc in range(32):
                        ins = T.matmul(pf[:], w2t[:, kc, :],
                                       hh_v[:, kc, 512 * g2:512 * (g2 + 1)],
                                       start=(kc == 0), stop=(kc == 31))
                    t_o = cT.tick(ins)
                    V.wait_ge(cT.sem, t_o)
                    cV.tick(V.tensor_tensor(out32[:, dc, 512 * g2:512 * (g2 + 1)],
                                            out32[:, dc, 512 * g2:512 * (g2 + 1)],
                                            pf[:], ALU.add))
            fence(ENG, ALL)
        for g2 in range(2):
            off = goff + 512 * g2
            for dc in range(8):
                cV.wait(V)
                cV.tick(V.scalar_tensor_tensor(out32[:, dc, 512 * g2:512 * (g2 + 1)],
                                               ysrc[:, dc, off:off + 512], 2.0,
                                               out32[:, dc, 512 * g2:512 * (g2 + 1)],
                                               ALU.mult, ALU.add))
            emit_ln_group(env, out32[:, :, 512 * g2:512 * (g2 + 1)], ysq, y4sb, yout, off)
        fence(ENG, ALL)
    Y.wait_ge(cY.sem, cY.n)
    return nc


def _stage_D(inputs, y3_shards):
    rb1s = np.ascontiguousarray(inputs["L1_ffn_rb1"].reshape(8, 128).T)
    be1s = np.ascontiguousarray(
        inputs["L1_ffn_be1"].reshape(8, 32, 128).transpose(0, 2, 1))
    maps = []
    for c in range(8):
        maps.append({
            "y3": np.asarray(y3_shards[c]),
            "rw1": inputs["L1_ffn_rw1"],
            "rb1s": rb1s,
            "rw2": inputs["L1_ffn_rw2"],
            "rb2": inputs["L1_ffn_rb2"].reshape(8, 1),
            "we1": inputs["L1_ffn_we1"],
            "be1s": be1s,
            "we2": inputs["L1_ffn_we2"],
            "be2": inputs["L1_ffn_be2"],
        })
    return maps


def run_launch_D(inputs, y3_shards):
    from contextlib import ExitStack
    nc = bass.Bass("TRN2", target_bir_lowering=False)
    with ExitStack() as stack:
        build_launch_D(nc, stack)
    res = run_bass_kernel_spmd(nc, _stage_D(inputs, y3_shards), core_ids=list(range(8)))
    return [r["yout"] for r in res.results]



def _stage_A(inputs):
    x0 = inputs["L0_attn_x"]
    maps = []
    for c in range(8):
        b, h = c // 2, c % 2
        start = 2048 * h
        seg = np.zeros((TOKH, D), np.float32)
        lo = start - 128
        off = 128 if h == 0 else 0
        seg[off:] = x0[b, max(lo, 0):start + 2048]
        valid = np.zeros(TOKH, np.float32)
        valid[off:] = 1.0
        maps.append({
            "x0T": np.ascontiguousarray(seg.T),
            "validT": np.ascontiguousarray(valid.reshape(17, 128).T),
            "Wq": inputs["L0_attn_Wq"],
            "Wk": inputs["L0_attn_Wk"],
            "Wv": inputs["L0_attn_Wv"],
            "Wo": inputs["L0_attn_Wo"],
        })
    return maps


def run_launch_A(inputs):
    from contextlib import ExitStack
    nc = bass.Bass("TRN2", target_bir_lowering=False)
    with ExitStack() as stack:
        build_launch_A(nc, stack)
    res = run_bass_kernel_spmd(nc, _stage_A(inputs), core_ids=list(range(8)))
    return [r["y1"] for r in res.results]


def kernel(**inputs):
    y1 = run_launch_A(inputs)
    y2 = run_launch_B(inputs, y1)
    y3 = run_launch_C(inputs, y2)
    y4 = run_launch_D(inputs, y3)
    out = np.empty((4, 4096, 1024), np.float32)
    for c in range(8):
        out[c // 2, 2048 * (c % 2):2048 * (c % 2 + 1), :] = np.asarray(y4[c]).T
    return out


if __name__ == "__main__":
    pass


# version 02
# speedup vs baseline: 1.0017x; 1.0017x over previous
import numpy as np
import ml_dtypes
import concourse.bass as bass
import concourse.mybir as mybir
from concourse.bass_utils import run_bass_kernel_spmd

F32 = mybir.dt.float32
BF16 = mybir.dt.bfloat16
AF = mybir.ActivationFunctionType
ALU = mybir.AluOpType
AX = mybir.AxisListType
BF = ml_dtypes.bfloat16

D = 1024
NCHUNK = 8
TOK = 2048
TOKH = TOK + 128
NH = 16
HD = 64
NPAIR = 8
WIN = 64


class Ctr:
    def __init__(self, nc, stack, name, step=1):
        self.sem = stack.enter_context(nc.semaphore(name))
        self.n = 0
        self.step = step

    def tick(self, ins, k=None):
        k = self.step if k is None else k
        ins.then_inc(self.sem, k)
        self.n += k
        return self.n

    def wait(self, eng, n=None):
        eng.wait_ge(self.sem, self.n if n is None else n)


def fence(engines, ctrs):
    for e in engines:
        for c in ctrs:
            if c.n > 0:
                c.wait(e)



def emit_ln_group(env, ysb, ysq, outsb, out_dram, off, out_w=512, sb_off=None):
    (G, T, V, A, Y) = env["eng"]
    (cG, cY, cP, cT, cV, cA) = env["ctr"]
    PF = env["PF"]
    onescol_f, ones1 = env["onescol_f"], env["ones1"]
    st_sb, mu, m2, var, rstd, nmur = env["st_sb"], env["mu"], env["m2"], env["var"], env["rstd"], env["nmur"]
    rstd_bf, nmur_bf, bcln = env["rstd_bf"], env["nmur_bf"], env["bcln"]
    AFL, ALUL, w = AF, ALU, out_w
    fence([T, A], [cV])
    for dc in range(8):
        ins = T.matmul(PF[2][0:1, 0:w], onescol_f[:, 0:1], ysb[:, dc, 0:w],
                       start=(dc == 0), stop=(dc == 7))
    cT.tick(ins)
    for dc in range(8):
        a_sq = cA.tick(A.activation(ysq[:, 0:w], ysb[:, dc, 0:w], AFL.Square))
        T.wait_ge(cA.sem, a_sq)
        ins = T.matmul(PF[3][0:1, 0:w], onescol_f[:, 0:1], ysq[:, 0:w],
                       start=(dc == 0), stop=(dc == 7))
        t_s2 = cT.tick(ins)
        if dc < 7:
            A.wait_ge(cT.sem, t_s2)
    V.wait_ge(cT.sem, cT.n)
    cV.tick(V.tensor_copy(st_sb[0:1, 0:w], PF[2][0:1, 0:w]))
    cV.tick(V.tensor_copy(st_sb[0:1, 512:512 + w], PF[3][0:1, 0:w]))
    cV.wait(V)
    cV.tick(V.tensor_single_scalar(mu[0:1, 0:w], st_sb[0:1, 0:w], 1.0 / 1024, ALUL.mult))
    cV.tick(V.tensor_single_scalar(m2[0:1, 0:w], st_sb[0:1, 512:512 + w], 1.0 / 1024, ALUL.mult))
    cV.wait(V)
    cV.tick(V.tensor_mul(var[0:1, 0:w], mu[0:1, 0:w], mu[0:1, 0:w]))
    cV.wait(V)
    cV.tick(V.tensor_sub(var[0:1, 0:w], m2[0:1, 0:w], var[0:1, 0:w]))
    cV.wait(V)
    cV.tick(V.tensor_single_scalar(var[0:1, 0:w], var[0:1, 0:w], 1e-5, ALUL.add))
    cV.wait(A)
    cA.tick(A.activation(var[0:1, 0:w], var[0:1, 0:w], AFL.Ln))
    cA.wait(A)
    a_r = cA.tick(A.activation(rstd[0:1, 0:w], var[0:1, 0:w], AFL.Exp, scale=-0.5))
    V.wait_ge(cA.sem, a_r)
    cV.tick(V.tensor_mul(nmur[0:1, 0:w], mu[0:1, 0:w], rstd[0:1, 0:w]))
    cV.wait(V)
    cV.tick(V.tensor_single_scalar(nmur[0:1, 0:w], nmur[0:1, 0:w], -1.0, ALUL.mult))
    cV.tick(V.tensor_copy(rstd_bf[0:1, 0:w], rstd[0:1, 0:w]))
    cV.wait(V)
    cV.tick(V.tensor_copy(nmur_bf[0:1, 0:w], nmur[0:1, 0:w]))
    cV.wait(T)
    cT.tick(T.matmul(PF[4][:, 0:w], ones1[0:1, :], rstd_bf[0:1, 0:w], start=True, stop=True))
    t_b2 = cT.tick(T.matmul(PF[5][:, 0:w], ones1[0:1, :], nmur_bf[0:1, 0:w], start=True, stop=True))
    A.wait_ge(cT.sem, t_b2)
    cA.tick(A.activation(bcln[:, 0, 0:w], PF[4][:, 0:w], AFL.Copy))
    a_bc = cA.tick(A.activation(bcln[:, 1, 0:w], PF[5][:, 0:w], AFL.Copy))
    V.wait_ge(cA.sem, a_bc)
    so = 0 if sb_off is None else sb_off
    for dc in range(8):
        cV.tick(V.tensor_mul(ysq[:, 0:w], ysb[:, dc, 0:w], bcln[:, 0, 0:w]))
        V.wait_ge(cV.sem, cV.n)
        v_ap = cV.tick(V.tensor_add(outsb[:, dc, so:so + w], ysq[:, 0:w], bcln[:, 1, 0:w]))
        if out_dram is not None:
            Y.wait_ge(cV.sem, v_ap)
            cY.tick(Y.dma_start(out=out_dram[128 * dc:128 * (dc + 1), off:off + w],
                                in_=outsb[:, dc, so:so + w]))
    fence([T, A], [cV, cY])



def mk_env(nc, stack):
    G, T, V, A, Y = nc.gpsimd, nc.tensor, nc.vector, nc.scalar, nc.sync
    sb = lambda name, shape, dt=BF16: stack.enter_context(nc.sbuf_tensor(name, shape, dt))
    ps = lambda name, shape, dt=F32: stack.enter_context(nc.psum_tensor(name, shape, dt))
    env = {}
    env["nc"] = nc
    env["eng"] = (G, T, V, A, Y)
    PB0 = ps("PB0", [128, 1024], BF16)
    PB1 = ps("PB1", [128, 1024], BF16)
    PF = [ps(f"PF{i}", [128, 512], F32) for i in range(6)]
    env["PB"] = (PB0, PB1)
    env["PF"] = PF
    cG = Ctr(nc, stack, "cG", 16)
    cY = Ctr(nc, stack, "cY", 16)
    cP = Ctr(nc, stack, "cP")
    cT = Ctr(nc, stack, "cT")
    cV = Ctr(nc, stack, "cV")
    cA = Ctr(nc, stack, "cA")
    env["ctr"] = (cG, cY, cP, cT, cV, cA)
    env["ALL"] = [cG, cY, cP, cT, cV, cA]
    env["ENG"] = [G, T, V, A, Y]
    for n in ("icol", "iot"):
        env[n] = sb(n, [128, 128], F32)
    env["irow"] = sb("irow", [128, 1], F32)
    env["irow64"] = sb("irow64", [128, 1], F32)
    env["id128"] = sb("id128", [128, 128])
    env["id64d"] = sb("id64d", [128, 64])
    env["ones1"] = sb("ones1", [1, 128])
    env["onescol_f"] = sb("onescol_f", [128, 1], F32)
    env["onescol_b"] = sb("onescol_b", [128, 1])
    env["ysq"] = sb("ysq", [128, 512], F32)
    env["st_sb"] = sb("st_sb", [1, 1024], F32)
    env["mu"] = sb("mu", [1, 512], F32)
    env["m2"] = sb("m2", [1, 512], F32)
    env["var"] = sb("var", [1, 512], F32)
    env["rstd"] = sb("rstd", [1, 512], F32)
    env["nmur"] = sb("nmur", [1, 512], F32)
    env["rstd_bf"] = sb("rstd_bf", [1, 512])
    env["nmur_bf"] = sb("nmur_bf", [1, 512])
    env["bcln"] = sb("bcln", [128, 2, 512])
    env["stream"] = sb("stream", [128, NCHUNK, TOK])
    G, T, V, A, Y = env["eng"]
    cG, cY, cP, cT, cV, cA = env["ctr"]
    cP.tick(G.iota(env["iot"][:], pattern=[[-1, 128]], base=0, channel_multiplier=1,
                   allow_small_or_imprecise_dtypes=True))
    cP.tick(G.iota(env["icol"][:], pattern=[[1, 128]], base=0, channel_multiplier=0,
                   allow_small_or_imprecise_dtypes=True))
    cP.tick(G.iota(env["irow"][:], pattern=[[0, 1]], base=0, channel_multiplier=1,
                   allow_small_or_imprecise_dtypes=True))
    cP.tick(G.memset(env["ones1"][:], 1.0))
    cP.tick(G.memset(env["onescol_f"][:], 1.0))
    cP.tick(G.memset(env["onescol_b"][:], 1.0))
    cP.wait(V)
    cV.tick(V.tensor_scalar(env["id128"][:], env["icol"][:], env["irow"][:, 0:1], None, ALU.is_equal))
    cV.tick(V.tensor_single_scalar(env["irow64"][:], env["irow"][:], 64.0, ALU.is_ge))
    cV.wait(V)
    cV.tick(V.scalar_tensor_tensor(env["irow64"][:], env["irow64"][:], -64.0, env["irow"][:], ALU.mult, ALU.add))
    cV.wait(V)
    cV.tick(V.tensor_scalar(env["id64d"][:], env["icol"][:, 0:64], env["irow64"][:, 0:1], None, ALU.is_equal))
    fence(env["ENG"], env["ALL"])
    return env



def emit_A(nc, stack, env):
    G, T, V, A, Y = env["eng"]
    x0T = nc.dram_tensor("x0T", [D, TOKH], F32, kind="ExternalInput").ap()
    validT = nc.dram_tensor("validT", [128, 17], F32, kind="ExternalInput").ap()
    Wq = nc.dram_tensor("Wq", [D, D], F32, kind="ExternalInput").ap()
    Wk = nc.dram_tensor("Wk", [D, D], F32, kind="ExternalInput").ap()
    Wv = nc.dram_tensor("Wv", [D, D], F32, kind="ExternalInput").ap()
    Wo = nc.dram_tensor("Wo", [D, D], F32, kind="ExternalInput").ap()

    sb = lambda name, shape, dt=BF16: stack.enter_context(nc.sbuf_tensor(name, shape, dt))

    x0b = sb("x0b", [128, NCHUNK, TOKH])
    vld = sb("vld", [128, 17], F32)
    wraw = sb("wraw", [128, D])
    wT = {w: sb(f"w{w}T", [128, 8, 128]) for w in "qkv"}
    qT = sb("qT", [128, TOK])
    kT = sb("kT", [128, TOKH])
    vT = sb("vT", [128, TOKH])
    vt = sb("vt", [128, 2, 17, 65])
    sq = sb("sq", [128, TOKH])
    pt = sb("pt", [128, 2, 256])
    tmpm = sb("tmpm", [128, 2, 128], F32)
    maskAB = sb("maskAB", [128, 256])
    nmax = sb("nmax", [1, 32], F32)
    bnd = sb("bnd", [1, 8], F32)
    bndbf = sb("bndbf", [1, 1])
    boundc = sb("boundc", [128, 2], F32)
    lr = sb("lr", [1, 256], F32)
    rbf = sb("rbf", [1, 256])
    bc_sb = sb("bc_sb", [128, 256], F32)
    attnT = sb("attnT", [128, NPAIR, TOK])
    woT = sb("woT", [128, 8, 8, 128])

    cG, cY, cP, cT, cV, cA = env["ctr"]
    PB0, PB1 = env["PB"]
    PF = env["PF"]
    ALL = env["ALL"]
    ENG = env["ENG"]
    id128, id64d = env["id128"], env["id64d"]
    ones1, onescol_f, onescol_b = env["ones1"], env["onescol_f"], env["onescol_b"]
    iot = env["iot"]
    ysb = env["ysb"] = sb("ysbA", [128, NCHUNK, 512], F32)
    ysq = env["ysq"]
    stream = env["stream"]

    MDL = dict(max_dma_last_dim=8192)

    for c in range(NCHUNK):
        cG.tick(G.dma_start(out=x0b[:, c, :], in_=x0T[128 * c:128 * (c + 1), :], **MDL))
    cG.tick(G.dma_start(out=vld[:], in_=validT[:, :]))
    cP.wait(V)
    cV.tick(V.tensor_single_scalar(tmpm[:, 0, :], iot[:], 65.0, ALU.is_ge))
    cV.tick(V.tensor_single_scalar(tmpm[:, 1, :], iot[:], 128.0, ALU.is_le))
    cV.wait(V)
    cV.tick(V.tensor_mul(maskAB[:, 0:128], tmpm[:, 0, :], tmpm[:, 1, :]))
    cV.tick(V.tensor_single_scalar(tmpm[:, 0, :], iot[:], -63.0, ALU.is_ge))
    cV.tick(V.tensor_single_scalar(tmpm[:, 1, :], iot[:], 0.0, ALU.is_le))
    cV.wait(V)
    cV.tick(V.tensor_mul(maskAB[:, 128:256], tmpm[:, 0, :], tmpm[:, 1, :]))
    fence(ENG, ALL)

    kv_groups = [(0, 512), (512, 512), (1024, 512), (1536, 512), (2048, 128)]
    q_groups = [(0, 512), (512, 512), (1024, 512), (1536, 512)]

    for p in range(NPAIR):
        for wname, wdram in (("q", Wq), ("k", Wk), ("v", Wv)):
            fence([G], [cT])
            g_ld = cG.tick(G.dma_start(out=wraw[:], in_=wdram[128 * p:128 * (p + 1), :], **MDL))
            T.wait_ge(cG.sem, g_ld)
            for half in range(2):
                pb = (PB0, PB1)[half]
                fence([T], [cV])
                for k4 in range(4):
                    k = half * 4 + k4
                    ins = T.transpose(pb[:, 128 * k4:128 * (k4 + 1)],
                                      wraw[:, 128 * k:128 * (k + 1)], id128[:])
                cT.tick(ins)
                cT.wait(V)
                cV.tick(V.tensor_copy(wT[wname][:, 4 * half:4 * half + 4, :],
                                      pb[:, 0:512].rearrange("p (a b) -> p a b", b=128)))
        fence([T], [cV, cG])
        for wname, dst, groups, coloff in (("q", qT, q_groups, 128),
                                           ("k", kT, kv_groups, 0),
                                           ("v", vT, kv_groups, 0)):
            for gi, (off, gw) in enumerate(groups):
                pf = PF[gi % 2]
                fence([T], [cA])
                for k in range(8):
                    ins = T.matmul(pf[:, 0:gw], wT[wname][:, k, :],
                                   x0b[:, k, coloff + off:coloff + off + gw],
                                   start=(k == 0), stop=(k == 7))
                cT.tick(ins)
                cT.wait(A)
                cA.tick(A.activation(dst[:, off:off + gw], pf[:, 0:gw], AF.Copy))
        fence([T, V], [cA])
        for h2 in range(2):
            for c0 in range(0, 17, 4):
                cnt = min(4, 17 - c0)
                pb = (PB0, PB1)[(c0 // 4) % 2]
                fence([T], [cV])
                for cc in range(cnt):
                    ins = T.transpose(pb[:, 64 * cc:64 * (cc + 1)],
                                      vT[64 * h2:64 * (h2 + 1), 128 * (c0 + cc):128 * (c0 + cc + 1)],
                                      id64d[64 * h2:64 * (h2 + 1), 0:64])
                cT.tick(ins)
                cT.wait(V)
                cV.tick(V.tensor_copy(vt[:, h2, c0:c0 + cnt, 0:64],
                                      pb[:, 0:64 * cnt].rearrange("p (a b) -> p a b", b=64)))
            cV.tick(V.tensor_copy(vt[:, h2, :, 64], vld[:, 0:17]))
        fence([A], [cT, cV])
        for src, groups, gbase in ((qT, q_groups, 0), (kT, kv_groups, 16)):
            a_sq = cA.tick(A.activation(sq[:, 0:src.shape[-1]], src[:], AF.Square))
            T.wait_ge(cA.sem, a_sq)
            for h2 in range(2):
                for gi, (off, gw) in enumerate(groups):
                    cV.wait(T)
                    ins = T.matmul(PF[5][0:1, 0:gw],
                                   onescol_b[64 * h2:64 * (h2 + 1), 0:1],
                                   sq[64 * h2:64 * (h2 + 1), off:off + gw],
                                   start=True, stop=True)
                    t_mm = cT.tick(ins)
                    V.wait_ge(cT.sem, t_mm)
                    cV.tick(V.reduce_max(nmax[0:1, gbase + 8 * h2 + gi:gbase + 8 * h2 + gi + 1],
                                         PF[5][0:1, 0:gw], axis=AX.X))
            fence([A], [cT])
        for h2 in range(2):
            cV.wait(V)
            cV.tick(V.reduce_max(bnd[0:1, 0:1], nmax[0:1, 8 * h2:8 * h2 + 4], axis=AX.X))
            cV.tick(V.reduce_max(bnd[0:1, 1:2], nmax[0:1, 16 + 8 * h2:16 + 8 * h2 + 5], axis=AX.X))
            cV.wait(V)
            cV.tick(V.tensor_mul(bnd[0:1, 2:3], bnd[0:1, 0:1], bnd[0:1, 1:2]))
            cV.wait(A)
            cA.tick(A.activation(bnd[0:1, 3:4], bnd[0:1, 2:3], AF.Sqrt))
            cA.wait(V)
            cV.tick(V.tensor_single_scalar(bnd[0:1, 4:5], bnd[0:1, 3:4], -0.125, ALU.mult))
            cV.wait(V)
            cV.tick(V.tensor_copy(bndbf[0:1, 0:1], bnd[0:1, 4:5]))
            cV.wait(T)
            cA.wait(T)
            t_mm = cT.tick(T.matmul(PF[5][:, 0:1], ones1[0:1, :], bndbf[0:1, 0:1],
                                    start=True, stop=True))
            A.wait_ge(cT.sem, t_mm)
            cA.tick(A.activation(boundc[:, h2:h2 + 1], PF[5][:, 0:1], AF.Copy))
        fence(ENG, [cT, cV, cA])
        for i in range(16):
            for h2 in range(2):
                spf = PF[2 + h2]
                for ci, cc in enumerate((i, i + 1)):
                    ins = T.matmul(spf[:, 128 * ci:128 * (ci + 1)],
                                   kT[64 * h2:64 * (h2 + 1), 128 * cc:128 * (cc + 1)],
                                   qT[64 * h2:64 * (h2 + 1), 128 * i:128 * (i + 1)],
                                   start=True, stop=True)
                t_s = cT.tick(ins)
                A.wait_ge(cT.sem, t_s)
                cA.tick(A.activation(pt[:, h2, :], spf[:, 0:256], AF.Exp,
                                     bias=boundc[:, h2:h2 + 1], scale=0.125))
            cA.wait(V)
            for h2 in range(2):
                cV.tick(V.tensor_mul(pt[:, h2, :], pt[:, h2, :], maskAB[:]))
            cV.wait(T)
            for h2 in range(2):
                T.matmul(PF[4][64 * h2:64 * (h2 + 1), 0:128], vt[:, h2, i, 0:64],
                         pt[:, h2, 0:128], start=True, stop=False,
                         tile_position=(0, 64 * h2))
                ins = T.matmul(PF[4][64 * h2:64 * (h2 + 1), 0:128], vt[:, h2, i + 1, 0:64],
                               pt[:, h2, 128:256], start=False, stop=True,
                               tile_position=(0, 64 * h2))
                cT.tick(ins)
                T.matmul(PF[5][0:1, 128 * h2:128 * (h2 + 1)], vt[:, h2, i, 64:65],
                         pt[:, h2, 0:128], start=True, stop=False)
                ins = T.matmul(PF[5][0:1, 128 * h2:128 * (h2 + 1)], vt[:, h2, i + 1, 64:65],
                               pt[:, h2, 128:256], start=False, stop=True)
                t_pv = cT.tick(ins)
            V.wait_ge(cT.sem, t_pv)
            cV.tick(V.tensor_copy(lr[0:1, 0:256], PF[5][0:1, 0:256]))
            cV.wait(A)
            cA.tick(A.activation(lr[0:1, :], lr[0:1, :], AF.Ln))
            cA.wait(A)
            a_r = cA.tick(A.activation(rbf[0:1, :], lr[0:1, :], AF.Exp, scale=-1.0))
            T.wait_ge(cA.sem, a_r)
            for h2 in range(2):
                ins = T.matmul(PF[0][:, 128 * h2:128 * (h2 + 1)], ones1[0:1, :],
                               rbf[0:1, 128 * h2:128 * (h2 + 1)], start=True, stop=True)
            t_bc = cT.tick(ins)
            A.wait_ge(cT.sem, t_bc)
            a_bc = cA.tick(A.activation(bc_sb[:, 0:256], PF[0][:, 0:256], AF.Copy))
            V.wait_ge(cA.sem, a_bc)
            for h2 in range(2):
                cV.tick(V.tensor_mul(attnT[64 * h2:64 * (h2 + 1), p, 128 * i:128 * (i + 1)],
                                     PF[4][64 * h2:64 * (h2 + 1), 0:128],
                                     bc_sb[64 * h2:64 * (h2 + 1), 128 * h2:128 * (h2 + 1)]))
            fence([T], [cA, cV])
        fence(ENG, ALL)

    for dc in range(8):
        fence([G], [cT])
        g_ld = cG.tick(G.dma_start(out=wraw[:], in_=Wo[128 * dc:128 * (dc + 1), :], **MDL))
        T.wait_ge(cG.sem, g_ld)
        for half in range(2):
            pb = (PB0, PB1)[half]
            fence([T], [cV])
            for k4 in range(4):
                k = half * 4 + k4
                ins = T.transpose(pb[:, 128 * k4:128 * (k4 + 1)],
                                  wraw[:, 128 * k:128 * (k + 1)], id128[:])
            cT.tick(ins)
            cT.wait(V)
            cV.tick(V.tensor_copy(woT[:, dc, 4 * half:4 * half + 4, :],
                                  pb[:, 0:512].rearrange("p (a b) -> p a b", b=128)))
    fence(ENG, ALL)

    for g in range(4):
        off = 512 * g
        for dc in range(8):
            pf = PF[dc % 2]
            fence([T], [cV])
            for k in range(8):
                ins = T.matmul(pf[:], woT[:, dc, k, :], attnT[:, k, off:off + 512],
                               start=(k == 0), stop=(k == 7))
            t_mm = cT.tick(ins)
            V.wait_ge(cT.sem, t_mm)
            cV.tick(V.tensor_add(ysb[:, dc, :], pf[:], x0b[:, dc, 128 + off:128 + off + 512]))
        emit_ln_group(env, ysb, ysq, stream, None, off, sb_off=off)
    fence(ENG, ALL)



def emit_B(nc, stack, env, debug=False):
    G, T, V, A, Y = env["eng"]
    w_red = nc.dram_tensor("w_red", [D, 16], F32, kind="ExternalInput").ap()
    wg = nc.dram_tensor("wg", [8, 16], F32, kind="ExternalInput").ap()
    W1 = nc.dram_tensor("W1", [8, D, 2048], F32, kind="ExternalInput").ap()
    b1s = nc.dram_tensor("b1s", [8, 128, 16], F32, kind="ExternalInput").ap()
    W2 = nc.dram_tensor("W2", [8, 2048, D], F32, kind="ExternalInput").ap()
    b2 = nc.dram_tensor("b2", [8, D], F32, kind="ExternalInput").ap()
    y2 = nc.dram_tensor("y2", [D, TOK], BF16, kind="ExternalOutput").ap()
    if debug:
        dbg_m8 = nc.dram_tensor("dbg_m8", [8, TOK], F32, kind="ExternalOutput").ap()
        dbg_mrows = nc.dram_tensor("dbg_mrows", [8, TOK], F32, kind="ExternalOutput").ap()
        dbg_out32 = nc.dram_tensor("dbg_out32", [D, 1024], F32, kind="ExternalOutput").ap()
        dbg_lgt = nc.dram_tensor("dbg_lgt", [128, 4, 8], F32, kind="ExternalOutput").ap()
        dbg_r = nc.dram_tensor("dbg_r", [16, 520], F32, kind="ExternalOutput").ap()

    sb = lambda name, shape, dt=BF16: stack.enter_context(nc.sbuf_tensor(name, shape, dt))

    ysrc = env["stream"]
    wred_sb = sb("wred_sb", [128, 8, 16])
    wg_sb = sb("wg_sb", [8, 16], F32)
    wgsq = sb("wgsq", [8, 16], F32)
    wgn = sb("wgn", [8, 2], F32)
    wgnb = sb("wgnb", [8, 16])
    wgnT = sb("wgnT", [16, 8])
    xr_bf = sb("xr_bf", [16, 512])
    lg_sb = sb("lg_sb", [8, 512])
    lgt = sb("lgt", [128, 4, 8], F32)
    nm1 = sb("nm1", [128, 1], F32)
    eq8 = sb("eq8", [128, 8], F32)
    e8 = sb("e8", [128, 8], F32)
    ssum = sb("ssum", [128, 1], F32)
    rs = sb("rs", [128, 1], F32)
    mtok = sb("mtok", [128, 8])
    m8 = sb("m8", [8, TOK])
    mrows = sb("mrows", [1, 8, TOK])
    b1c = sb("b1c", [128, 16], F32)
    b2_sb = sb("b2_sb", [8, 8, 128])
    w1t = sb("w1t", [128, 2, 8, 128])
    w2t = sb("w2t", [128, 2, 16, 128])
    hh = sb("hh", [128, 16, 1024])
    bcm = sb("bcm", [128, 512])
    out32 = sb("out32", [128, 8, 1024], F32)

    cG, cY, cP, cT, cV, cA = env["ctr"]
    PB0, PB1 = env["PB"]
    PF = env["PF"]
    ALL = env["ALL"]
    ENG = env["ENG"]
    id128 = env["id128"]
    ones1, onescol_f = env["ones1"], env["onescol_f"]
    ysq = env["ysq"]
    MDL = dict(max_dma_last_dim=8192)

    cG.tick(G.dma_start(out=wred_sb[:],
                        in_=w_red.rearrange("(kc p) g -> p kc g", p=128)))
    cG.tick(G.dma_start(out=wg_sb[:], in_=wg[:, :]))
    cG.tick(G.dma_start(out=b2_sb[:],
                        in_=b2.rearrange("e (dc m) -> e dc m", m=128)))
    fence(ENG, ALL)

    cA.tick(A.activation(wgsq[:], wg_sb[:], AF.Square))
    cA.wait(V)
    cV.tick(V.reduce_sum(wgn[:, 0:1], wgsq[:], axis=AX.X))
    cV.wait(A)
    cA.tick(A.activation(wgn[:, 1:2], wgn[:, 0:1], AF.Sqrt))
    cA.wait(V)
    cV.tick(V.tensor_scalar_max(wgn[:, 1:2], wgn[:, 1:2], 1e-4))
    cV.wait(V)
    cV.tick(V.reciprocal(wgn[:, 0:1], wgn[:, 1:2]))
    cV.wait(V)
    cV.tick(V.tensor_scalar(wgnb[:], wg_sb[:], wgn[:, 0:1], None, ALU.mult))
    cV.wait(T)
    cT.tick(T.transpose(PB0[0:16, 0:8], wgnb[:], id128[0:8, 0:8]))
    cT.wait(V)
    cV.tick(V.tensor_copy(wgnT[:], PB0[0:16, 0:8]))
    fence(ENG, ALL)

    if debug:
        fence(ENG, ALL)
        cG.tick(G.dma_start(out=dbg_r[0:8, 0:2], in_=wgn[:]))
        cG.tick(G.dma_start(out=dbg_r[0:8, 2:18], in_=wgnb[:]))
        cG.tick(G.dma_start(out=dbg_r[0:16, 18:26], in_=wgnT[:]))
        cG.tick(G.dma_start(out=dbg_r[0:8, 26:42], in_=wg_sb[:]))
    for g in range(4):
        off = 512 * g
        fence([T], [cA, cV])
        for kc in range(8):
            ins = T.matmul(PF[0][0:16, :], wred_sb[:, kc, :], ysrc[:, kc, off:off + 512],
                           start=(kc == 0), stop=(kc == 7))
        t_xr = cT.tick(ins)
        A.wait_ge(cT.sem, t_xr)
        a_xr = cA.tick(A.activation(xr_bf[:], PF[0][0:16, :], AF.Copy))
        T.wait_ge(cA.sem, a_xr)
        t_lg = cT.tick(T.matmul(PF[1][0:8, :], wgnT[:], xr_bf[:], start=True, stop=True))
        V.wait_ge(cT.sem, t_lg)
        cV.tick(V.tensor_copy(lg_sb[:], PF[1][0:8, :]))
        if debug and g == 0:
            v_d = cV.n
            G.wait_ge(cV.sem, v_d)
            cG.tick(G.dma_start(out=dbg_r[0:16, 42:42 + 128], in_=xr_bf[:, 0:128]))
            cG.tick(G.dma_start(out=dbg_r[0:8, 256:256 + 256], in_=lg_sb[:, 0:256]))
        cV.wait(T)
        for t4 in range(4):
            ins = T.transpose(PB0[0:128, 8 * t4:8 * (t4 + 1)],
                              lg_sb[:, 128 * t4:128 * (t4 + 1)], id128[0:8, 0:8])
        t_tr = cT.tick(ins)
        V.wait_ge(cT.sem, t_tr)
        cV.tick(V.tensor_copy(lgt[:], PB0[0:128, 0:32].rearrange("p (a b) -> p a b", b=8)))
        for t4 in range(4):
            cV.wait(V)
            cV.tick(V.reduce_max(nm1[:], lgt[:, t4, :], axis=AX.X, negate=True))
            cV.wait(V)
            cV.tick(V.tensor_scalar(eq8[:], lgt[:, t4, :], nm1[:, 0:1], 0.0,
                                    ALU.add, ALU.is_equal))
            cV.wait(A)
            cA.tick(A.activation(e8[:], lgt[:, t4, :], AF.Exp, bias=nm1[:, 0:1],
                                 accum_out=ssum[:, 0:1]))
            cA.wait(V)
            cV.tick(V.reciprocal(rs[:], ssum[:]))
            cV.wait(V)
            cV.tick(V.tensor_scalar(mtok[:], eq8[:], rs[:, 0:1], None, ALU.mult))
            cV.wait(T)
            t_f = cT.tick(T.transpose(PB1[0:8, 128 * t4:128 * (t4 + 1)], mtok[:], id128[:]))
            for e in range(8):
                ins = T.transpose(PB0[0:1, 128 * e:128 * (e + 1)],
                                  mtok[:, e:e + 1], id128[:])
            t_r = cT.tick(ins)
            V.wait_ge(cT.sem, t_r)
            cV.tick(V.tensor_copy(mrows[0:1, :, off + 128 * t4:off + 128 * (t4 + 1)],
                                  PB0[0:1, 0:1024].rearrange("p (a b) -> p a b", b=128)))
        V.wait_ge(cT.sem, cT.n)
        cV.tick(V.tensor_copy(m8[:, off:off + 512], PB1[0:8, 0:512]))
        fence([T], [cV])
        if debug and g == 0:
            cY.tick(Y.dma_start(out=dbg_lgt[:, :, :], in_=lgt[:]))
    fence(ENG, ALL)
    if debug:
        cG.tick(G.dma_start(out=dbg_m8[:, :], in_=m8[:]))
        for e in range(8):
            cG.tick(G.dma_start(out=dbg_mrows[e:e + 1, :], in_=mrows[0:1, e, :]))

    for gh in range(2):
        goff = 1024 * gh
        fence([T], [cV, cA])
        for dc in range(8):
            for g2 in range(2):
                pf = PF[(dc * 2 + g2) % 2]
                fence([T], [cV])
                t_b = cT.tick(T.matmul(pf[:], b2_sb[:, dc, :],
                                       m8[:, goff + 512 * g2:goff + 512 * (g2 + 1)],
                                       start=True, stop=True))
                V.wait_ge(cT.sem, t_b)
                cV.tick(V.tensor_copy(out32[:, dc, 512 * g2:512 * (g2 + 1)], pf[:]))
        fence(ENG, ALL)
        for e in range(8):
            g_b1 = cG.tick(G.dma_start(out=b1c[:], in_=b1s[e, :, :]))
            A.wait_ge(cG.sem, g_b1)
            for mc in range(16):
                buf = mc % 2
                fence([G], [cT])
                g_w = cG.tick(G.dma_start(
                    out=w1t[:, buf, :, :],
                    in_=W1[e].rearrange("(kc p) hh -> p kc hh", p=128)[:, :, 128 * mc:128 * (mc + 1)],
                    **MDL))
                T.wait_ge(cG.sem, g_w)
                for g2 in range(2):
                    pf = PF[2 + (mc * 2 + g2) % 2]
                    fence([T], [cA])
                    for kc in range(8):
                        ins = T.matmul(pf[:], w1t[:, buf, kc, :],
                                       ysrc[:, kc, goff + 512 * g2:goff + 512 * (g2 + 1)],
                                       start=(kc == 0), stop=(kc == 7))
                    t_h = cT.tick(ins)
                    A.wait_ge(cT.sem, t_h)
                    cA.tick(A.activation(hh[:, mc, 512 * g2:512 * (g2 + 1)], pf[:],
                                         AF.Gelu, bias=b1c[:, mc:mc + 1]))
            for g2 in range(2):
                fence([T], [cA])
                t_bc = cT.tick(T.matmul(PF[4][:], ones1[0:1, :],
                                        mrows[0:1, e, goff + 512 * g2:goff + 512 * (g2 + 1)],
                                        start=True, stop=True))
                A.wait_ge(cT.sem, t_bc)
                cV.wait(A)
                a_bc = cA.tick(A.activation(bcm[:], PF[4][:], AF.Copy))
                V.wait_ge(cA.sem, a_bc)
                V.wait_ge(cA.sem, cA.n)
                for mc in range(16):
                    cV.tick(V.tensor_mul(hh[:, mc, 512 * g2:512 * (g2 + 1)],
                                         hh[:, mc, 512 * g2:512 * (g2 + 1)], bcm[:]))
            for dc in range(8):
                buf = dc % 2
                fence([G], [cT])
                fence([T], [cA])
                g_w = cG.tick(G.dma_start(
                    out=w2t[:, buf, :, :],
                    in_=W2[e].rearrange("(kc p) d -> p kc d", p=128)[:, :, 128 * dc:128 * (dc + 1)],
                    **MDL))
                T.wait_ge(cG.sem, g_w)
                cV.wait(T)
                for g2 in range(2):
                    pf = PF[4 + (dc * 2 + g2) % 2]
                    fence([T], [cV])
                    for kc in range(16):
                        ins = T.matmul(pf[:], w2t[:, buf, kc, :],
                                       hh[:, kc, 512 * g2:512 * (g2 + 1)],
                                       start=(kc == 0), stop=(kc == 15))
                    t_o = cT.tick(ins)
                    V.wait_ge(cT.sem, t_o)
                    cV.tick(V.tensor_tensor(out32[:, dc, 512 * g2:512 * (g2 + 1)],
                                            out32[:, dc, 512 * g2:512 * (g2 + 1)],
                                            pf[:], ALU.add))
            fence(ENG, ALL)
        if debug and gh == 0:
            for dc in range(8):
                cY.tick(Y.dma_start(out=dbg_out32[128 * dc:128 * (dc + 1), :],
                                    in_=out32[:, dc, :]))
            Y.wait_ge(cY.sem, cY.n)
        for g2 in range(2):
            off = goff + 512 * g2
            for dc in range(8):
                cV.wait(V)
                cV.tick(V.tensor_add(out32[:, dc, 512 * g2:512 * (g2 + 1)],
                                     out32[:, dc, 512 * g2:512 * (g2 + 1)],
                                     ysrc[:, dc, off:off + 512]))
            emit_ln_group(env, out32[:, :, 512 * g2:512 * (g2 + 1)], ysq,
                          env["stream"], y2, off, sb_off=off)
        fence(ENG, ALL)


def _stage_B(inputs, y1_shards):
    maps = []
    b1s = np.ascontiguousarray(
        inputs["L0_ffn_b1"].reshape(8, 16, 128).transpose(0, 2, 1))
    for c in range(8):
        maps.append({
            "y1": None if y1_shards[c] is None else np.asarray(y1_shards[c]),
            "w_red": inputs["L0_ffn_w_red"],
            "wg": inputs["L0_ffn_wg"],
            "W1": inputs["L0_ffn_W1"],
            "b1s": b1s,
            "W2": inputs["L0_ffn_W2"],
            "b2": inputs["L0_ffn_b2"],
        })
    return maps






def emit_C(nc, stack, env):
    G, T, V, A, Y = env["eng"]
    S = 4096
    kvT = nc.dram_tensor("kvT", [D, S], BF16, kind="ExternalInput").ap()
    qinT = nc.dram_tensor("qinT", [D, TOK], BF16, kind="ExternalInput").ap()
    resT = nc.dram_tensor("resT", [D, TOK], BF16, kind="ExternalInput").ap()
    h32 = nc.dram_tensor("h32", [128, 1], F32, kind="ExternalInput").ap()
    Wqkv = nc.dram_tensor("Wqkv", [NH, D, 192], F32, kind="ExternalInput").ap()
    Wout = nc.dram_tensor("Wout", [D, D], F32, kind="ExternalInput").ap()
    o_dram = nc.dram_tensor("o_dram", [NH, TOK, HD], BF16, kind="Internal").ap()

    sb = lambda name, shape, dt=BF16: stack.enter_context(nc.sbuf_tensor(name, shape, dt))

    kvc = sb("kvc", [128, 2, 512])
    qc = sb("qc", [128, 2, 512])
    wqkv_sb = sb("wqkv_sb", [128, 8, 2, 192])
    KT = sb("KT", [128, S])
    vstage = sb("vstage", [128, S])
    vt = sb("vt", [128, 2, 32, 65])
    QT = sb("QT", [128, TOK])
    sq = sb("sq", [128, S])
    ptile = sb("ptile", [128, 512])
    h32c = sb("h32c", [128, 1], F32)
    m1t = sb("m1t", [128, 128])
    m2t = sb("m2t", [128, 128])
    nmax = sb("nmax", [1, 48], F32)
    bnd = sb("bnd", [1, 8], F32)
    bndbf = sb("bndbf", [1, 1])
    boundc = sb("boundc", [128, 2], F32)
    lr = sb("lr", [1, 2, 512], F32)
    r_bf = sb("r_bf", [1, 2, 512])
    rcol = sb("rcol", [128, 2, 4], F32)
    oevac = sb("oevac", [128, 512])
    osb = sb("osb", [128, 2, 4, 64])
    wout_c = sb("wout_c", [128, 2, D])
    wsum_sb = sb("wsum_sb", [1, D], F32)
    wsum_bf = sb("wsum_bf", [1, D])
    wsumT = sb("wsumT", [128, 8], F32)
    att = sb("att", [128, 512])
    resc = sb("resc", [128, NCHUNK, 512])

    cG, cY, cP, cT, cV, cA = env["ctr"]
    PB0, PB1 = env["PB"]
    PF = env["PF"]
    ALL = env["ALL"]
    ENG = env["ENG"]
    id128, id64d = env["id128"], env["id64d"]
    ones1, onescol_f, onescol_b = env["ones1"], env["onescol_f"], env["onescol_b"]
    ysb = env["ysb"] = sb("ysbC", [128, NCHUNK, 512], F32)
    ysq = env["ysq"]
    io2 = sb("io2", [128, 128], F32)

    cY.tick(Y.dma_start(out=h32c[:], in_=h32[:, :]))
    cP.tick(G.iota(io2[:], pattern=[[64, 4], [1, 32]], base=0, channel_multiplier=-1,
                   allow_small_or_imprecise_dtypes=True))
    cP.tick(G.memset(vt[:, :, :, 64], 1.0))
    cP.wait(V)
    cY.wait(V)
    cV.tick(V.tensor_scalar(m1t[:], io2[:], h32c[:, 0:1], 0.0, ALU.add, ALU.is_ge))
    cV.tick(V.tensor_scalar(m2t[:], io2[:], h32c[:, 0:1], 128.0, ALU.add, ALU.is_ge))
    for kc in range(8):
        buf = kc % 2
        fence([G], [cT])
        g_w = cG.tick(G.dma_start(out=wout_c[:, buf, :],
                                  in_=Wout[128 * kc:128 * (kc + 1), :], max_dma_last_dim=8192))
        T.wait_ge(cG.sem, g_w)
        for half in range(2):
            ins = T.matmul(PF[half][0:1, :], onescol_b[:, 0:1],
                           wout_c[:, buf, 512 * half:512 * (half + 1)],
                           start=(kc == 0), stop=(kc == 7))
        cT.tick(ins)
    cT.wait(V)
    cV.tick(V.tensor_copy(wsum_sb[0:1, 0:512], PF[0][0:1, :]))
    cV.tick(V.tensor_copy(wsum_sb[0:1, 512:1024], PF[1][0:1, :]))
    cV.wait(V)
    cV.tick(V.tensor_copy(wsum_bf[0:1, :], wsum_sb[0:1, :]))
    cV.wait(T)
    for kc in range(8):
        ins = T.transpose(PB0[0:128, 2 * kc:2 * kc + 1], wsum_bf[0:1, 128 * kc:128 * (kc + 1)],
                          id128[0:1, 0:1])
    t_ws = cT.tick(ins)
    A.wait_ge(cT.sem, t_ws)
    cA.tick(A.activation(wsumT[:], PB0[0:128, 0:16].rearrange("p (a b) -> p a b", b=2)[:, :, 0],
                         AF.Copy))
    fence(ENG, ALL)

    for p in range(NPAIR):
        fence([G], [cT])
        for kc in range(8):
            g_w = cG.tick(G.dma_start(
                out=wqkv_sb[:, kc, :, :],
                in_=Wqkv[2 * p:2 * p + 2, 128 * kc:128 * (kc + 1), :].rearrange("h p w -> p h w")))
        T.wait_ge(cG.sem, cG.n)
        for g in range(8):
            fence([T], [cA, cV])
            for kc in range(8):
                buf = kc % 2
                fence([Y], [cT])
                y_kv = cY.tick(Y.dma_start(out=kvc[:, buf, 0:512],
                                           in_=kvT[128 * kc:128 * (kc + 1), 512 * g:512 * (g + 1)]))
                T.wait_ge(cY.sem, y_kv)
                for h2 in range(2):
                    T.matmul(PF[0][64 * h2:64 * (h2 + 1), :], wqkv_sb[:, kc, h2, 64:128],
                             kvc[:, buf, 0:512], start=(kc == 0), stop=(kc == 7),
                             tile_position=(0, 64 * h2), skip_group_check=True)
                    ins = T.matmul(PF[1][64 * h2:64 * (h2 + 1), :], wqkv_sb[:, kc, h2, 128:192],
                                   kvc[:, buf, 0:512], start=(kc == 0), stop=(kc == 7),
                                   tile_position=(0, 64 * h2), skip_group_check=True)
                cT.tick(ins)
            cT.wait(A)
            cT.wait(V)
            cA.tick(A.activation(KT[:, 512 * g:512 * (g + 1)], PF[0][:], AF.Copy))
            cV.tick(V.tensor_copy(vstage[:, 512 * g:512 * (g + 1)], PF[1][:]))
        for g in range(4):
            fence([T], [cA])
            for kc in range(8):
                buf = kc % 2
                fence([Y], [cT])
                y_q = cY.tick(Y.dma_start(out=qc[:, buf, 0:512],
                                          in_=qinT[128 * kc:128 * (kc + 1), 512 * g:512 * (g + 1)]))
                T.wait_ge(cY.sem, y_q)
                for h2 in range(2):
                    ins = T.matmul(PF[0][64 * h2:64 * (h2 + 1), :], wqkv_sb[:, kc, h2, 0:64],
                                   qc[:, buf, 0:512], start=(kc == 0), stop=(kc == 7),
                                   tile_position=(0, 64 * h2), skip_group_check=True)
                cT.tick(ins)
            cT.wait(A)
            cA.tick(A.activation(QT[:, 512 * g:512 * (g + 1)], PF[0][:], AF.Copy))
        fence(ENG, [cT, cV, cA])
        for h2 in range(2):
            for c0 in range(0, 32, 4):
                pb = (PB0, PB1)[(c0 // 4) % 2]
                fence([T], [cV])
                for cc in range(4):
                    ins = T.transpose(pb[:, 64 * cc:64 * (cc + 1)],
                                      vstage[64 * h2:64 * (h2 + 1), 128 * (c0 + cc):128 * (c0 + cc + 1)],
                                      id64d[64 * h2:64 * (h2 + 1), 0:64])
                cT.tick(ins)
                cT.wait(V)
                cV.tick(V.tensor_copy(vt[:, h2, c0:c0 + 4, 0:64],
                                      pb[:, 0:256].rearrange("p (a b) -> p a b", b=64)))
        fence([A], [cT, cV])
        for src, width, ng, gbase in ((QT, TOK, 4, 0), (KT, S, 8, 16)):
            a_sq = cA.tick(A.activation(sq[:, 0:width], src[:, 0:width], AF.Square))
            T.wait_ge(cA.sem, a_sq)
            for h2 in range(2):
                for gi in range(ng):
                    cV.wait(T)
                    ins = T.matmul(PF[5][0:1, 0:512],
                                   onescol_b[64 * h2:64 * (h2 + 1), 0:1],
                                   sq[64 * h2:64 * (h2 + 1), 512 * gi:512 * (gi + 1)],
                                   start=True, stop=True)
                    t_mm = cT.tick(ins)
                    V.wait_ge(cT.sem, t_mm)
                    cV.tick(V.reduce_max(nmax[0:1, gbase + 8 * h2 + gi:gbase + 8 * h2 + gi + 1],
                                         PF[5][0:1, 0:512], axis=AX.X))
            fence([A], [cT])
        for h2 in range(2):
            cV.wait(V)
            cV.tick(V.reduce_max(bnd[0:1, 0:1], nmax[0:1, 8 * h2:8 * h2 + 4], axis=AX.X))
            cV.tick(V.reduce_max(bnd[0:1, 1:2], nmax[0:1, 16 + 8 * h2:16 + 8 * h2 + 8], axis=AX.X))
            cV.wait(V)
            cV.tick(V.tensor_mul(bnd[0:1, 2:3], bnd[0:1, 0:1], bnd[0:1, 1:2]))
            cV.wait(A)
            cA.tick(A.activation(bnd[0:1, 3:4], bnd[0:1, 2:3], AF.Sqrt))
            cA.wait(V)
            cV.tick(V.tensor_single_scalar(bnd[0:1, 4:5], bnd[0:1, 3:4], -0.125, ALU.mult))
            cV.wait(V)
            cV.tick(V.tensor_copy(bndbf[0:1, 0:1], bnd[0:1, 4:5]))
            cV.wait(T)
            cA.wait(T)
            t_mm = cT.tick(T.matmul(PF[5][:, 0:1], ones1[0:1, :], bndbf[0:1, 0:1],
                                    start=True, stop=True))
            A.wait_ge(cT.sem, t_mm)
            cA.tick(A.activation(boundc[:, h2:h2 + 1], PF[5][:, 0:1], AF.Copy))
        fence(ENG, [cT, cV, cA])
        for g4 in range(4):
            nchunks = 8 * g4 + 8
            for h2 in range(2):
                ltgt = PF[5] if h2 == 0 else PF[0]
                for c in range(nchunks):
                    j0 = c // 2
                    qoff = 128 * max(0, j0 - 4 * g4)
                    aw = 512 - qoff
                    spf = PF[2 + h2]
                    t_s = cT.tick(T.matmul(spf[:, 0:aw],
                                           KT[64 * h2:64 * (h2 + 1), 128 * c:128 * (c + 1)],
                                           QT[64 * h2:64 * (h2 + 1), 512 * g4 + qoff:512 * (g4 + 1)],
                                           start=True, stop=True))
                    A.wait_ge(cT.sem, t_s)
                    fence([A], [cT])
                    a_e = cA.tick(A.activation(ptile[:, 0:aw], spf[:, 0:aw], AF.Exp,
                                               bias=boundc[:, h2:h2 + 1], scale=0.125))
                    if c >= 8 * g4:
                        V.wait_ge(cA.sem, a_e)
                        cV.tick(V.tensor_mul(ptile[:, 0:128], ptile[:, 0:128],
                                             (m1t if c % 2 == 0 else m2t)[:]))
                        T.wait_ge(cV.sem, cV.n)
                    else:
                        T.wait_ge(cA.sem, a_e)
                    T.matmul(PF[4][64 * h2:64 * (h2 + 1), qoff:512], vt[:, h2, c, 0:64],
                             ptile[:, 0:aw], start=(c == 0), stop=(c == nchunks - 1),
                             tile_position=(0, 64 * h2), skip_group_check=True)
                    ins = T.matmul(ltgt[0:1, qoff:512], vt[:, h2, c, 64:65],
                                   ptile[:, 0:aw], start=(c == 0), stop=(c == nchunks - 1),
                                   skip_group_check=True)
                    cT.tick(ins)
            V.wait_ge(cT.sem, cT.n)
            cV.tick(V.tensor_copy(lr[0:1, 0, :], PF[5][0:1, 0:512]))
            cV.tick(V.tensor_copy(lr[0:1, 1, :], PF[0][0:1, 0:512]))
            cV.wait(A)
            cA.tick(A.activation(lr[0:1, :, :], lr[0:1, :, :], AF.Ln))
            cA.wait(A)
            a_r = cA.tick(A.activation(r_bf[0:1, :, :], lr[0:1, :, :], AF.Exp, scale=-1.0))
            A.wait_ge(cA.sem, a_r)
            a_o = cA.tick(A.activation(oevac[:], PF[4][:], AF.Copy))
            T.wait_ge(cA.sem, a_o)
            for h2 in range(2):
                fence([T], [cA])
                for t4 in range(4):
                    ins = T.transpose(PB0[0:128, 2 * t4:2 * t4 + 1],
                                      r_bf[0:1, h2, 128 * t4:128 * (t4 + 1)], id128[0:1, 0:1])
                cT.tick(ins)
                cT.wait(A)
                cA.tick(A.activation(rcol[:, h2, :],
                                     PB0[0:128, 0:8].rearrange("p (a b) -> p a b", b=2)[:, :, 0],
                                     AF.Copy))
            for rnd in range(2):
                h2 = rnd
                fence([T], [cV])
                for t4 in range(4):
                    ins = T.transpose(PB1[:, 64 * t4:64 * (t4 + 1)],
                                      oevac[64 * h2:64 * (h2 + 1), 128 * t4:128 * (t4 + 1)],
                                      id64d[64 * h2:64 * (h2 + 1), 0:64])
                cT.tick(ins)
                cT.wait(V)
                cA.wait(V)
                for t4 in range(4):
                    cV.tick(V.tensor_scalar(osb[:, h2, t4, :], PB1[:, 64 * t4:64 * (t4 + 1)],
                                            rcol[:, h2, t4:t4 + 1], None, ALU.mult))
            v_done = cV.n
            Y.wait_ge(cV.sem, v_done)
            for h2 in range(2):
                cY.tick(Y.dma_start(
                    out=o_dram[2 * p + h2, 512 * g4:512 * (g4 + 1), :].rearrange(
                        "(t4 r) e -> r t4 e", t4=4),
                    in_=osb[:, h2, :, :]))
            fence([T], [cA, cV, cY])
        fence(ENG, ALL)

    o_flat = o_dram.rearrange("h q e -> (h q e)")
    for g in range(4):
        off = 512 * g
        for dc in range(8):
            fence([Y], [cV])
            y_r = cY.tick(Y.dma_start(out=resc[:, dc, :],
                                      in_=resT[128 * dc:128 * (dc + 1), off:off + 512]))
            y_a = cY.tick(Y.dma_start(
                out=att[:, 0:512],
                in_=o_flat[dc * 262144:(dc + 1) * 262144].rearrange(
                    "(r c) -> r c", r=128)[:, off:off + 512]))
            V.wait_ge(cY.sem, y_a)
            cV.wait(V)
            cV.tick(V.scalar_tensor_tensor(ysb[:, dc, :], att[:, 0:512],
                                           wsumT[:, dc:dc + 1], resc[:, dc, :],
                                           ALU.mult, ALU.add))
        emit_ln_group(env, ysb, ysq, env["stream"], None, off, sb_off=off)
    fence(ENG, ALL)


def _stage_C(inputs, y2_shards):
    wqkv = inputs["L1_attn_W_qkv"]
    wout = np.ascontiguousarray(inputs["L1_attn_W_out"].reshape(D, D))
    maps = []
    y2f = []
    for b in range(4):
        y2f.append(np.concatenate([np.asarray(y2_shards[2 * b]),
                                   np.asarray(y2_shards[2 * b + 1])], axis=1))
    for c in range(8):
        b, h = c // 2, c % 2
        full = y2f[b]
        qcols = (np.arange(4096).reshape(64, 64)[:, 32 * h:32 * h + 32]).reshape(-1)
        maps.append({
            "kvT": full,
            "qinT": np.ascontiguousarray(full[:, qcols]),
            "resT": np.ascontiguousarray(full[:, 2048 * h:2048 * (h + 1)]),
            "h32": np.full((128, 1), 32.0 * h, np.float32),
            "Wqkv": wqkv,
            "Wout": wout,
        })
    return maps






def emit_D(nc, stack, env):
    G, T, V, A, Y = env["eng"]
    H1 = 4096
    rw1 = nc.dram_tensor("rw1", [D, D], F32, kind="ExternalInput").ap()
    rb1s = nc.dram_tensor("rb1s", [128, 8], F32, kind="ExternalInput").ap()
    rw2 = nc.dram_tensor("rw2", [D, 8], F32, kind="ExternalInput").ap()
    rb2 = nc.dram_tensor("rb2", [8, 1], F32, kind="ExternalInput").ap()
    we1 = nc.dram_tensor("we1", [8, D, H1], F32, kind="ExternalInput").ap()
    be1s = nc.dram_tensor("be1s", [8, 128, 32], F32, kind="ExternalInput").ap()
    we2 = nc.dram_tensor("we2", [8, H1, D], F32, kind="ExternalInput").ap()
    be2 = nc.dram_tensor("be2", [8, D], F32, kind="ExternalInput").ap()
    yout = nc.dram_tensor("yout", [D, TOK], F32, kind="ExternalOutput").ap()

    sb = lambda name, shape, dt=BF16: stack.enter_context(nc.sbuf_tensor(name, shape, dt))

    ysrc = env["stream"]
    big = sb("big", [128, 32768])
    t_v = big[:].rearrange("p (a b) -> p a b", b=TOK)
    hh_v = big[:].rearrange("p (a b) -> p a b", b=1024)
    rw1t = sb("rw1t", [128, 2, 8, 128])
    rw2_sb = sb("rw2_sb", [128, 8, 8])
    rb1c = sb("rb1c", [128, 8], F32)
    rb2c = sb("rb2c", [8, 1], F32)
    lg_sb = sb("lg_sb", [8, 512])
    lgt = sb("lgt", [128, 4, 8], F32)
    nm1 = sb("nm1", [128, 1], F32)
    nm2 = sb("nm2", [128, 1], F32)
    eq8 = sb("eq8", [128, 8], F32)
    l2 = sb("l2", [128, 8], F32)
    msk8 = sb("msk8", [128, 8], F32)
    e8 = sb("e8", [128, 8], F32)
    wr8 = sb("wr8", [128, 8], F32)
    ssum = sb("ssum", [128, 1], F32)
    rs = sb("rs", [128, 1], F32)
    wtok = sb("wtok", [128, 8])
    w8 = sb("w8", [8, TOK])
    wrows = sb("wrows", [1, 8, TOK])
    be1c = sb("be1c", [128, 32], F32)
    be2_sb = sb("be2_sb", [8, 8, 128])
    w1t = sb("w1t", [128, 2, 8, 128])
    w2t = sb("w2t", [128, 32, 128])
    bcm = sb("bcm", [128, 512])
    out32 = sb("out32", [128, 8, 1024], F32)
    y4sb = big[:, 16384:24576].bitcast(F32).rearrange("p (a b) -> p a b", b=512)

    cG, cY, cP, cT, cV, cA = env["ctr"]
    PB0, PB1 = env["PB"]
    PF = env["PF"]
    ALL = env["ALL"]
    ENG = env["ENG"]
    id128 = env["id128"]
    ones1, onescol_f = env["ones1"], env["onescol_f"]
    ysq = env["ysq"]
    MDL = dict(max_dma_last_dim=8192)

    cG.tick(G.dma_start(out=rw2_sb[:],
                        in_=rw2.rearrange("(kc p) e -> p kc e", p=128)))
    cG.tick(G.dma_start(out=rb1c[:], in_=rb1s[:, :]))
    cG.tick(G.dma_start(out=rb2c[:], in_=rb2[:, :]))
    cG.tick(G.dma_start(out=be2_sb[:],
                        in_=be2.rearrange("e (dc m) -> e dc m", m=128)))
    fence(ENG, ALL)

    for mc in range(8):
        buf = mc % 2
        fence([G], [cT])
        g_w = cG.tick(G.dma_start(
            out=rw1t[:, buf, :, :],
            in_=rw1.rearrange("(kc p) d -> p kc d", p=128)[:, :, 128 * mc:128 * (mc + 1)],
            **MDL))
        T.wait_ge(cG.sem, g_w)
        for g in range(4):
            pf = PF[(mc * 4 + g) % 2]
            fence([T], [cA])
            for kc in range(8):
                ins = T.matmul(pf[:], rw1t[:, buf, kc, :],
                               ysrc[:, kc, 512 * g:512 * (g + 1)],
                               start=(kc == 0), stop=(kc == 7))
            t_h = cT.tick(ins)
            A.wait_ge(cT.sem, t_h)
            cA.tick(A.activation(t_v[:, mc, 512 * g:512 * (g + 1)], pf[:],
                                 AF.Silu, bias=rb1c[:, mc:mc + 1]))
    fence(ENG, ALL)
    for g in range(4):
        off = 512 * g
        fence([T], [cA, cV])
        for kc in range(8):
            ins = T.matmul(PF[1][0:8, :], rw2_sb[:, kc, :], t_v[:, kc, off:off + 512],
                           start=(kc == 0), stop=(kc == 7))
        t_lg = cT.tick(ins)
        V.wait_ge(cT.sem, t_lg)
        cV.tick(V.tensor_scalar(lg_sb[:], PF[1][0:8, :], rb2c[:, 0:1], None, ALU.add))
        cV.wait(T)
        for t4 in range(4):
            ins = T.transpose(PB0[0:128, 8 * t4:8 * (t4 + 1)],
                              lg_sb[:, 128 * t4:128 * (t4 + 1)], id128[0:8, 0:8])
        t_tr = cT.tick(ins)
        V.wait_ge(cT.sem, t_tr)
        cV.tick(V.tensor_copy(lgt[:], PB0[0:128, 0:32].rearrange("p (a b) -> p a b", b=8)))
        for t4 in range(4):
            cV.wait(V)
            cV.tick(V.reduce_max(nm1[:], lgt[:, t4, :], axis=AX.X, negate=True))
            cV.wait(V)
            cV.tick(V.tensor_scalar(eq8[:], lgt[:, t4, :], nm1[:, 0:1], 0.0,
                                    ALU.add, ALU.is_equal))
            cV.wait(V)
            cV.tick(V.scalar_tensor_tensor(l2[:], eq8[:], -1e9, lgt[:, t4, :],
                                           ALU.mult, ALU.add))
            cV.wait(V)
            cV.tick(V.reduce_max(nm2[:], l2[:], axis=AX.X, negate=True))
            cV.wait(V)
            cV.tick(V.tensor_scalar(msk8[:], lgt[:, t4, :], nm2[:, 0:1], 0.0,
                                    ALU.add, ALU.is_ge))
            cV.wait(A)
            cA.tick(A.activation(e8[:], lgt[:, t4, :], AF.Exp, bias=nm1[:, 0:1]))
            cA.wait(V)
            cV.tick(V.tensor_mul(wr8[:], e8[:], msk8[:]))
            cV.wait(V)
            cV.tick(V.reduce_sum(ssum[:], wr8[:], axis=AX.X))
            cV.wait(V)
            cV.tick(V.reciprocal(rs[:], ssum[:]))
            cV.wait(V)
            cV.tick(V.tensor_scalar(wtok[:], wr8[:], rs[:, 0:1], None, ALU.mult))
            cV.wait(T)
            t_f = cT.tick(T.transpose(PB1[0:8, 128 * t4:128 * (t4 + 1)], wtok[:], id128[:]))
            for e in range(8):
                ins = T.transpose(PB0[0:1, 128 * e:128 * (e + 1)],
                                  wtok[:, e:e + 1], id128[:])
            t_r = cT.tick(ins)
            V.wait_ge(cT.sem, t_r)
            cV.tick(V.tensor_copy(wrows[0:1, :, off + 128 * t4:off + 128 * (t4 + 1)],
                                  PB0[0:1, 0:1024].rearrange("p (a b) -> p a b", b=128)))
        V.wait_ge(cT.sem, cT.n)
        cV.tick(V.tensor_copy(w8[:, off:off + 512], PB1[0:8, 0:512]))
        fence([T], [cV])
    fence(ENG, ALL)

    for gh in range(2):
        goff = 1024 * gh
        fence([T], [cV, cA])
        for dc in range(8):
            for g2 in range(2):
                pf = PF[(dc * 2 + g2) % 2]
                fence([T], [cV])
                t_b = cT.tick(T.matmul(pf[:], be2_sb[:, dc, :],
                                       w8[:, goff + 512 * g2:goff + 512 * (g2 + 1)],
                                       start=True, stop=True))
                V.wait_ge(cT.sem, t_b)
                cV.tick(V.tensor_copy(out32[:, dc, 512 * g2:512 * (g2 + 1)], pf[:]))
        fence(ENG, ALL)
        for e in range(8):
            g_b1 = cG.tick(G.dma_start(out=be1c[:], in_=be1s[e, :, :]))
            A.wait_ge(cG.sem, g_b1)
            for mc in range(32):
                buf = mc % 2
                fence([G], [cT])
                g_w = cG.tick(G.dma_start(
                    out=w1t[:, buf, :, :],
                    in_=we1[e].rearrange("(kc p) hh -> p kc hh", p=128)[:, :, 128 * mc:128 * (mc + 1)],
                    **MDL))
                T.wait_ge(cG.sem, g_w)
                for g2 in range(2):
                    pf = PF[2 + (mc * 2 + g2) % 2]
                    fence([T], [cA])
                    for kc in range(8):
                        ins = T.matmul(pf[:], w1t[:, buf, kc, :],
                                       ysrc[:, kc, goff + 512 * g2:goff + 512 * (g2 + 1)],
                                       start=(kc == 0), stop=(kc == 7))
                    t_h = cT.tick(ins)
                    A.wait_ge(cT.sem, t_h)
                    cA.tick(A.activation(hh_v[:, mc, 512 * g2:512 * (g2 + 1)], pf[:],
                                         AF.Silu, bias=be1c[:, mc:mc + 1]))
            for g2 in range(2):
                fence([T], [cA])
                t_bc = cT.tick(T.matmul(PF[4][:], ones1[0:1, :],
                                        wrows[0:1, e, goff + 512 * g2:goff + 512 * (g2 + 1)],
                                        start=True, stop=True))
                A.wait_ge(cT.sem, t_bc)
                cV.wait(A)
                a_bc = cA.tick(A.activation(bcm[:], PF[4][:], AF.Copy))
                V.wait_ge(cA.sem, a_bc)
                V.wait_ge(cA.sem, cA.n)
                for mc in range(32):
                    cV.tick(V.tensor_mul(hh_v[:, mc, 512 * g2:512 * (g2 + 1)],
                                         hh_v[:, mc, 512 * g2:512 * (g2 + 1)], bcm[:]))
            for dc in range(8):
                fence([G], [cT])
                fence([T], [cA])
                g_w = cG.tick(G.dma_start(
                    out=w2t[:, :, :],
                    in_=we2[e].rearrange("(kc p) d -> p kc d", p=128)[:, :, 128 * dc:128 * (dc + 1)],
                    **MDL))
                T.wait_ge(cG.sem, g_w)
                cV.wait(T)
                for g2 in range(2):
                    pf = PF[4 + (dc * 2 + g2) % 2]
                    fence([T], [cV])
                    for kc in range(32):
                        ins = T.matmul(pf[:], w2t[:, kc, :],
                                       hh_v[:, kc, 512 * g2:512 * (g2 + 1)],
                                       start=(kc == 0), stop=(kc == 31))
                    t_o = cT.tick(ins)
                    V.wait_ge(cT.sem, t_o)
                    cV.tick(V.tensor_tensor(out32[:, dc, 512 * g2:512 * (g2 + 1)],
                                            out32[:, dc, 512 * g2:512 * (g2 + 1)],
                                            pf[:], ALU.add))
            fence(ENG, ALL)
        for g2 in range(2):
            off = goff + 512 * g2
            for dc in range(8):
                cV.wait(V)
                cV.tick(V.scalar_tensor_tensor(out32[:, dc, 512 * g2:512 * (g2 + 1)],
                                               ysrc[:, dc, off:off + 512], 2.0,
                                               out32[:, dc, 512 * g2:512 * (g2 + 1)],
                                               ALU.mult, ALU.add))
            emit_ln_group(env, out32[:, :, 512 * g2:512 * (g2 + 1)], ysq, y4sb, yout, off)
        fence(ENG, ALL)


def _stage_D(inputs, y3_shards):
    rb1s = np.ascontiguousarray(inputs["L1_ffn_rb1"].reshape(8, 128).T)
    be1s = np.ascontiguousarray(
        inputs["L1_ffn_be1"].reshape(8, 32, 128).transpose(0, 2, 1))
    maps = []
    for c in range(8):
        maps.append({
            "y3": None if y3_shards[c] is None else np.asarray(y3_shards[c]),
            "rw1": inputs["L1_ffn_rw1"],
            "rb1s": rb1s,
            "rw2": inputs["L1_ffn_rw2"],
            "rb2": inputs["L1_ffn_rb2"].reshape(8, 1),
            "we1": inputs["L1_ffn_we1"],
            "be1s": be1s,
            "we2": inputs["L1_ffn_we2"],
            "be2": inputs["L1_ffn_be2"],
        })
    return maps






def _stage_A(inputs):
    x0 = inputs["L0_attn_x"]
    maps = []
    for c in range(8):
        b, h = c // 2, c % 2
        start = 2048 * h
        seg = np.zeros((TOKH, D), np.float32)
        lo = start - 128
        off = 128 if h == 0 else 0
        seg[off:] = x0[b, max(lo, 0):start + 2048]
        valid = np.zeros(TOKH, np.float32)
        valid[off:] = 1.0
        maps.append({
            "x0T": np.ascontiguousarray(seg.T),
            "validT": np.ascontiguousarray(valid.reshape(17, 128).T),
            "Wq": inputs["L0_attn_Wq"],
            "Wk": inputs["L0_attn_Wk"],
            "Wv": inputs["L0_attn_Wv"],
            "Wo": inputs["L0_attn_Wo"],
        })
    return maps





def build_launch_AB(nc):
    from contextlib import ExitStack
    stack = ExitStack()
    with stack:
        env = mk_env(nc, stack)
        with ExitStack() as sA:
            emit_A(nc, sA, env)
        with ExitStack() as sB:
            emit_B(nc, sB, env)
        cY = env["ctr"][1]
        env["eng"][4].wait_ge(cY.sem, cY.n)
    return nc


def build_launch_CD(nc):
    from contextlib import ExitStack
    stack = ExitStack()
    with stack:
        env = mk_env(nc, stack)
        with ExitStack() as sC:
            emit_C(nc, sC, env)
        with ExitStack() as sD:
            emit_D(nc, sD, env)
        cY = env["ctr"][1]
        env["eng"][4].wait_ge(cY.sem, cY.n)
    return nc


def run_launch_AB(inputs):
    nc = bass.Bass("TRN2", target_bir_lowering=False)
    build_launch_AB(nc)
    mapsA = _stage_A(inputs)
    mapsB = _stage_B(inputs, [None] * 8)
    maps = []
    for c in range(8):
        m = dict(mapsA[c])
        mB = dict(mapsB[c]); mB.pop("y1")
        m.update(mB)
        maps.append(m)
    res = run_bass_kernel_spmd(nc, maps, core_ids=list(range(8)))
    return [r["y2"] for r in res.results]


def run_launch_CD(inputs, y2_shards):
    nc = bass.Bass("TRN2", target_bir_lowering=False)
    build_launch_CD(nc)
    mapsC = _stage_C(inputs, y2_shards)
    mapsD = _stage_D(inputs, [None] * 8)
    maps = []
    for c in range(8):
        m = dict(mapsC[c])
        mD = dict(mapsD[c]); mD.pop("y3")
        m.update(mD)
        maps.append(m)
    res = run_bass_kernel_spmd(nc, maps, core_ids=list(range(8)))
    return [r["yout"] for r in res.results]


def kernel(**inputs):
    y2 = run_launch_AB(inputs)
    y4 = run_launch_CD(inputs, y2)
    out = np.empty((4, 4096, 1024), np.float32)
    for c in range(8):
        out[c // 2, 2048 * (c % 2):2048 * (c % 2 + 1), :] = np.asarray(y4[c]).T
    return out


if __name__ == "__main__":
    pass


# version 04 (final)
# speedup vs baseline: 1.6523x; 1.6494x over previous
import numpy as np
import ml_dtypes
import concourse.bass as bass
import concourse.mybir as mybir
from concourse.bass_utils import run_bass_kernel_spmd

F32 = mybir.dt.float32
BF16 = mybir.dt.bfloat16
AF = mybir.ActivationFunctionType
ALU = mybir.AluOpType
AX = mybir.AxisListType
BF = ml_dtypes.bfloat16

D = 1024
NCHUNK = 8
TOK = 2048
TOKH = TOK + 128
NH = 16
HD = 64
NPAIR = 8
WIN = 64


class Ctr:
    def __init__(self, nc, stack, name, step=1):
        self.sem = stack.enter_context(nc.semaphore(name))
        self.n = 0
        self.step = step

    def tick(self, ins, k=None):
        k = self.step if k is None else k
        ins.then_inc(self.sem, k)
        self.n += k
        return self.n

    def wait(self, eng, n=None):
        eng.wait_ge(self.sem, self.n if n is None else n)


def fence(engines, ctrs):
    for e in engines:
        for c in ctrs:
            if c.n > 0:
                c.wait(e)



def emit_ln_group(env, ysb, ysq, outsb, out_dram, off, out_w=512, sb_off=None):
    (G, T, V, A, Y) = env["eng"]
    (cG, cY, cP, cT, cV, cA) = env["ctr"]
    PF = env["PF"]
    onescol_f, ones1 = env["onescol_f"], env["ones1"]
    st_sb, mu, m2, var, rstd, nmur = env["st_sb"], env["mu"], env["m2"], env["var"], env["rstd"], env["nmur"]
    rstd_bf, nmur_bf, bcln = env["rstd_bf"], env["nmur_bf"], env["bcln"]
    AFL, ALUL, w = AF, ALU, out_w
    fence([T, A], [cV])
    for dc in range(8):
        ins = T.matmul(PF[2][0:1, 0:w], onescol_f[:, 0:1], ysb[:, dc, 0:w],
                       start=(dc == 0), stop=(dc == 7))
    cT.tick(ins)
    for dc in range(8):
        a_sq = cA.tick(A.activation(ysq[:, 0:w], ysb[:, dc, 0:w], AFL.Square))
        T.wait_ge(cA.sem, a_sq)
        ins = T.matmul(PF[3][0:1, 0:w], onescol_f[:, 0:1], ysq[:, 0:w],
                       start=(dc == 0), stop=(dc == 7))
        t_s2 = cT.tick(ins)
        if dc < 7:
            A.wait_ge(cT.sem, t_s2)
    V.wait_ge(cT.sem, cT.n)
    cV.tick(V.tensor_copy(st_sb[0:1, 0:w], PF[2][0:1, 0:w]))
    cV.tick(V.tensor_copy(st_sb[0:1, 512:512 + w], PF[3][0:1, 0:w]))
    cV.wait(V)
    cV.tick(V.tensor_single_scalar(mu[0:1, 0:w], st_sb[0:1, 0:w], 1.0 / 1024, ALUL.mult))
    cV.tick(V.tensor_single_scalar(m2[0:1, 0:w], st_sb[0:1, 512:512 + w], 1.0 / 1024, ALUL.mult))
    cV.wait(V)
    cV.tick(V.tensor_mul(var[0:1, 0:w], mu[0:1, 0:w], mu[0:1, 0:w]))
    cV.wait(V)
    cV.tick(V.tensor_sub(var[0:1, 0:w], m2[0:1, 0:w], var[0:1, 0:w]))
    cV.wait(V)
    cV.tick(V.tensor_single_scalar(var[0:1, 0:w], var[0:1, 0:w], 1e-5, ALUL.add))
    cV.wait(A)
    cA.tick(A.activation(var[0:1, 0:w], var[0:1, 0:w], AFL.Ln))
    cA.wait(A)
    a_r = cA.tick(A.activation(rstd[0:1, 0:w], var[0:1, 0:w], AFL.Exp, scale=-0.5))
    V.wait_ge(cA.sem, a_r)
    cV.tick(V.tensor_mul(nmur[0:1, 0:w], mu[0:1, 0:w], rstd[0:1, 0:w]))
    cV.wait(V)
    cV.tick(V.tensor_single_scalar(nmur[0:1, 0:w], nmur[0:1, 0:w], -1.0, ALUL.mult))
    cV.tick(V.tensor_copy(rstd_bf[0:1, 0:w], rstd[0:1, 0:w]))
    cV.wait(V)
    cV.tick(V.tensor_copy(nmur_bf[0:1, 0:w], nmur[0:1, 0:w]))
    cV.wait(T)
    cT.tick(T.matmul(PF[4][:, 0:w], ones1[0:1, :], rstd_bf[0:1, 0:w], start=True, stop=True))
    t_b2 = cT.tick(T.matmul(PF[5][:, 0:w], ones1[0:1, :], nmur_bf[0:1, 0:w], start=True, stop=True))
    A.wait_ge(cT.sem, t_b2)
    cA.tick(A.activation(bcln[:, 0, 0:w], PF[4][:, 0:w], AFL.Copy))
    a_bc = cA.tick(A.activation(bcln[:, 1, 0:w], PF[5][:, 0:w], AFL.Copy))
    V.wait_ge(cA.sem, a_bc)
    so = 0 if sb_off is None else sb_off
    for dc in range(8):
        cV.tick(V.tensor_mul(ysq[:, 0:w], ysb[:, dc, 0:w], bcln[:, 0, 0:w]))
        V.wait_ge(cV.sem, cV.n)
        v_ap = cV.tick(V.tensor_add(outsb[:, dc, so:so + w], ysq[:, 0:w], bcln[:, 1, 0:w]))
        if out_dram is not None:
            Y.wait_ge(cV.sem, v_ap)
            cY.tick(Y.dma_start(out=out_dram[128 * dc:128 * (dc + 1), off:off + w],
                                in_=outsb[:, dc, so:so + w]))
    fence([T, A, V], [cV, cY])



def mk_env(nc, stack):
    G, T, V, A, Y = nc.gpsimd, nc.tensor, nc.vector, nc.scalar, nc.sync
    sb = lambda name, shape, dt=BF16: stack.enter_context(nc.sbuf_tensor(name, shape, dt))
    ps = lambda name, shape, dt=F32: stack.enter_context(nc.psum_tensor(name, shape, dt))
    env = {}
    env["nc"] = nc
    env["eng"] = (G, T, V, A, Y)
    PB0 = ps("PB0", [128, 1024], BF16)
    PB1 = ps("PB1", [128, 1024], BF16)
    PF = [ps(f"PF{i}", [128, 512], F32) for i in range(6)]
    env["PB"] = (PB0, PB1)
    env["PF"] = PF
    cG = Ctr(nc, stack, "cG", 16)
    cY = Ctr(nc, stack, "cY", 16)
    cP = Ctr(nc, stack, "cP")
    cT = Ctr(nc, stack, "cT")
    cV = Ctr(nc, stack, "cV")
    cA = Ctr(nc, stack, "cA")
    env["ctr"] = (cG, cY, cP, cT, cV, cA)
    env["ALL"] = [cG, cY, cP, cT, cV, cA]
    env["ENG"] = [G, T, V, A, Y]
    for n in ("icol", "iot"):
        env[n] = sb(n, [128, 128], F32)
    env["irow"] = sb("irow", [128, 1], F32)
    env["irow64"] = sb("irow64", [128, 1], F32)
    env["id128"] = sb("id128", [128, 128])
    env["id64d"] = sb("id64d", [128, 64])
    env["ones1"] = sb("ones1", [1, 128])
    env["onescol_f"] = sb("onescol_f", [128, 1], F32)
    env["onescol_b"] = sb("onescol_b", [128, 1])
    env["ysq"] = sb("ysq", [128, 512], F32)
    env["st_sb"] = sb("st_sb", [1, 1024], F32)
    env["mu"] = sb("mu", [1, 512], F32)
    env["m2"] = sb("m2", [1, 512], F32)
    env["var"] = sb("var", [1, 512], F32)
    env["rstd"] = sb("rstd", [1, 512], F32)
    env["nmur"] = sb("nmur", [1, 512], F32)
    env["rstd_bf"] = sb("rstd_bf", [1, 512])
    env["nmur_bf"] = sb("nmur_bf", [1, 512])
    env["bcln"] = sb("bcln", [128, 2, 512])
    env["stream"] = sb("stream", [128, NCHUNK, TOK])
    G, T, V, A, Y = env["eng"]
    cG, cY, cP, cT, cV, cA = env["ctr"]
    cP.tick(G.iota(env["iot"][:], pattern=[[-1, 128]], base=0, channel_multiplier=1,
                   allow_small_or_imprecise_dtypes=True))
    cP.tick(G.iota(env["icol"][:], pattern=[[1, 128]], base=0, channel_multiplier=0,
                   allow_small_or_imprecise_dtypes=True))
    cP.tick(G.iota(env["irow"][:], pattern=[[0, 1]], base=0, channel_multiplier=1,
                   allow_small_or_imprecise_dtypes=True))
    cP.tick(G.memset(env["ones1"][:], 1.0))
    cP.tick(G.memset(env["onescol_f"][:], 1.0))
    cP.tick(G.memset(env["onescol_b"][:], 1.0))
    cP.wait(V)
    cV.tick(V.tensor_scalar(env["id128"][:], env["icol"][:], env["irow"][:, 0:1], None, ALU.is_equal))
    cV.tick(V.tensor_single_scalar(env["irow64"][:], env["irow"][:], 64.0, ALU.is_ge))
    cV.wait(V)
    cV.tick(V.scalar_tensor_tensor(env["irow64"][:], env["irow64"][:], -64.0, env["irow"][:], ALU.mult, ALU.add))
    cV.wait(V)
    cV.tick(V.tensor_scalar(env["id64d"][:], env["icol"][:, 0:64], env["irow64"][:, 0:1], None, ALU.is_equal))
    fence(env["ENG"], env["ALL"])
    return env



def emit_A(nc, stack, env):
    G, T, V, A, Y = env["eng"]
    x0T = nc.dram_tensor("x0T", [D, TOKH], F32, kind="ExternalInput").ap()
    validT = nc.dram_tensor("validT", [128, 17], F32, kind="ExternalInput").ap()
    Wq = nc.dram_tensor("Wq", [D, D], F32, kind="ExternalInput").ap()
    Wk = nc.dram_tensor("Wk", [D, D], F32, kind="ExternalInput").ap()
    Wv = nc.dram_tensor("Wv", [D, D], F32, kind="ExternalInput").ap()
    Wo = nc.dram_tensor("Wo", [D, D], F32, kind="ExternalInput").ap()

    sb = lambda name, shape, dt=BF16: stack.enter_context(nc.sbuf_tensor(name, shape, dt))

    x0b = sb("x0b", [128, NCHUNK, TOKH])
    vld = sb("vld", [128, 17], F32)
    wraw = sb("wraw", [128, D])
    wT = {w: sb(f"w{w}T", [128, 8, 128]) for w in "qkv"}
    qT = sb("qT", [128, TOK])
    kT = sb("kT", [128, TOKH])
    vT = sb("vT", [128, TOKH])
    vt = sb("vt", [128, 2, 17, 65])
    sq = sb("sq", [128, TOKH])
    pt = sb("pt", [128, 2, 256])
    tmpm = sb("tmpm", [128, 2, 128], F32)
    maskAB = sb("maskAB", [128, 256])
    nmax = sb("nmax", [1, 32], F32)
    bnd = sb("bnd", [1, 8], F32)
    bndbf = sb("bndbf", [1, 1])
    boundc = sb("boundc", [128, 2], F32)
    lr = sb("lr", [1, 256], F32)
    rbf = sb("rbf", [1, 256])
    bc_sb = sb("bc_sb", [128, 256], F32)
    attnT = sb("attnT", [128, NPAIR, TOK])
    woT = sb("woT", [128, 8, 8, 128])

    cG, cY, cP, cT, cV, cA = env["ctr"]
    PB0, PB1 = env["PB"]
    PF = env["PF"]
    ALL = env["ALL"]
    ENG = env["ENG"]
    id128, id64d = env["id128"], env["id64d"]
    ones1, onescol_f, onescol_b = env["ones1"], env["onescol_f"], env["onescol_b"]
    iot = env["iot"]
    ysb = env["ysb"] = sb("ysbA", [128, NCHUNK, 512], F32)
    ysq = env["ysq"]
    stream = env["stream"]

    MDL = dict(max_dma_last_dim=8192)

    for c in range(NCHUNK):
        cG.tick(G.dma_start(out=x0b[:, c, :], in_=x0T[128 * c:128 * (c + 1), :], **MDL))
    cG.tick(G.dma_start(out=vld[:], in_=validT[:, :]))
    cP.wait(V)
    cV.tick(V.tensor_single_scalar(tmpm[:, 0, :], iot[:], 65.0, ALU.is_ge))
    cV.tick(V.tensor_single_scalar(tmpm[:, 1, :], iot[:], 128.0, ALU.is_le))
    cV.wait(V)
    cV.tick(V.tensor_mul(maskAB[:, 0:128], tmpm[:, 0, :], tmpm[:, 1, :]))
    cV.tick(V.tensor_single_scalar(tmpm[:, 0, :], iot[:], -63.0, ALU.is_ge))
    cV.tick(V.tensor_single_scalar(tmpm[:, 1, :], iot[:], 0.0, ALU.is_le))
    cV.wait(V)
    cV.tick(V.tensor_mul(maskAB[:, 128:256], tmpm[:, 0, :], tmpm[:, 1, :]))
    fence(ENG, ALL)

    kv_groups = [(0, 512), (512, 512), (1024, 512), (1536, 512), (2048, 128)]
    q_groups = [(0, 512), (512, 512), (1024, 512), (1536, 512)]

    for p in range(NPAIR):
        for wname, wdram in (("q", Wq), ("k", Wk), ("v", Wv)):
            fence([G], [cT])
            g_ld = cG.tick(G.dma_start(out=wraw[:], in_=wdram[128 * p:128 * (p + 1), :], **MDL))
            T.wait_ge(cG.sem, g_ld)
            for half in range(2):
                pb = (PB0, PB1)[half]
                fence([T], [cV])
                for k4 in range(4):
                    k = half * 4 + k4
                    ins = T.transpose(pb[:, 128 * k4:128 * (k4 + 1)],
                                      wraw[:, 128 * k:128 * (k + 1)], id128[:])
                cT.tick(ins)
                cT.wait(V)
                cV.tick(V.tensor_copy(wT[wname][:, 4 * half:4 * half + 4, :],
                                      pb[:, 0:512].rearrange("p (a b) -> p a b", b=128)))
        fence([T], [cV, cG])
        for wname, dst, groups, coloff in (("q", qT, q_groups, 128),
                                           ("k", kT, kv_groups, 0),
                                           ("v", vT, kv_groups, 0)):
            for gi, (off, gw) in enumerate(groups):
                pf = PF[gi % 2]
                fence([T], [cA])
                for k in range(8):
                    ins = T.matmul(pf[:, 0:gw], wT[wname][:, k, :],
                                   x0b[:, k, coloff + off:coloff + off + gw],
                                   start=(k == 0), stop=(k == 7))
                cT.tick(ins)
                cT.wait(A)
                cA.tick(A.activation(dst[:, off:off + gw], pf[:, 0:gw], AF.Copy))
        fence([T, V], [cA])
        for h2 in range(2):
            for c0 in range(0, 17, 4):
                cnt = min(4, 17 - c0)
                pb = (PB0, PB1)[(c0 // 4) % 2]
                fence([T], [cV])
                for cc in range(cnt):
                    ins = T.transpose(pb[:, 64 * cc:64 * (cc + 1)],
                                      vT[64 * h2:64 * (h2 + 1), 128 * (c0 + cc):128 * (c0 + cc + 1)],
                                      id64d[64 * h2:64 * (h2 + 1), 0:64])
                cT.tick(ins)
                cT.wait(V)
                cV.tick(V.tensor_copy(vt[:, h2, c0:c0 + cnt, 0:64],
                                      pb[:, 0:64 * cnt].rearrange("p (a b) -> p a b", b=64)))
            cV.tick(V.tensor_copy(vt[:, h2, :, 64], vld[:, 0:17]))
        fence([A], [cT, cV])
        for src, groups, gbase in ((qT, q_groups, 0), (kT, kv_groups, 16)):
            a_sq = cA.tick(A.activation(sq[:, 0:src.shape[-1]], src[:], AF.Square))
            T.wait_ge(cA.sem, a_sq)
            for h2 in range(2):
                for gi, (off, gw) in enumerate(groups):
                    cV.wait(T)
                    ins = T.matmul(PF[5][0:1, 0:gw],
                                   onescol_b[64 * h2:64 * (h2 + 1), 0:1],
                                   sq[64 * h2:64 * (h2 + 1), off:off + gw],
                                   start=True, stop=True)
                    t_mm = cT.tick(ins)
                    V.wait_ge(cT.sem, t_mm)
                    cV.tick(V.reduce_max(nmax[0:1, gbase + 8 * h2 + gi:gbase + 8 * h2 + gi + 1],
                                         PF[5][0:1, 0:gw], axis=AX.X))
            fence([A], [cT])
        for h2 in range(2):
            cV.wait(V)
            cV.tick(V.reduce_max(bnd[0:1, 0:1], nmax[0:1, 8 * h2:8 * h2 + 4], axis=AX.X))
            cV.tick(V.reduce_max(bnd[0:1, 1:2], nmax[0:1, 16 + 8 * h2:16 + 8 * h2 + 5], axis=AX.X))
            cV.wait(V)
            cV.tick(V.tensor_mul(bnd[0:1, 2:3], bnd[0:1, 0:1], bnd[0:1, 1:2]))
            cV.wait(A)
            cA.tick(A.activation(bnd[0:1, 3:4], bnd[0:1, 2:3], AF.Sqrt))
            cA.wait(V)
            cV.tick(V.tensor_single_scalar(bnd[0:1, 4:5], bnd[0:1, 3:4], -0.125, ALU.mult))
            cV.wait(V)
            cV.tick(V.tensor_copy(bndbf[0:1, 0:1], bnd[0:1, 4:5]))
            cV.wait(T)
            cA.wait(T)
            t_mm = cT.tick(T.matmul(PF[5][:, 0:1], ones1[0:1, :], bndbf[0:1, 0:1],
                                    start=True, stop=True))
            A.wait_ge(cT.sem, t_mm)
            cA.tick(A.activation(boundc[:, h2:h2 + 1], PF[5][:, 0:1], AF.Copy))
        fence(ENG, [cT, cV, cA])
        for i in range(16):
            for h2 in range(2):
                spf = PF[2 + h2]
                for ci, cc in enumerate((i, i + 1)):
                    ins = T.matmul(spf[:, 128 * ci:128 * (ci + 1)],
                                   kT[64 * h2:64 * (h2 + 1), 128 * cc:128 * (cc + 1)],
                                   qT[64 * h2:64 * (h2 + 1), 128 * i:128 * (i + 1)],
                                   start=True, stop=True)
                t_s = cT.tick(ins)
                A.wait_ge(cT.sem, t_s)
                cA.tick(A.activation(pt[:, h2, :], spf[:, 0:256], AF.Exp,
                                     bias=boundc[:, h2:h2 + 1], scale=0.125))
            cA.wait(V)
            for h2 in range(2):
                cV.tick(V.tensor_mul(pt[:, h2, :], pt[:, h2, :], maskAB[:]))
            cV.wait(T)
            for h2 in range(2):
                T.matmul(PF[4][64 * h2:64 * (h2 + 1), 0:128], vt[:, h2, i, 0:64],
                         pt[:, h2, 0:128], start=True, stop=False,
                         tile_position=(0, 64 * h2))
                ins = T.matmul(PF[4][64 * h2:64 * (h2 + 1), 0:128], vt[:, h2, i + 1, 0:64],
                               pt[:, h2, 128:256], start=False, stop=True,
                               tile_position=(0, 64 * h2))
                cT.tick(ins)
                T.matmul(PF[5][0:1, 128 * h2:128 * (h2 + 1)], vt[:, h2, i, 64:65],
                         pt[:, h2, 0:128], start=True, stop=False)
                ins = T.matmul(PF[5][0:1, 128 * h2:128 * (h2 + 1)], vt[:, h2, i + 1, 64:65],
                               pt[:, h2, 128:256], start=False, stop=True)
                t_pv = cT.tick(ins)
            V.wait_ge(cT.sem, t_pv)
            cV.tick(V.tensor_copy(lr[0:1, 0:256], PF[5][0:1, 0:256]))
            cV.wait(A)
            cA.tick(A.activation(lr[0:1, :], lr[0:1, :], AF.Ln))
            cA.wait(A)
            a_r = cA.tick(A.activation(rbf[0:1, :], lr[0:1, :], AF.Exp, scale=-1.0))
            T.wait_ge(cA.sem, a_r)
            for h2 in range(2):
                ins = T.matmul(PF[0][:, 128 * h2:128 * (h2 + 1)], ones1[0:1, :],
                               rbf[0:1, 128 * h2:128 * (h2 + 1)], start=True, stop=True)
            t_bc = cT.tick(ins)
            A.wait_ge(cT.sem, t_bc)
            a_bc = cA.tick(A.activation(bc_sb[:, 0:256], PF[0][:, 0:256], AF.Copy))
            V.wait_ge(cA.sem, a_bc)
            for h2 in range(2):
                cV.tick(V.tensor_mul(attnT[64 * h2:64 * (h2 + 1), p, 128 * i:128 * (i + 1)],
                                     PF[4][64 * h2:64 * (h2 + 1), 0:128],
                                     bc_sb[64 * h2:64 * (h2 + 1), 128 * h2:128 * (h2 + 1)]))
            fence([T], [cA, cV])
        fence(ENG, ALL)

    for dc in range(8):
        fence([G], [cT])
        g_ld = cG.tick(G.dma_start(out=wraw[:], in_=Wo[128 * dc:128 * (dc + 1), :], **MDL))
        T.wait_ge(cG.sem, g_ld)
        for half in range(2):
            pb = (PB0, PB1)[half]
            fence([T], [cV])
            for k4 in range(4):
                k = half * 4 + k4
                ins = T.transpose(pb[:, 128 * k4:128 * (k4 + 1)],
                                  wraw[:, 128 * k:128 * (k + 1)], id128[:])
            cT.tick(ins)
            cT.wait(V)
            cV.tick(V.tensor_copy(woT[:, dc, 4 * half:4 * half + 4, :],
                                  pb[:, 0:512].rearrange("p (a b) -> p a b", b=128)))
    fence(ENG, ALL)

    for g in range(4):
        off = 512 * g
        for dc in range(8):
            pf = PF[dc % 2]
            fence([T], [cV])
            for k in range(8):
                ins = T.matmul(pf[:], woT[:, dc, k, :], attnT[:, k, off:off + 512],
                               start=(k == 0), stop=(k == 7))
            t_mm = cT.tick(ins)
            V.wait_ge(cT.sem, t_mm)
            cV.tick(V.tensor_add(ysb[:, dc, :], pf[:], x0b[:, dc, 128 + off:128 + off + 512]))
        emit_ln_group(env, ysb, ysq, stream, None, off, sb_off=off)
    fence(ENG, ALL)



def emit_moe_experts(nc, env, P, res_scale, ln_out_sb, ln_dram, ln_sb_off):
    G, T, V, A, Y = env["eng"]
    cG, cY, cP, cT, cV, cA = env["ctr"]
    PF = env["PF"]
    ALL, ENG = env["ALL"], env["ENG"]
    ones1 = env["ones1"]
    ysq = env["ysq"]
    MDL = dict(max_dma_last_dim=8192)
    W1e, W2e, b1s = P["W1e"], P["W2e"], P["b1s"]
    b2_sb, m8, mrows = P["b2_sb"], P["m8"], P["mrows"]
    w1t, w2t, hhb, bcm4, b1c = P["w1t"], P["w2t"], P["hhb"], P["bcm4"], P["b1c"]
    out32, ysrc, act, HC = P["out32"], P["ysrc"], P["act"], P["HC"]
    NBLK = HC // 8

    fence(ENG, ALL)
    bank_free = {}
    for g in range(4):
        for dc in range(8):
            bi = (g * 8 + dc) % 2
            pf = PF[bi]
            if bi in bank_free:
                cV.wait(T, bank_free[bi][1]) if False else T.wait_ge(cV.sem, bank_free[bi])
            t_b = cT.tick(T.matmul(pf[:], b2_sb[:, dc, :],
                                   m8[:, 512 * g:512 * (g + 1)], start=True, stop=True))
            V.wait_ge(cT.sem, t_b)
            bank_free[bi] = cV.tick(V.tensor_copy(out32[:, dc, 512 * g:512 * (g + 1)], pf[:]))
    fence(ENG, ALL)

    w1_free = {}
    w2_free = {}
    hbank_free = {}
    obank_free = {}
    bc_free = 0
    b1_free = 0
    hh_free = 0
    mask_done = 0
    for e in range(8):
        if b1_free:
            A_done = b1_free
            G.wait_ge(cA.sem, A_done)
        g_b1 = cG.tick(G.dma_start(out=b1c[:], in_=b1s[e, :, :]))
        A.wait_ge(cG.sem, g_b1)
        bcbank = {}
        for g in range(4):
            bi = g % 2
            pf = PF[bi]
            if bi in bcbank:
                T.wait_ge(cA.sem, bcbank[bi])
            elif bc_free:
                T.wait_ge(cA.sem, bc_free)
            t_bc = cT.tick(T.matmul(pf[:], ones1[0:1, :],
                                    mrows[0:1, e, 512 * g:512 * (g + 1)],
                                    start=True, stop=True))
            A.wait_ge(cT.sem, t_bc)
            if e > 0:
                A.wait_ge(cV.sem, mask_done)
            bcbank[bi] = cA.tick(A.activation(bcm4[:, g, :], pf[:], AF.Copy))
        bc_free = cA.n
        for blk in range(NBLK):
            for mcl in range(8):
                mc = blk * 8 + mcl
                buf = mc % 2
                if buf in w1_free:
                    G.wait_ge(cT.sem, w1_free[buf])
                g_w = cG.tick(G.dma_start(
                    out=w1t[:, buf, :, :],
                    in_=W1e[e][:, 128 * mc:128 * (mc + 1)].rearrange("(kc p) m -> p kc m", p=128),
                    **MDL))
                T.wait_ge(cG.sem, g_w)
                if e > 0 or blk > 0:
                    A.wait_ge(cT.sem, hh_free)
                for g in range(4):
                    bi = 2 + (mc * 4 + g) % 2
                    pf = PF[bi]
                    if bi in hbank_free:
                        T.wait_ge(cA.sem, hbank_free[bi])
                    for kc in range(8):
                        ins = T.matmul(pf[:], w1t[:, buf, kc, :],
                                       ysrc[:, kc, 512 * g:512 * (g + 1)],
                                       start=(kc == 0), stop=(kc == 7))
                    t_h = cT.tick(ins)
                    A.wait_ge(cT.sem, t_h)
                    hbank_free[bi] = cA.tick(A.activation(
                        hhb[:, mcl, 512 * g:512 * (g + 1)], pf[:], act,
                        bias=b1c[:, mc:mc + 1]))
                w1_free[buf] = cT.n
            V.wait_ge(cA.sem, cA.n)
            for mcl in range(8):
                for g in range(4):
                    cV.tick(V.tensor_mul(hhb[:, mcl, 512 * g:512 * (g + 1)],
                                         hhb[:, mcl, 512 * g:512 * (g + 1)],
                                         bcm4[:, g, :]))
            mask_done = cV.n
            T.wait_ge(cV.sem, mask_done)
            for dc in range(8):
                buf = dc % 2
                if buf in w2_free:
                    G.wait_ge(cT.sem, w2_free[buf])
                g_w2 = cG.tick(G.dma_start(
                    out=w2t[:, buf, :, :],
                    in_=W2e[e][1024 * blk:1024 * (blk + 1), 128 * dc:128 * (dc + 1)].rearrange(
                        "(kc p) m -> p kc m", p=128),
                    **MDL))
                T.wait_ge(cG.sem, g_w2)
                for g in range(4):
                    bi = 4 + (dc * 4 + g) % 2
                    pf = PF[bi]
                    if bi in obank_free:
                        T.wait_ge(cV.sem, obank_free[bi])
                    for kc in range(8):
                        ins = T.matmul(pf[:], w2t[:, buf, kc, :],
                                       hhb[:, kc, 512 * g:512 * (g + 1)],
                                       start=(kc == 0), stop=(kc == 7))
                    t_o = cT.tick(ins)
                    V.wait_ge(cT.sem, t_o)
                    cV.wait(V)
                    obank_free[bi] = cV.tick(V.tensor_tensor(
                        out32[:, dc, 512 * g:512 * (g + 1)],
                        out32[:, dc, 512 * g:512 * (g + 1)], pf[:], ALU.add))
                w2_free[buf] = cT.n
            hh_free = cT.n
        b1_free = cA.n
    fence(ENG, ALL)
    for g in range(4):
        off = 512 * g
        for dc in range(8):
            cV.wait(V)
            cV.tick(V.scalar_tensor_tensor(out32[:, dc, off:off + 512],
                                           ysrc[:, dc, off:off + 512], float(res_scale),
                                           out32[:, dc, off:off + 512],
                                           ALU.mult, ALU.add))
        emit_ln_group(env, out32[:, :, off:off + 512], ysq, ln_out_sb, ln_dram, off,
                      sb_off=(off if ln_sb_off else None))
    fence(ENG, ALL)



def emit_B(nc, stack, env, debug=False):
    G, T, V, A, Y = env["eng"]
    w_red = nc.dram_tensor("w_red", [D, 16], F32, kind="ExternalInput").ap()
    wg = nc.dram_tensor("wg", [8, 16], F32, kind="ExternalInput").ap()
    W1 = nc.dram_tensor("W1", [8, D, 2048], F32, kind="ExternalInput").ap()
    b1s = nc.dram_tensor("b1s", [8, 128, 16], F32, kind="ExternalInput").ap()
    W2 = nc.dram_tensor("W2", [8, 2048, D], F32, kind="ExternalInput").ap()
    b2 = nc.dram_tensor("b2", [8, D], F32, kind="ExternalInput").ap()
    y2 = nc.dram_tensor("y2", [D, TOK], BF16, kind="ExternalOutput").ap()
    if debug:
        dbg_m8 = nc.dram_tensor("dbg_m8", [8, TOK], F32, kind="ExternalOutput").ap()
        dbg_mrows = nc.dram_tensor("dbg_mrows", [8, TOK], F32, kind="ExternalOutput").ap()
        dbg_out32 = nc.dram_tensor("dbg_out32", [D, 1024], F32, kind="ExternalOutput").ap()
        dbg_lgt = nc.dram_tensor("dbg_lgt", [128, 4, 8], F32, kind="ExternalOutput").ap()
        dbg_r = nc.dram_tensor("dbg_r", [16, 520], F32, kind="ExternalOutput").ap()

    sb = lambda name, shape, dt=BF16: stack.enter_context(nc.sbuf_tensor(name, shape, dt))

    ysrc = env["stream"]
    wred_sb = sb("wred_sb", [128, 8, 16])
    wg_sb = sb("wg_sb", [8, 16], F32)
    wgsq = sb("wgsq", [8, 16], F32)
    wgn = sb("wgn", [8, 2], F32)
    wgnb = sb("wgnb", [8, 16])
    wgnT = sb("wgnT", [16, 8])
    xr_bf = sb("xr_bf", [16, 512])
    lg_sb = sb("lg_sb", [8, 512])
    lgt = sb("lgt", [128, 4, 8], F32)
    nm1 = sb("nm1", [128, 1], F32)
    eq8 = sb("eq8", [128, 8], F32)
    e8 = sb("e8", [128, 8], F32)
    ssum = sb("ssum", [128, 1], F32)
    rs = sb("rs", [128, 1], F32)
    mtok = sb("mtok", [128, 8])
    m8 = sb("m8", [8, TOK])
    mrows = sb("mrows", [1, 8, TOK])
    b1c = sb("b1c", [128, 16], F32)
    b2_sb = sb("b2_sb", [8, 8, 128])
    w1t = sb("w1t", [128, 2, 8, 128])
    w2t = sb("w2t", [128, 2, 8, 128])
    hhb = sb("hhb", [128, 8, 2048])
    bcm4 = sb("bcm4", [128, 4, 512])
    out32 = sb("out32", [128, 8, 2048], F32)

    cG, cY, cP, cT, cV, cA = env["ctr"]
    PB0, PB1 = env["PB"]
    PF = env["PF"]
    ALL = env["ALL"]
    ENG = env["ENG"]
    id128 = env["id128"]
    ones1, onescol_f = env["ones1"], env["onescol_f"]
    ysq = env["ysq"]
    MDL = dict(max_dma_last_dim=8192)

    cG.tick(G.dma_start(out=wred_sb[:],
                        in_=w_red.rearrange("(kc p) g -> p kc g", p=128)))
    cG.tick(G.dma_start(out=wg_sb[:], in_=wg[:, :]))
    cG.tick(G.dma_start(out=b2_sb[:],
                        in_=b2.rearrange("e (dc m) -> e dc m", m=128)))
    fence(ENG, ALL)

    cA.tick(A.activation(wgsq[:], wg_sb[:], AF.Square))
    cA.wait(V)
    cV.tick(V.reduce_sum(wgn[:, 0:1], wgsq[:], axis=AX.X))
    cV.wait(A)
    cA.tick(A.activation(wgn[:, 1:2], wgn[:, 0:1], AF.Sqrt))
    cA.wait(V)
    cV.tick(V.tensor_scalar_max(wgn[:, 1:2], wgn[:, 1:2], 1e-4))
    cV.wait(V)
    cV.tick(V.reciprocal(wgn[:, 0:1], wgn[:, 1:2]))
    cV.wait(V)
    cV.tick(V.tensor_scalar(wgnb[:], wg_sb[:], wgn[:, 0:1], None, ALU.mult))
    cV.wait(T)
    cT.tick(T.transpose(PB0[0:16, 0:8], wgnb[:], id128[0:8, 0:8]))
    cT.wait(V)
    cV.tick(V.tensor_copy(wgnT[:], PB0[0:16, 0:8]))
    fence(ENG, ALL)

    if debug:
        fence(ENG, ALL)
        cG.tick(G.dma_start(out=dbg_r[0:8, 0:2], in_=wgn[:]))
        cG.tick(G.dma_start(out=dbg_r[0:8, 2:18], in_=wgnb[:]))
        cG.tick(G.dma_start(out=dbg_r[0:16, 18:26], in_=wgnT[:]))
        cG.tick(G.dma_start(out=dbg_r[0:8, 26:42], in_=wg_sb[:]))
    for g in range(4):
        off = 512 * g
        fence([T], [cA, cV])
        for kc in range(8):
            ins = T.matmul(PF[0][0:16, :], wred_sb[:, kc, :], ysrc[:, kc, off:off + 512],
                           start=(kc == 0), stop=(kc == 7))
        t_xr = cT.tick(ins)
        A.wait_ge(cT.sem, t_xr)
        a_xr = cA.tick(A.activation(xr_bf[:], PF[0][0:16, :], AF.Copy))
        T.wait_ge(cA.sem, a_xr)
        t_lg = cT.tick(T.matmul(PF[1][0:8, :], wgnT[:], xr_bf[:], start=True, stop=True))
        V.wait_ge(cT.sem, t_lg)
        cV.tick(V.tensor_copy(lg_sb[:], PF[1][0:8, :]))
        if debug and g == 0:
            v_d = cV.n
            G.wait_ge(cV.sem, v_d)
            cG.tick(G.dma_start(out=dbg_r[0:16, 42:42 + 128], in_=xr_bf[:, 0:128]))
            cG.tick(G.dma_start(out=dbg_r[0:8, 256:256 + 256], in_=lg_sb[:, 0:256]))
        cV.wait(T)
        for t4 in range(4):
            ins = T.transpose(PB0[0:128, 8 * t4:8 * (t4 + 1)],
                              lg_sb[:, 128 * t4:128 * (t4 + 1)], id128[0:8, 0:8])
        t_tr = cT.tick(ins)
        V.wait_ge(cT.sem, t_tr)
        cV.tick(V.tensor_copy(lgt[:], PB0[0:128, 0:32].rearrange("p (a b) -> p a b", b=8)))
        for t4 in range(4):
            cV.wait(V)
            cV.tick(V.reduce_max(nm1[:], lgt[:, t4, :], axis=AX.X, negate=True))
            cV.wait(V)
            cV.tick(V.tensor_scalar(eq8[:], lgt[:, t4, :], nm1[:, 0:1], 0.0,
                                    ALU.add, ALU.is_equal))
            cV.wait(A)
            cA.tick(A.activation(e8[:], lgt[:, t4, :], AF.Exp, bias=nm1[:, 0:1],
                                 accum_out=ssum[:, 0:1]))
            cA.wait(V)
            cV.tick(V.reciprocal(rs[:], ssum[:]))
            cV.wait(V)
            cV.tick(V.tensor_scalar(mtok[:], eq8[:], rs[:, 0:1], None, ALU.mult))
            cV.wait(T)
            t_f = cT.tick(T.transpose(PB1[0:8, 128 * t4:128 * (t4 + 1)], mtok[:], id128[:]))
            for e in range(8):
                ins = T.transpose(PB0[0:1, 128 * e:128 * (e + 1)],
                                  mtok[:, e:e + 1], id128[:])
            t_r = cT.tick(ins)
            V.wait_ge(cT.sem, t_r)
            cV.tick(V.tensor_copy(mrows[0:1, :, off + 128 * t4:off + 128 * (t4 + 1)],
                                  PB0[0:1, 0:1024].rearrange("p (a b) -> p a b", b=128)))
        V.wait_ge(cT.sem, cT.n)
        cV.tick(V.tensor_copy(m8[:, off:off + 512], PB1[0:8, 0:512]))
        fence([T], [cV])
        if debug and g == 0:
            cY.tick(Y.dma_start(out=dbg_lgt[:, :, :], in_=lgt[:]))
    fence(ENG, ALL)
    if debug:
        cG.tick(G.dma_start(out=dbg_m8[:, :], in_=m8[:]))
        for e in range(8):
            cG.tick(G.dma_start(out=dbg_mrows[e:e + 1, :], in_=mrows[0:1, e, :]))

    P = dict(W1e=[W1[e] for e in range(8)], W2e=[W2[e] for e in range(8)],
             b1s=b1s, b2_sb=b2_sb, m8=m8, mrows=mrows,
             w1t=w1t, w2t=w2t, hhb=hhb, bcm4=bcm4, b1c=b1c, out32=out32,
             ysrc=ysrc, act=AF.Gelu, HC=16)
    emit_moe_experts(nc, env, P, 1.0, env["stream"], y2, True)


def _stage_B(inputs, y1_shards):
    maps = []
    b1s = np.ascontiguousarray(
        inputs["L0_ffn_b1"].reshape(8, 16, 128).transpose(0, 2, 1))
    for c in range(8):
        maps.append({
            "y1": None if y1_shards[c] is None else np.asarray(y1_shards[c]),
            "w_red": inputs["L0_ffn_w_red"],
            "wg": inputs["L0_ffn_wg"],
            "W1": inputs["L0_ffn_W1"],
            "b1s": b1s,
            "W2": inputs["L0_ffn_W2"],
            "b2": inputs["L0_ffn_b2"],
        })
    return maps






def emit_C(nc, stack, env):
    G, T, V, A, Y = env["eng"]
    S = 4096
    kvT = nc.dram_tensor("kvT", [D, S], BF16, kind="ExternalInput").ap()
    qinT = nc.dram_tensor("qinT", [D, TOK], BF16, kind="ExternalInput").ap()
    resT = nc.dram_tensor("resT", [D, TOK], BF16, kind="ExternalInput").ap()
    h32 = nc.dram_tensor("h32", [128, 1], F32, kind="ExternalInput").ap()
    Wqkv = nc.dram_tensor("Wqkv", [NH, D, 192], F32, kind="ExternalInput").ap()
    Wout = nc.dram_tensor("Wout", [D, D], F32, kind="ExternalInput").ap()
    o_dram = nc.dram_tensor("o_dram", [NH, TOK, HD], BF16, kind="Internal").ap()

    sb = lambda name, shape, dt=BF16: stack.enter_context(nc.sbuf_tensor(name, shape, dt))

    kvc = sb("kvc", [128, 2, 512])
    qc = sb("qc", [128, 2, 512])
    wqkv_sb = sb("wqkv_sb", [128, 8, 2, 192])
    KT = sb("KT", [128, S])
    vstage = sb("vstage", [128, S])
    vt = sb("vt", [128, 2, 32, 65])
    QT = sb("QT", [128, TOK])
    sq = sb("sq", [128, S])
    ptile = sb("ptile", [128, 512])
    h32c = sb("h32c", [128, 1], F32)
    m1t = sb("m1t", [128, 128])
    m2t = sb("m2t", [128, 128])
    nmax = sb("nmax", [1, 48], F32)
    bnd = sb("bnd", [1, 8], F32)
    bndbf = sb("bndbf", [1, 1])
    boundc = sb("boundc", [128, 2], F32)
    lr = sb("lr", [1, 2, 512], F32)
    r_bf = sb("r_bf", [1, 2, 512])
    rcol = sb("rcol", [128, 2, 4], F32)
    oevac = sb("oevac", [128, 512])
    osb = sb("osb", [128, 2, 4, 64])
    wout_c = sb("wout_c", [128, 2, D])
    wsum_sb = sb("wsum_sb", [1, D], F32)
    wsum_bf = sb("wsum_bf", [1, D])
    wsumT = sb("wsumT", [128, 8], F32)
    att = sb("att", [128, 512])
    resc = sb("resc", [128, NCHUNK, 512])

    cG, cY, cP, cT, cV, cA = env["ctr"]
    PB0, PB1 = env["PB"]
    PF = env["PF"]
    ALL = env["ALL"]
    ENG = env["ENG"]
    id128, id64d = env["id128"], env["id64d"]
    ones1, onescol_f, onescol_b = env["ones1"], env["onescol_f"], env["onescol_b"]
    ysb = env["ysb"] = sb("ysbC", [128, NCHUNK, 512], F32)
    ysq = env["ysq"]
    io2 = sb("io2", [128, 128], F32)

    cY.tick(Y.dma_start(out=h32c[:], in_=h32[:, :]))
    cP.tick(G.iota(io2[:], pattern=[[64, 4], [1, 32]], base=0, channel_multiplier=-1,
                   allow_small_or_imprecise_dtypes=True))
    cP.tick(G.memset(vt[:, :, :, 64], 1.0))
    cP.wait(V)
    cY.wait(V)
    cV.tick(V.tensor_scalar(m1t[:], io2[:], h32c[:, 0:1], 0.0, ALU.add, ALU.is_ge))
    cV.tick(V.tensor_scalar(m2t[:], io2[:], h32c[:, 0:1], 128.0, ALU.add, ALU.is_ge))
    for kc in range(8):
        buf = kc % 2
        fence([G], [cT])
        g_w = cG.tick(G.dma_start(out=wout_c[:, buf, :],
                                  in_=Wout[128 * kc:128 * (kc + 1), :], max_dma_last_dim=8192))
        T.wait_ge(cG.sem, g_w)
        for half in range(2):
            ins = T.matmul(PF[half][0:1, :], onescol_b[:, 0:1],
                           wout_c[:, buf, 512 * half:512 * (half + 1)],
                           start=(kc == 0), stop=(kc == 7))
        cT.tick(ins)
    cT.wait(V)
    cV.tick(V.tensor_copy(wsum_sb[0:1, 0:512], PF[0][0:1, :]))
    cV.tick(V.tensor_copy(wsum_sb[0:1, 512:1024], PF[1][0:1, :]))
    cV.wait(V)
    cV.tick(V.tensor_copy(wsum_bf[0:1, :], wsum_sb[0:1, :]))
    cV.wait(T)
    for kc in range(8):
        ins = T.transpose(PB0[0:128, 2 * kc:2 * kc + 1], wsum_bf[0:1, 128 * kc:128 * (kc + 1)],
                          id128[0:1, 0:1])
    t_ws = cT.tick(ins)
    A.wait_ge(cT.sem, t_ws)
    cA.tick(A.activation(wsumT[:], PB0[0:128, 0:16].rearrange("p (a b) -> p a b", b=2)[:, :, 0],
                         AF.Copy))
    fence(ENG, ALL)

    for p in range(NPAIR):
        fence([G], [cT])
        for kc in range(8):
            g_w = cG.tick(G.dma_start(
                out=wqkv_sb[:, kc, :, :],
                in_=Wqkv[2 * p:2 * p + 2, 128 * kc:128 * (kc + 1), :].rearrange("h p w -> p h w")))
        T.wait_ge(cG.sem, cG.n)
        for g in range(8):
            fence([T], [cA, cV])
            for kc in range(8):
                buf = kc % 2
                fence([Y], [cT])
                y_kv = cY.tick(Y.dma_start(out=kvc[:, buf, 0:512],
                                           in_=kvT[128 * kc:128 * (kc + 1), 512 * g:512 * (g + 1)]))
                T.wait_ge(cY.sem, y_kv)
                for h2 in range(2):
                    T.matmul(PF[0][64 * h2:64 * (h2 + 1), :], wqkv_sb[:, kc, h2, 64:128],
                             kvc[:, buf, 0:512], start=(kc == 0), stop=(kc == 7),
                             tile_position=(0, 64 * h2), skip_group_check=True)
                    ins = T.matmul(PF[1][64 * h2:64 * (h2 + 1), :], wqkv_sb[:, kc, h2, 128:192],
                                   kvc[:, buf, 0:512], start=(kc == 0), stop=(kc == 7),
                                   tile_position=(0, 64 * h2), skip_group_check=True)
                cT.tick(ins)
            cT.wait(A)
            cT.wait(V)
            cA.tick(A.activation(KT[:, 512 * g:512 * (g + 1)], PF[0][:], AF.Copy))
            cV.tick(V.tensor_copy(vstage[:, 512 * g:512 * (g + 1)], PF[1][:]))
        for g in range(4):
            fence([T], [cA])
            for kc in range(8):
                buf = kc % 2
                fence([Y], [cT])
                y_q = cY.tick(Y.dma_start(out=qc[:, buf, 0:512],
                                          in_=qinT[128 * kc:128 * (kc + 1), 512 * g:512 * (g + 1)]))
                T.wait_ge(cY.sem, y_q)
                for h2 in range(2):
                    ins = T.matmul(PF[0][64 * h2:64 * (h2 + 1), :], wqkv_sb[:, kc, h2, 0:64],
                                   qc[:, buf, 0:512], start=(kc == 0), stop=(kc == 7),
                                   tile_position=(0, 64 * h2), skip_group_check=True)
                cT.tick(ins)
            cT.wait(A)
            cA.tick(A.activation(QT[:, 512 * g:512 * (g + 1)], PF[0][:], AF.Copy))
        fence(ENG, [cT, cV, cA])
        for h2 in range(2):
            for c0 in range(0, 32, 4):
                pb = (PB0, PB1)[(c0 // 4) % 2]
                fence([T], [cV])
                for cc in range(4):
                    ins = T.transpose(pb[:, 64 * cc:64 * (cc + 1)],
                                      vstage[64 * h2:64 * (h2 + 1), 128 * (c0 + cc):128 * (c0 + cc + 1)],
                                      id64d[64 * h2:64 * (h2 + 1), 0:64])
                cT.tick(ins)
                cT.wait(V)
                cV.tick(V.tensor_copy(vt[:, h2, c0:c0 + 4, 0:64],
                                      pb[:, 0:256].rearrange("p (a b) -> p a b", b=64)))
        fence([A], [cT, cV])
        for src, width, ng, gbase in ((QT, TOK, 4, 0), (KT, S, 8, 16)):
            a_sq = cA.tick(A.activation(sq[:, 0:width], src[:, 0:width], AF.Square))
            T.wait_ge(cA.sem, a_sq)
            for h2 in range(2):
                for gi in range(ng):
                    cV.wait(T)
                    ins = T.matmul(PF[5][0:1, 0:512],
                                   onescol_b[64 * h2:64 * (h2 + 1), 0:1],
                                   sq[64 * h2:64 * (h2 + 1), 512 * gi:512 * (gi + 1)],
                                   start=True, stop=True)
                    t_mm = cT.tick(ins)
                    V.wait_ge(cT.sem, t_mm)
                    cV.tick(V.reduce_max(nmax[0:1, gbase + 8 * h2 + gi:gbase + 8 * h2 + gi + 1],
                                         PF[5][0:1, 0:512], axis=AX.X))
            fence([A], [cT])
        for h2 in range(2):
            cV.wait(V)
            cV.tick(V.reduce_max(bnd[0:1, 0:1], nmax[0:1, 8 * h2:8 * h2 + 4], axis=AX.X))
            cV.tick(V.reduce_max(bnd[0:1, 1:2], nmax[0:1, 16 + 8 * h2:16 + 8 * h2 + 8], axis=AX.X))
            cV.wait(V)
            cV.tick(V.tensor_mul(bnd[0:1, 2:3], bnd[0:1, 0:1], bnd[0:1, 1:2]))
            cV.wait(A)
            cA.tick(A.activation(bnd[0:1, 3:4], bnd[0:1, 2:3], AF.Sqrt))
            cA.wait(V)
            cV.tick(V.tensor_single_scalar(bnd[0:1, 4:5], bnd[0:1, 3:4], -0.125, ALU.mult))
            cV.wait(V)
            cV.tick(V.tensor_copy(bndbf[0:1, 0:1], bnd[0:1, 4:5]))
            cV.wait(T)
            cA.wait(T)
            t_mm = cT.tick(T.matmul(PF[5][:, 0:1], ones1[0:1, :], bndbf[0:1, 0:1],
                                    start=True, stop=True))
            A.wait_ge(cT.sem, t_mm)
            cA.tick(A.activation(boundc[:, h2:h2 + 1], PF[5][:, 0:1], AF.Copy))
        fence(ENG, [cT, cV, cA])
        for g4 in range(4):
            nchunks = 8 * g4 + 8
            for h2 in range(2):
                ltgt = PF[5] if h2 == 0 else PF[0]
                for c in range(nchunks):
                    j0 = c // 2
                    qoff = 128 * max(0, j0 - 4 * g4)
                    aw = 512 - qoff
                    spf = PF[2 + h2]
                    t_s = cT.tick(T.matmul(spf[:, 0:aw],
                                           KT[64 * h2:64 * (h2 + 1), 128 * c:128 * (c + 1)],
                                           QT[64 * h2:64 * (h2 + 1), 512 * g4 + qoff:512 * (g4 + 1)],
                                           start=True, stop=True))
                    A.wait_ge(cT.sem, t_s)
                    fence([A], [cT])
                    a_e = cA.tick(A.activation(ptile[:, 0:aw], spf[:, 0:aw], AF.Exp,
                                               bias=boundc[:, h2:h2 + 1], scale=0.125))
                    if c >= 8 * g4:
                        V.wait_ge(cA.sem, a_e)
                        cV.tick(V.tensor_mul(ptile[:, 0:128], ptile[:, 0:128],
                                             (m1t if c % 2 == 0 else m2t)[:]))
                        T.wait_ge(cV.sem, cV.n)
                    else:
                        T.wait_ge(cA.sem, a_e)
                    T.matmul(PF[4][64 * h2:64 * (h2 + 1), qoff:512], vt[:, h2, c, 0:64],
                             ptile[:, 0:aw], start=(c == 0), stop=(c == nchunks - 1),
                             tile_position=(0, 64 * h2), skip_group_check=True)
                    ins = T.matmul(ltgt[0:1, qoff:512], vt[:, h2, c, 64:65],
                                   ptile[:, 0:aw], start=(c == 0), stop=(c == nchunks - 1),
                                   skip_group_check=True)
                    cT.tick(ins)
            V.wait_ge(cT.sem, cT.n)
            cV.tick(V.tensor_copy(lr[0:1, 0, :], PF[5][0:1, 0:512]))
            cV.tick(V.tensor_copy(lr[0:1, 1, :], PF[0][0:1, 0:512]))
            cV.wait(A)
            cA.tick(A.activation(lr[0:1, :, :], lr[0:1, :, :], AF.Ln))
            cA.wait(A)
            a_r = cA.tick(A.activation(r_bf[0:1, :, :], lr[0:1, :, :], AF.Exp, scale=-1.0))
            A.wait_ge(cA.sem, a_r)
            a_o = cA.tick(A.activation(oevac[:], PF[4][:], AF.Copy))
            T.wait_ge(cA.sem, a_o)
            for h2 in range(2):
                fence([T], [cA])
                for t4 in range(4):
                    ins = T.transpose(PB0[0:128, 2 * t4:2 * t4 + 1],
                                      r_bf[0:1, h2, 128 * t4:128 * (t4 + 1)], id128[0:1, 0:1])
                cT.tick(ins)
                cT.wait(A)
                cA.tick(A.activation(rcol[:, h2, :],
                                     PB0[0:128, 0:8].rearrange("p (a b) -> p a b", b=2)[:, :, 0],
                                     AF.Copy))
            for rnd in range(2):
                h2 = rnd
                fence([T], [cV])
                for t4 in range(4):
                    ins = T.transpose(PB1[:, 64 * t4:64 * (t4 + 1)],
                                      oevac[64 * h2:64 * (h2 + 1), 128 * t4:128 * (t4 + 1)],
                                      id64d[64 * h2:64 * (h2 + 1), 0:64])
                cT.tick(ins)
                cT.wait(V)
                cA.wait(V)
                for t4 in range(4):
                    cV.tick(V.tensor_scalar(osb[:, h2, t4, :], PB1[:, 64 * t4:64 * (t4 + 1)],
                                            rcol[:, h2, t4:t4 + 1], None, ALU.mult))
            v_done = cV.n
            Y.wait_ge(cV.sem, v_done)
            for h2 in range(2):
                cY.tick(Y.dma_start(
                    out=o_dram[2 * p + h2, 512 * g4:512 * (g4 + 1), :].rearrange(
                        "(t4 r) e -> r t4 e", t4=4),
                    in_=osb[:, h2, :, :]))
            fence([T], [cA, cV, cY])
        fence(ENG, ALL)

    o_flat = o_dram.rearrange("h q e -> (h q e)")
    for g in range(4):
        off = 512 * g
        for dc in range(8):
            fence([Y], [cV])
            y_r = cY.tick(Y.dma_start(out=resc[:, dc, :],
                                      in_=resT[128 * dc:128 * (dc + 1), off:off + 512]))
            y_a = cY.tick(Y.dma_start(
                out=att[:, 0:512],
                in_=o_flat[dc * 262144:(dc + 1) * 262144].rearrange(
                    "(r c) -> r c", r=128)[:, off:off + 512]))
            V.wait_ge(cY.sem, y_a)
            cV.wait(V)
            cV.tick(V.scalar_tensor_tensor(ysb[:, dc, :], att[:, 0:512],
                                           wsumT[:, dc:dc + 1], resc[:, dc, :],
                                           ALU.mult, ALU.add))
        emit_ln_group(env, ysb, ysq, env["stream"], None, off, sb_off=off)
    fence(ENG, ALL)


def _stage_C(inputs, y2_shards):
    wqkv = inputs["L1_attn_W_qkv"]
    wout = np.ascontiguousarray(inputs["L1_attn_W_out"].reshape(D, D))
    maps = []
    y2f = []
    for b in range(4):
        y2f.append(np.concatenate([np.asarray(y2_shards[2 * b]),
                                   np.asarray(y2_shards[2 * b + 1])], axis=1))
    for c in range(8):
        b, h = c // 2, c % 2
        full = y2f[b]
        qcols = (np.arange(4096).reshape(64, 64)[:, 32 * h:32 * h + 32]).reshape(-1)
        maps.append({
            "kvT": full,
            "qinT": np.ascontiguousarray(full[:, qcols]),
            "resT": np.ascontiguousarray(full[:, 2048 * h:2048 * (h + 1)]),
            "h32": np.full((128, 1), 32.0 * h, np.float32),
            "Wqkv": wqkv,
            "Wout": wout,
        })
    return maps






def emit_D(nc, stack, env):
    G, T, V, A, Y = env["eng"]
    H1 = 4096
    rw1 = nc.dram_tensor("rw1", [D, D], F32, kind="ExternalInput").ap()
    rb1s = nc.dram_tensor("rb1s", [128, 8], F32, kind="ExternalInput").ap()
    rw2 = nc.dram_tensor("rw2", [D, 8], F32, kind="ExternalInput").ap()
    rb2 = nc.dram_tensor("rb2", [8, 1], F32, kind="ExternalInput").ap()
    we1 = nc.dram_tensor("we1", [8, D, H1], F32, kind="ExternalInput").ap()
    be1s = nc.dram_tensor("be1s", [8, 128, 32], F32, kind="ExternalInput").ap()
    we2 = nc.dram_tensor("we2", [8, H1, D], F32, kind="ExternalInput").ap()
    be2 = nc.dram_tensor("be2", [8, D], F32, kind="ExternalInput").ap()
    yout = nc.dram_tensor("yout", [D, TOK], F32, kind="ExternalOutput").ap()

    sb = lambda name, shape, dt=BF16: stack.enter_context(nc.sbuf_tensor(name, shape, dt))

    ysrc = env["stream"]
    big = sb("big", [128, 16384])
    t_v = big[:].rearrange("p (a b) -> p a b", b=TOK)
    hhb = t_v
    rw1t = sb("rw1t", [128, 2, 8, 128])
    rw2_sb = sb("rw2_sb", [128, 8, 8])
    rb1c = sb("rb1c", [128, 8], F32)
    rb2c = sb("rb2c", [8, 1], F32)
    lg_sb = sb("lg_sb", [8, 512])
    lgt = sb("lgt", [128, 4, 8], F32)
    nm1 = sb("nm1", [128, 1], F32)
    nm2 = sb("nm2", [128, 1], F32)
    eq8 = sb("eq8", [128, 8], F32)
    l2 = sb("l2", [128, 8], F32)
    msk8 = sb("msk8", [128, 8], F32)
    e8 = sb("e8", [128, 8], F32)
    wr8 = sb("wr8", [128, 8], F32)
    ssum = sb("ssum", [128, 1], F32)
    rs = sb("rs", [128, 1], F32)
    wtok = sb("wtok", [128, 8])
    w8 = sb("w8", [8, TOK])
    wrows = sb("wrows", [1, 8, TOK])
    be1c = sb("be1c", [128, 32], F32)
    be2_sb = sb("be2_sb", [8, 8, 128])
    w1t = sb("w1t", [128, 2, 8, 128])
    w2t = sb("w2t", [128, 2, 8, 128])
    bcm4 = sb("bcm4", [128, 4, 512])
    out32 = sb("out32", [128, 8, 2048], F32)
    y4sb = big[:, 0:8192].bitcast(F32).rearrange("p (a b) -> p a b", b=512)

    cG, cY, cP, cT, cV, cA = env["ctr"]
    PB0, PB1 = env["PB"]
    PF = env["PF"]
    ALL = env["ALL"]
    ENG = env["ENG"]
    id128 = env["id128"]
    ones1, onescol_f = env["ones1"], env["onescol_f"]
    ysq = env["ysq"]
    MDL = dict(max_dma_last_dim=8192)

    cG.tick(G.dma_start(out=rw2_sb[:],
                        in_=rw2.rearrange("(kc p) e -> p kc e", p=128)))
    cG.tick(G.dma_start(out=rb1c[:], in_=rb1s[:, :]))
    cG.tick(G.dma_start(out=rb2c[:], in_=rb2[:, :]))
    cG.tick(G.dma_start(out=be2_sb[:],
                        in_=be2.rearrange("e (dc m) -> e dc m", m=128)))
    fence(ENG, ALL)

    for mc in range(8):
        buf = mc % 2
        fence([G], [cT])
        g_w = cG.tick(G.dma_start(
            out=rw1t[:, buf, :, :],
            in_=rw1.rearrange("(kc p) d -> p kc d", p=128)[:, :, 128 * mc:128 * (mc + 1)],
            **MDL))
        T.wait_ge(cG.sem, g_w)
        for g in range(4):
            pf = PF[(mc * 4 + g) % 2]
            fence([T], [cA])
            for kc in range(8):
                ins = T.matmul(pf[:], rw1t[:, buf, kc, :],
                               ysrc[:, kc, 512 * g:512 * (g + 1)],
                               start=(kc == 0), stop=(kc == 7))
            t_h = cT.tick(ins)
            A.wait_ge(cT.sem, t_h)
            cA.tick(A.activation(t_v[:, mc, 512 * g:512 * (g + 1)], pf[:],
                                 AF.Silu, bias=rb1c[:, mc:mc + 1]))
    fence(ENG, ALL)
    for g in range(4):
        off = 512 * g
        fence([T], [cA, cV])
        for kc in range(8):
            ins = T.matmul(PF[1][0:8, :], rw2_sb[:, kc, :], t_v[:, kc, off:off + 512],
                           start=(kc == 0), stop=(kc == 7))
        t_lg = cT.tick(ins)
        V.wait_ge(cT.sem, t_lg)
        cV.tick(V.tensor_scalar(lg_sb[:], PF[1][0:8, :], rb2c[:, 0:1], None, ALU.add))
        cV.wait(T)
        for t4 in range(4):
            ins = T.transpose(PB0[0:128, 8 * t4:8 * (t4 + 1)],
                              lg_sb[:, 128 * t4:128 * (t4 + 1)], id128[0:8, 0:8])
        t_tr = cT.tick(ins)
        V.wait_ge(cT.sem, t_tr)
        cV.tick(V.tensor_copy(lgt[:], PB0[0:128, 0:32].rearrange("p (a b) -> p a b", b=8)))
        for t4 in range(4):
            cV.wait(V)
            cV.tick(V.reduce_max(nm1[:], lgt[:, t4, :], axis=AX.X, negate=True))
            cV.wait(V)
            cV.tick(V.tensor_scalar(eq8[:], lgt[:, t4, :], nm1[:, 0:1], 0.0,
                                    ALU.add, ALU.is_equal))
            cV.wait(V)
            cV.tick(V.scalar_tensor_tensor(l2[:], eq8[:], -1e9, lgt[:, t4, :],
                                           ALU.mult, ALU.add))
            cV.wait(V)
            cV.tick(V.reduce_max(nm2[:], l2[:], axis=AX.X, negate=True))
            cV.wait(V)
            cV.tick(V.tensor_scalar(msk8[:], lgt[:, t4, :], nm2[:, 0:1], 0.0,
                                    ALU.add, ALU.is_ge))
            cV.wait(A)
            cA.tick(A.activation(e8[:], lgt[:, t4, :], AF.Exp, bias=nm1[:, 0:1]))
            cA.wait(V)
            cV.tick(V.tensor_mul(wr8[:], e8[:], msk8[:]))
            cV.wait(V)
            cV.tick(V.reduce_sum(ssum[:], wr8[:], axis=AX.X))
            cV.wait(V)
            cV.tick(V.reciprocal(rs[:], ssum[:]))
            cV.wait(V)
            cV.tick(V.tensor_scalar(wtok[:], wr8[:], rs[:, 0:1], None, ALU.mult))
            cV.wait(T)
            t_f = cT.tick(T.transpose(PB1[0:8, 128 * t4:128 * (t4 + 1)], wtok[:], id128[:]))
            for e in range(8):
                ins = T.transpose(PB0[0:1, 128 * e:128 * (e + 1)],
                                  wtok[:, e:e + 1], id128[:])
            t_r = cT.tick(ins)
            V.wait_ge(cT.sem, t_r)
            cV.tick(V.tensor_copy(wrows[0:1, :, off + 128 * t4:off + 128 * (t4 + 1)],
                                  PB0[0:1, 0:1024].rearrange("p (a b) -> p a b", b=128)))
        V.wait_ge(cT.sem, cT.n)
        cV.tick(V.tensor_copy(w8[:, off:off + 512], PB1[0:8, 0:512]))
        fence([T], [cV])
    fence(ENG, ALL)

    P = dict(W1e=[we1[e] for e in range(8)], W2e=[we2[e] for e in range(8)],
             b1s=be1s, b2_sb=be2_sb, m8=w8, mrows=wrows,
             w1t=w1t, w2t=w2t, hhb=hhb, bcm4=bcm4, b1c=be1c, out32=out32,
             ysrc=ysrc, act=AF.Silu, HC=32)
    emit_moe_experts(nc, env, P, 2.0, y4sb, yout, False)


def _stage_D(inputs, y3_shards):
    rb1s = np.ascontiguousarray(inputs["L1_ffn_rb1"].reshape(8, 128).T)
    be1s = np.ascontiguousarray(
        inputs["L1_ffn_be1"].reshape(8, 32, 128).transpose(0, 2, 1))
    maps = []
    for c in range(8):
        maps.append({
            "y3": None if y3_shards[c] is None else np.asarray(y3_shards[c]),
            "rw1": inputs["L1_ffn_rw1"],
            "rb1s": rb1s,
            "rw2": inputs["L1_ffn_rw2"],
            "rb2": inputs["L1_ffn_rb2"].reshape(8, 1),
            "we1": inputs["L1_ffn_we1"],
            "be1s": be1s,
            "we2": inputs["L1_ffn_we2"],
            "be2": inputs["L1_ffn_be2"],
        })
    return maps






def _stage_A(inputs):
    x0 = inputs["L0_attn_x"]
    maps = []
    for c in range(8):
        b, h = c // 2, c % 2
        start = 2048 * h
        seg = np.zeros((TOKH, D), np.float32)
        lo = start - 128
        off = 128 if h == 0 else 0
        seg[off:] = x0[b, max(lo, 0):start + 2048]
        valid = np.zeros(TOKH, np.float32)
        valid[off:] = 1.0
        maps.append({
            "x0T": np.ascontiguousarray(seg.T),
            "validT": np.ascontiguousarray(valid.reshape(17, 128).T),
            "Wq": inputs["L0_attn_Wq"],
            "Wk": inputs["L0_attn_Wk"],
            "Wv": inputs["L0_attn_Wv"],
            "Wo": inputs["L0_attn_Wo"],
        })
    return maps





def build_launch_AB(nc):
    from contextlib import ExitStack
    stack = ExitStack()
    with stack:
        env = mk_env(nc, stack)
        with ExitStack() as sA:
            emit_A(nc, sA, env)
        with ExitStack() as sB:
            emit_B(nc, sB, env)
        cY = env["ctr"][1]
        env["eng"][4].wait_ge(cY.sem, cY.n)
    return nc


def build_launch_CD(nc):
    from contextlib import ExitStack
    stack = ExitStack()
    with stack:
        env = mk_env(nc, stack)
        with ExitStack() as sC:
            emit_C(nc, sC, env)
        with ExitStack() as sD:
            emit_D(nc, sD, env)
        cY = env["ctr"][1]
        env["eng"][4].wait_ge(cY.sem, cY.n)
    return nc


def run_launch_AB(inputs):
    nc = bass.Bass("TRN2", target_bir_lowering=False)
    build_launch_AB(nc)
    mapsA = _stage_A(inputs)
    mapsB = _stage_B(inputs, [None] * 8)
    maps = []
    for c in range(8):
        m = dict(mapsA[c])
        mB = dict(mapsB[c]); mB.pop("y1")
        m.update(mB)
        maps.append(m)
    res = run_bass_kernel_spmd(nc, maps, core_ids=list(range(8)))
    return [r["y2"] for r in res.results]


def run_launch_CD(inputs, y2_shards):
    nc = bass.Bass("TRN2", target_bir_lowering=False)
    build_launch_CD(nc)
    mapsC = _stage_C(inputs, y2_shards)
    mapsD = _stage_D(inputs, [None] * 8)
    maps = []
    for c in range(8):
        m = dict(mapsC[c])
        mD = dict(mapsD[c]); mD.pop("y3")
        m.update(mD)
        maps.append(m)
    res = run_bass_kernel_spmd(nc, maps, core_ids=list(range(8)))
    return [r["yout"] for r in res.results]


def kernel(**inputs):
    y2 = run_launch_AB(inputs)
    y4 = run_launch_CD(inputs, y2)
    out = np.empty((4, 4096, 1024), np.float32)
    for c in range(8):
        out[c // 2, 2048 * (c % 2):2048 * (c % 2 + 1), :] = np.asarray(y4[c]).T
    return out


if __name__ == "__main__":
    pass


# version 06
# speedup vs baseline: 2.1678x; 1.3120x over previous
import numpy as np
import ml_dtypes
import concourse.bass as bass
import concourse.mybir as mybir
from concourse.bass_utils import run_bass_kernel_spmd

F32 = mybir.dt.float32
BF16 = mybir.dt.bfloat16
AF = mybir.ActivationFunctionType
ALU = mybir.AluOpType
AX = mybir.AxisListType
BF = ml_dtypes.bfloat16

D = 1024
NCHUNK = 8
TOK = 2048
TOKH = TOK + 128
NH = 16
HD = 64
NPAIR = 8
WIN = 64


class Ctr:
    def __init__(self, nc, stack, name, step=1):
        self.sem = stack.enter_context(nc.semaphore(name))
        self.n = 0
        self.step = step

    def tick(self, ins, k=None):
        k = self.step if k is None else k
        ins.then_inc(self.sem, k)
        self.n += k
        return self.n

    def wait(self, eng, n=None):
        eng.wait_ge(self.sem, self.n if n is None else n)


def fence(engines, ctrs):
    for e in engines:
        for c in ctrs:
            if c.n > 0:
                c.wait(e)



def emit_ln_group(env, ysb, ysq, outsb, out_dram, off, out_w=512, sb_off=None):
    (G, T, V, A, Y) = env["eng"]
    (cG, cY, cP, cT, cV, cA) = env["ctr"]
    PF = env["PF"]
    onescol_f, ones1 = env["onescol_f"], env["ones1"]
    st_sb, mu, m2, var, rstd, nmur = env["st_sb"], env["mu"], env["m2"], env["var"], env["rstd"], env["nmur"]
    rstd_bf, nmur_bf, bcln = env["rstd_bf"], env["nmur_bf"], env["bcln"]
    AFL, ALUL, w = AF, ALU, out_w
    fence([T, A], [cV])
    for dc in range(8):
        ins = T.matmul(PF[2][0:1, 0:w], onescol_f[:, 0:1], ysb[:, dc, 0:w],
                       start=(dc == 0), stop=(dc == 7))
    cT.tick(ins)
    for dc in range(8):
        a_sq = cA.tick(A.activation(ysq[:, 0:w], ysb[:, dc, 0:w], AFL.Square))
        T.wait_ge(cA.sem, a_sq)
        ins = T.matmul(PF[3][0:1, 0:w], onescol_f[:, 0:1], ysq[:, 0:w],
                       start=(dc == 0), stop=(dc == 7))
        t_s2 = cT.tick(ins)
        if dc < 7:
            A.wait_ge(cT.sem, t_s2)
    V.wait_ge(cT.sem, cT.n)
    cV.tick(V.tensor_copy(st_sb[0:1, 0:w], PF[2][0:1, 0:w]))
    cV.tick(V.tensor_copy(st_sb[0:1, 512:512 + w], PF[3][0:1, 0:w]))
    cV.wait(V)
    cV.tick(V.tensor_single_scalar(mu[0:1, 0:w], st_sb[0:1, 0:w], 1.0 / 1024, ALUL.mult))
    cV.tick(V.tensor_single_scalar(m2[0:1, 0:w], st_sb[0:1, 512:512 + w], 1.0 / 1024, ALUL.mult))
    cV.wait(V)
    cV.tick(V.tensor_mul(var[0:1, 0:w], mu[0:1, 0:w], mu[0:1, 0:w]))
    cV.wait(V)
    cV.tick(V.tensor_sub(var[0:1, 0:w], m2[0:1, 0:w], var[0:1, 0:w]))
    cV.wait(V)
    cV.tick(V.tensor_single_scalar(var[0:1, 0:w], var[0:1, 0:w], 1e-5, ALUL.add))
    cV.wait(A)
    cA.tick(A.activation(var[0:1, 0:w], var[0:1, 0:w], AFL.Ln))
    cA.wait(A)
    a_r = cA.tick(A.activation(rstd[0:1, 0:w], var[0:1, 0:w], AFL.Exp, scale=-0.5))
    V.wait_ge(cA.sem, a_r)
    cV.tick(V.tensor_mul(nmur[0:1, 0:w], mu[0:1, 0:w], rstd[0:1, 0:w]))
    cV.wait(V)
    cV.tick(V.tensor_single_scalar(nmur[0:1, 0:w], nmur[0:1, 0:w], -1.0, ALUL.mult))
    cV.tick(V.tensor_copy(rstd_bf[0:1, 0:w], rstd[0:1, 0:w]))
    cV.wait(V)
    cV.tick(V.tensor_copy(nmur_bf[0:1, 0:w], nmur[0:1, 0:w]))
    cV.wait(T)
    cT.tick(T.matmul(PF[4][:, 0:w], ones1[0:1, :], rstd_bf[0:1, 0:w], start=True, stop=True))
    t_b2 = cT.tick(T.matmul(PF[5][:, 0:w], ones1[0:1, :], nmur_bf[0:1, 0:w], start=True, stop=True))
    A.wait_ge(cT.sem, t_b2)
    cA.tick(A.activation(bcln[:, 0, 0:w], PF[4][:, 0:w], AFL.Copy))
    a_bc = cA.tick(A.activation(bcln[:, 1, 0:w], PF[5][:, 0:w], AFL.Copy))
    V.wait_ge(cA.sem, a_bc)
    so = 0 if sb_off is None else sb_off
    for dc in range(8):
        cV.tick(V.tensor_mul(ysq[:, 0:w], ysb[:, dc, 0:w], bcln[:, 0, 0:w]))
        V.wait_ge(cV.sem, cV.n)
        v_ap = cV.tick(V.tensor_add(outsb[:, dc, so:so + w], ysq[:, 0:w], bcln[:, 1, 0:w]))
        if out_dram is not None:
            Y.wait_ge(cV.sem, v_ap)
            cY.tick(Y.dma_start(out=out_dram[128 * dc:128 * (dc + 1), off:off + w],
                                in_=outsb[:, dc, so:so + w]))
    fence([T, A, V], [cV, cY])



def mk_env(nc, stack):
    G, T, V, A, Y = nc.gpsimd, nc.tensor, nc.vector, nc.scalar, nc.sync
    sb = lambda name, shape, dt=BF16: stack.enter_context(nc.sbuf_tensor(name, shape, dt))
    ps = lambda name, shape, dt=F32: stack.enter_context(nc.psum_tensor(name, shape, dt))
    env = {}
    env["nc"] = nc
    env["eng"] = (G, T, V, A, Y)
    PB0 = ps("PB0", [128, 1024], BF16)
    PB1 = ps("PB1", [128, 1024], BF16)
    PF = [ps(f"PF{i}", [128, 512], F32) for i in range(6)]
    env["PB"] = (PB0, PB1)
    env["PF"] = PF
    cG = Ctr(nc, stack, "cG", 16)
    cY = Ctr(nc, stack, "cY", 16)
    cP = Ctr(nc, stack, "cP")
    cT = Ctr(nc, stack, "cT")
    cV = Ctr(nc, stack, "cV")
    cA = Ctr(nc, stack, "cA")
    env["ctr"] = (cG, cY, cP, cT, cV, cA)
    env["ALL"] = [cG, cY, cP, cT, cV, cA]
    env["ENG"] = [G, T, V, A, Y]
    for n in ("icol", "iot"):
        env[n] = sb(n, [128, 128], F32)
    env["irow"] = sb("irow", [128, 1], F32)
    env["irow64"] = sb("irow64", [128, 1], F32)
    env["id128"] = sb("id128", [128, 128])
    env["id64d"] = sb("id64d", [128, 64])
    env["ones1"] = sb("ones1", [1, 128])
    env["onescol_f"] = sb("onescol_f", [128, 1], F32)
    env["onescol_b"] = sb("onescol_b", [128, 1])
    env["ysq"] = sb("ysq", [128, 512], F32)
    env["st_sb"] = sb("st_sb", [1, 1024], F32)
    env["mu"] = sb("mu", [1, 512], F32)
    env["m2"] = sb("m2", [1, 512], F32)
    env["var"] = sb("var", [1, 512], F32)
    env["rstd"] = sb("rstd", [1, 512], F32)
    env["nmur"] = sb("nmur", [1, 512], F32)
    env["rstd_bf"] = sb("rstd_bf", [1, 512])
    env["nmur_bf"] = sb("nmur_bf", [1, 512])
    env["bcln"] = sb("bcln", [128, 2, 512])
    env["stream"] = sb("stream", [128, NCHUNK, TOK])
    G, T, V, A, Y = env["eng"]
    cG, cY, cP, cT, cV, cA = env["ctr"]
    cP.tick(G.iota(env["iot"][:], pattern=[[-1, 128]], base=0, channel_multiplier=1,
                   allow_small_or_imprecise_dtypes=True))
    cP.tick(G.iota(env["icol"][:], pattern=[[1, 128]], base=0, channel_multiplier=0,
                   allow_small_or_imprecise_dtypes=True))
    cP.tick(G.iota(env["irow"][:], pattern=[[0, 1]], base=0, channel_multiplier=1,
                   allow_small_or_imprecise_dtypes=True))
    cP.tick(G.memset(env["ones1"][:], 1.0))
    cP.tick(G.memset(env["onescol_f"][:], 1.0))
    cP.tick(G.memset(env["onescol_b"][:], 1.0))
    cP.wait(V)
    cV.tick(V.tensor_scalar(env["id128"][:], env["icol"][:], env["irow"][:, 0:1], None, ALU.is_equal))
    cV.tick(V.tensor_single_scalar(env["irow64"][:], env["irow"][:], 64.0, ALU.is_ge))
    cV.wait(V)
    cV.tick(V.scalar_tensor_tensor(env["irow64"][:], env["irow64"][:], -64.0, env["irow"][:], ALU.mult, ALU.add))
    cV.wait(V)
    cV.tick(V.tensor_scalar(env["id64d"][:], env["icol"][:, 0:64], env["irow64"][:, 0:1], None, ALU.is_equal))
    fence(env["ENG"], env["ALL"])
    return env



def emit_A(nc, stack, env):
    G, T, V, A, Y = env["eng"]
    x0T = nc.dram_tensor("x0T", [D, TOKH], F32, kind="ExternalInput").ap()
    validT = nc.dram_tensor("validT", [128, 17], F32, kind="ExternalInput").ap()
    Wq = nc.dram_tensor("Wq", [D, D], F32, kind="ExternalInput").ap()
    Wk = nc.dram_tensor("Wk", [D, D], F32, kind="ExternalInput").ap()
    Wv = nc.dram_tensor("Wv", [D, D], F32, kind="ExternalInput").ap()
    Wo = nc.dram_tensor("Wo", [D, D], F32, kind="ExternalInput").ap()

    sb = lambda name, shape, dt=BF16: stack.enter_context(nc.sbuf_tensor(name, shape, dt))

    x0b = sb("x0b", [128, NCHUNK, TOKH])
    vld = sb("vld", [128, 17], F32)
    wraw = sb("wraw", [128, D])
    wT = {w: sb(f"w{w}T", [128, 8, 128]) for w in "qkv"}
    qT = sb("qT", [128, TOK])
    kT = sb("kT", [128, TOKH])
    vT = sb("vT", [128, TOKH])
    vt = sb("vt", [128, 2, 17, 65])
    sq = sb("sq", [128, TOKH])
    pt = sb("pt", [128, 2, 256])
    tmpm = sb("tmpm", [128, 2, 128], F32)
    maskAB = sb("maskAB", [128, 256])
    nmax = sb("nmax", [1, 32], F32)
    bnd = sb("bnd", [1, 8], F32)
    bndbf = sb("bndbf", [1, 1])
    boundc = sb("boundc", [128, 2], F32)
    lr = sb("lr", [1, 256], F32)
    rbf = sb("rbf", [1, 256])
    bc_sb = sb("bc_sb", [128, 256], F32)
    attnT = sb("attnT", [128, NPAIR, TOK])
    woT = sb("woT", [128, 8, 8, 128])

    cG, cY, cP, cT, cV, cA = env["ctr"]
    PB0, PB1 = env["PB"]
    PF = env["PF"]
    ALL = env["ALL"]
    ENG = env["ENG"]
    id128, id64d = env["id128"], env["id64d"]
    ones1, onescol_f, onescol_b = env["ones1"], env["onescol_f"], env["onescol_b"]
    iot = env["iot"]
    ysb = env["ysb"] = sb("ysbA", [128, NCHUNK, 512], F32)
    ysq = env["ysq"]
    stream = env["stream"]

    MDL = dict(max_dma_last_dim=8192)

    for c in range(NCHUNK):
        cG.tick(G.dma_start(out=x0b[:, c, :], in_=x0T[128 * c:128 * (c + 1), :], **MDL))
    cG.tick(G.dma_start(out=vld[:], in_=validT[:, :]))
    cP.wait(V)
    cV.tick(V.tensor_single_scalar(tmpm[:, 0, :], iot[:], 65.0, ALU.is_ge))
    cV.tick(V.tensor_single_scalar(tmpm[:, 1, :], iot[:], 128.0, ALU.is_le))
    cV.wait(V)
    cV.tick(V.tensor_mul(maskAB[:, 0:128], tmpm[:, 0, :], tmpm[:, 1, :]))
    cV.tick(V.tensor_single_scalar(tmpm[:, 0, :], iot[:], -63.0, ALU.is_ge))
    cV.tick(V.tensor_single_scalar(tmpm[:, 1, :], iot[:], 0.0, ALU.is_le))
    cV.wait(V)
    cV.tick(V.tensor_mul(maskAB[:, 128:256], tmpm[:, 0, :], tmpm[:, 1, :]))
    fence(ENG, ALL)

    kv_groups = [(0, 512), (512, 512), (1024, 512), (1536, 512), (2048, 128)]
    q_groups = [(0, 512), (512, 512), (1024, 512), (1536, 512)]

    for p in range(NPAIR):
        for wname, wdram in (("q", Wq), ("k", Wk), ("v", Wv)):
            fence([G], [cT])
            g_ld = cG.tick(G.dma_start(out=wraw[:], in_=wdram[128 * p:128 * (p + 1), :], **MDL))
            T.wait_ge(cG.sem, g_ld)
            for half in range(2):
                pb = (PB0, PB1)[half]
                fence([T], [cV])
                for k4 in range(4):
                    k = half * 4 + k4
                    ins = T.transpose(pb[:, 128 * k4:128 * (k4 + 1)],
                                      wraw[:, 128 * k:128 * (k + 1)], id128[:])
                cT.tick(ins)
                cT.wait(V)
                cV.tick(V.tensor_copy(wT[wname][:, 4 * half:4 * half + 4, :],
                                      pb[:, 0:512].rearrange("p (a b) -> p a b", b=128)))
        fence([T], [cV, cG])
        for wname, dst, groups, coloff in (("q", qT, q_groups, 128),
                                           ("k", kT, kv_groups, 0),
                                           ("v", vT, kv_groups, 0)):
            for gi, (off, gw) in enumerate(groups):
                pf = PF[gi % 2]
                fence([T], [cA])
                for k in range(8):
                    ins = T.matmul(pf[:, 0:gw], wT[wname][:, k, :],
                                   x0b[:, k, coloff + off:coloff + off + gw],
                                   start=(k == 0), stop=(k == 7))
                cT.tick(ins)
                cT.wait(A)
                cA.tick(A.activation(dst[:, off:off + gw], pf[:, 0:gw], AF.Copy))
        fence([T, V], [cA])
        for h2 in range(2):
            for c0 in range(0, 17, 4):
                cnt = min(4, 17 - c0)
                pb = (PB0, PB1)[(c0 // 4) % 2]
                fence([T], [cV])
                for cc in range(cnt):
                    ins = T.transpose(pb[:, 64 * cc:64 * (cc + 1)],
                                      vT[64 * h2:64 * (h2 + 1), 128 * (c0 + cc):128 * (c0 + cc + 1)],
                                      id64d[64 * h2:64 * (h2 + 1), 0:64])
                cT.tick(ins)
                cT.wait(V)
                cV.tick(V.tensor_copy(vt[:, h2, c0:c0 + cnt, 0:64],
                                      pb[:, 0:64 * cnt].rearrange("p (a b) -> p a b", b=64)))
            cV.tick(V.tensor_copy(vt[:, h2, :, 64], vld[:, 0:17]))
        fence([A], [cT, cV])
        for src, groups, gbase in ((qT, q_groups, 0), (kT, kv_groups, 16)):
            a_sq = cA.tick(A.activation(sq[:, 0:src.shape[-1]], src[:], AF.Square))
            T.wait_ge(cA.sem, a_sq)
            for h2 in range(2):
                for gi, (off, gw) in enumerate(groups):
                    cV.wait(T)
                    ins = T.matmul(PF[5][0:1, 0:gw],
                                   onescol_b[64 * h2:64 * (h2 + 1), 0:1],
                                   sq[64 * h2:64 * (h2 + 1), off:off + gw],
                                   start=True, stop=True)
                    t_mm = cT.tick(ins)
                    V.wait_ge(cT.sem, t_mm)
                    cV.tick(V.reduce_max(nmax[0:1, gbase + 8 * h2 + gi:gbase + 8 * h2 + gi + 1],
                                         PF[5][0:1, 0:gw], axis=AX.X))
            fence([A], [cT])
        for h2 in range(2):
            cV.wait(V)
            cV.tick(V.reduce_max(bnd[0:1, 0:1], nmax[0:1, 8 * h2:8 * h2 + 4], axis=AX.X))
            cV.tick(V.reduce_max(bnd[0:1, 1:2], nmax[0:1, 16 + 8 * h2:16 + 8 * h2 + 5], axis=AX.X))
            cV.wait(V)
            cV.tick(V.tensor_mul(bnd[0:1, 2:3], bnd[0:1, 0:1], bnd[0:1, 1:2]))
            cV.wait(A)
            cA.tick(A.activation(bnd[0:1, 3:4], bnd[0:1, 2:3], AF.Sqrt))
            cA.wait(V)
            cV.tick(V.tensor_single_scalar(bnd[0:1, 4:5], bnd[0:1, 3:4], -0.125, ALU.mult))
            cV.wait(V)
            cV.tick(V.tensor_copy(bndbf[0:1, 0:1], bnd[0:1, 4:5]))
            cV.wait(T)
            cA.wait(T)
            t_mm = cT.tick(T.matmul(PF[5][:, 0:1], ones1[0:1, :], bndbf[0:1, 0:1],
                                    start=True, stop=True))
            A.wait_ge(cT.sem, t_mm)
            cA.tick(A.activation(boundc[:, h2:h2 + 1], PF[5][:, 0:1], AF.Copy))
        fence(ENG, [cT, cV, cA])
        for i in range(16):
            for h2 in range(2):
                spf = PF[2 + h2]
                for ci, cc in enumerate((i, i + 1)):
                    ins = T.matmul(spf[:, 128 * ci:128 * (ci + 1)],
                                   kT[64 * h2:64 * (h2 + 1), 128 * cc:128 * (cc + 1)],
                                   qT[64 * h2:64 * (h2 + 1), 128 * i:128 * (i + 1)],
                                   start=True, stop=True)
                t_s = cT.tick(ins)
                A.wait_ge(cT.sem, t_s)
                cA.tick(A.activation(pt[:, h2, :], spf[:, 0:256], AF.Exp,
                                     bias=boundc[:, h2:h2 + 1], scale=0.125))
            cA.wait(V)
            for h2 in range(2):
                cV.tick(V.tensor_mul(pt[:, h2, :], pt[:, h2, :], maskAB[:]))
            cV.wait(T)
            for h2 in range(2):
                T.matmul(PF[4][64 * h2:64 * (h2 + 1), 0:128], vt[:, h2, i, 0:64],
                         pt[:, h2, 0:128], start=True, stop=False,
                         tile_position=(0, 64 * h2))
                ins = T.matmul(PF[4][64 * h2:64 * (h2 + 1), 0:128], vt[:, h2, i + 1, 0:64],
                               pt[:, h2, 128:256], start=False, stop=True,
                               tile_position=(0, 64 * h2))
                cT.tick(ins)
                T.matmul(PF[5][0:1, 128 * h2:128 * (h2 + 1)], vt[:, h2, i, 64:65],
                         pt[:, h2, 0:128], start=True, stop=False)
                ins = T.matmul(PF[5][0:1, 128 * h2:128 * (h2 + 1)], vt[:, h2, i + 1, 64:65],
                               pt[:, h2, 128:256], start=False, stop=True)
                t_pv = cT.tick(ins)
            V.wait_ge(cT.sem, t_pv)
            cV.tick(V.tensor_copy(lr[0:1, 0:256], PF[5][0:1, 0:256]))
            cV.wait(A)
            cA.tick(A.activation(lr[0:1, :], lr[0:1, :], AF.Ln))
            cA.wait(A)
            a_r = cA.tick(A.activation(rbf[0:1, :], lr[0:1, :], AF.Exp, scale=-1.0))
            T.wait_ge(cA.sem, a_r)
            for h2 in range(2):
                ins = T.matmul(PF[0][:, 128 * h2:128 * (h2 + 1)], ones1[0:1, :],
                               rbf[0:1, 128 * h2:128 * (h2 + 1)], start=True, stop=True)
            t_bc = cT.tick(ins)
            A.wait_ge(cT.sem, t_bc)
            a_bc = cA.tick(A.activation(bc_sb[:, 0:256], PF[0][:, 0:256], AF.Copy))
            V.wait_ge(cA.sem, a_bc)
            for h2 in range(2):
                cV.tick(V.tensor_mul(attnT[64 * h2:64 * (h2 + 1), p, 128 * i:128 * (i + 1)],
                                     PF[4][64 * h2:64 * (h2 + 1), 0:128],
                                     bc_sb[64 * h2:64 * (h2 + 1), 128 * h2:128 * (h2 + 1)]))
            fence([T], [cA, cV])
        fence(ENG, ALL)

    for dc in range(8):
        fence([G], [cT])
        g_ld = cG.tick(G.dma_start(out=wraw[:], in_=Wo[128 * dc:128 * (dc + 1), :], **MDL))
        T.wait_ge(cG.sem, g_ld)
        for half in range(2):
            pb = (PB0, PB1)[half]
            fence([T], [cV])
            for k4 in range(4):
                k = half * 4 + k4
                ins = T.transpose(pb[:, 128 * k4:128 * (k4 + 1)],
                                  wraw[:, 128 * k:128 * (k + 1)], id128[:])
            cT.tick(ins)
            cT.wait(V)
            cV.tick(V.tensor_copy(woT[:, dc, 4 * half:4 * half + 4, :],
                                  pb[:, 0:512].rearrange("p (a b) -> p a b", b=128)))
    fence(ENG, ALL)

    for g in range(4):
        off = 512 * g
        for dc in range(8):
            pf = PF[dc % 2]
            fence([T], [cV])
            for k in range(8):
                ins = T.matmul(pf[:], woT[:, dc, k, :], attnT[:, k, off:off + 512],
                               start=(k == 0), stop=(k == 7))
            t_mm = cT.tick(ins)
            V.wait_ge(cT.sem, t_mm)
            cV.tick(V.tensor_add(ysb[:, dc, :], pf[:], x0b[:, dc, 128 + off:128 + off + 512]))
        emit_ln_group(env, ysb, ysq, stream, None, off, sb_off=off)
    fence(ENG, ALL)



def emit_moe_experts(nc, env, P, res_scale, ln_out_sb, ln_dram, ln_sb_off):
    G, T, V, A, Y = env["eng"]
    cG, cY, cP, cT, cV, cA = env["ctr"]
    PF = env["PF"]
    ALL, ENG = env["ALL"], env["ENG"]
    ones1 = env["ones1"]
    ysq = env["ysq"]
    MDL = dict(max_dma_last_dim=8192)
    W1e, W2e, b1s = P["W1e"], P["W2e"], P["b1s"]
    b2_sb, m8, mrows = P["b2_sb"], P["m8"], P["mrows"]
    w1t, w2t, hhb, bcm4, b1c = P["w1t"], P["w2t"], P["hhb"], P["bcm4"], P["b1c"]
    out32, ysrc, act, HC = P["out32"], P["ysrc"], P["act"], P["HC"]
    NBLK = HC // 8

    fence(ENG, ALL)
    bank_free = {}
    for g in range(4):
        for dc in range(8):
            bi = (g * 8 + dc) % 2
            pf = PF[bi]
            if bi in bank_free:
                cV.wait(T, bank_free[bi][1]) if False else T.wait_ge(cV.sem, bank_free[bi])
            t_b = cT.tick(T.matmul(pf[:], b2_sb[:, dc, :],
                                   m8[:, 512 * g:512 * (g + 1)], start=True, stop=True))
            V.wait_ge(cT.sem, t_b)
            bank_free[bi] = cV.tick(V.tensor_copy(out32[:, dc, 512 * g:512 * (g + 1)], pf[:]))
    fence(ENG, ALL)

    w1_free = {}
    w2_free = {}
    hbank_free = {}
    obank_free = {}
    bc_free = 0
    b1_free = 0
    hh_free = 0
    mask_done = 0
    for e in range(8):
        if b1_free:
            A_done = b1_free
            G.wait_ge(cA.sem, A_done)
        g_b1 = cG.tick(G.dma_start(out=b1c[:], in_=b1s[e, :, :]))
        A.wait_ge(cG.sem, g_b1)
        bcbank = {}
        for g in range(4):
            bi = g % 2
            pf = PF[bi]
            if bi in bcbank:
                T.wait_ge(cA.sem, bcbank[bi])
            elif bc_free:
                T.wait_ge(cA.sem, bc_free)
            t_bc = cT.tick(T.matmul(pf[:], ones1[0:1, :],
                                    mrows[0:1, e, 512 * g:512 * (g + 1)],
                                    start=True, stop=True))
            A.wait_ge(cT.sem, t_bc)
            if e > 0:
                A.wait_ge(cV.sem, mask_done)
            bcbank[bi] = cA.tick(A.activation(bcm4[:, g, :], pf[:], AF.Copy))
        bc_free = cA.n
        for blk in range(NBLK):
            for mcl in range(8):
                mc = blk * 8 + mcl
                buf = mc % 2
                if buf in w1_free:
                    G.wait_ge(cT.sem, w1_free[buf])
                g_w = cG.tick(G.dma_start(
                    out=w1t[:, buf, :, :],
                    in_=W1e[e][:, 128 * mc:128 * (mc + 1)].rearrange("(kc p) m -> p kc m", p=128),
                    **MDL))
                T.wait_ge(cG.sem, g_w)
                if e > 0 or blk > 0:
                    A.wait_ge(cT.sem, hh_free)
                for g in range(4):
                    bi = 2 + (mc * 4 + g) % 2
                    pf = PF[bi]
                    if bi in hbank_free:
                        T.wait_ge(cA.sem, hbank_free[bi])
                    for kc in range(8):
                        ins = T.matmul(pf[:], w1t[:, buf, kc, :],
                                       ysrc[:, kc, 512 * g:512 * (g + 1)],
                                       start=(kc == 0), stop=(kc == 7))
                    t_h = cT.tick(ins)
                    A.wait_ge(cT.sem, t_h)
                    hbank_free[bi] = cA.tick(A.activation(
                        hhb[:, mcl, 512 * g:512 * (g + 1)], pf[:], act,
                        bias=b1c[:, mc:mc + 1]))
                w1_free[buf] = cT.n
            V.wait_ge(cA.sem, cA.n)
            for mcl in range(8):
                for g in range(4):
                    cV.tick(V.tensor_mul(hhb[:, mcl, 512 * g:512 * (g + 1)],
                                         hhb[:, mcl, 512 * g:512 * (g + 1)],
                                         bcm4[:, g, :]))
            mask_done = cV.n
            T.wait_ge(cV.sem, mask_done)
            for dc in range(8):
                buf = dc % 2
                if buf in w2_free:
                    G.wait_ge(cT.sem, w2_free[buf])
                g_w2 = cG.tick(G.dma_start(
                    out=w2t[:, buf, :, :],
                    in_=W2e[e][1024 * blk:1024 * (blk + 1), 128 * dc:128 * (dc + 1)].rearrange(
                        "(kc p) m -> p kc m", p=128),
                    **MDL))
                T.wait_ge(cG.sem, g_w2)
                for g in range(4):
                    bi = 4 + (dc * 4 + g) % 2
                    pf = PF[bi]
                    if bi in obank_free:
                        T.wait_ge(cV.sem, obank_free[bi])
                    for kc in range(8):
                        ins = T.matmul(pf[:], w2t[:, buf, kc, :],
                                       hhb[:, kc, 512 * g:512 * (g + 1)],
                                       start=(kc == 0), stop=(kc == 7))
                    t_o = cT.tick(ins)
                    V.wait_ge(cT.sem, t_o)
                    cV.wait(V)
                    obank_free[bi] = cV.tick(V.tensor_tensor(
                        out32[:, dc, 512 * g:512 * (g + 1)],
                        out32[:, dc, 512 * g:512 * (g + 1)], pf[:], ALU.add))
                w2_free[buf] = cT.n
            hh_free = cT.n
        b1_free = cA.n
    fence(ENG, ALL)
    for g in range(4):
        off = 512 * g
        for dc in range(8):
            cV.wait(V)
            cV.tick(V.scalar_tensor_tensor(out32[:, dc, off:off + 512],
                                           ysrc[:, dc, off:off + 512], float(res_scale),
                                           out32[:, dc, off:off + 512],
                                           ALU.mult, ALU.add))
        emit_ln_group(env, out32[:, :, off:off + 512], ysq, ln_out_sb, ln_dram, off,
                      sb_off=(off if ln_sb_off else None))
    fence(ENG, ALL)



def emit_B(nc, stack, env, debug=False):
    G, T, V, A, Y = env["eng"]
    w_red = nc.dram_tensor("w_red", [D, 16], F32, kind="ExternalInput").ap()
    wg = nc.dram_tensor("wg", [8, 16], F32, kind="ExternalInput").ap()
    W1 = nc.dram_tensor("W1", [8, D, 2048], F32, kind="ExternalInput").ap()
    b1s = nc.dram_tensor("b1s", [8, 128, 16], F32, kind="ExternalInput").ap()
    W2 = nc.dram_tensor("W2", [8, 2048, D], F32, kind="ExternalInput").ap()
    b2 = nc.dram_tensor("b2", [8, D], F32, kind="ExternalInput").ap()
    y2 = nc.dram_tensor("y2", [D, TOK], BF16, kind="ExternalOutput").ap()
    if debug:
        dbg_m8 = nc.dram_tensor("dbg_m8", [8, TOK], F32, kind="ExternalOutput").ap()
        dbg_mrows = nc.dram_tensor("dbg_mrows", [8, TOK], F32, kind="ExternalOutput").ap()
        dbg_out32 = nc.dram_tensor("dbg_out32", [D, 1024], F32, kind="ExternalOutput").ap()
        dbg_lgt = nc.dram_tensor("dbg_lgt", [128, 4, 8], F32, kind="ExternalOutput").ap()
        dbg_r = nc.dram_tensor("dbg_r", [16, 520], F32, kind="ExternalOutput").ap()

    sb = lambda name, shape, dt=BF16: stack.enter_context(nc.sbuf_tensor(name, shape, dt))

    ysrc = env["stream"]
    wred_sb = sb("wred_sb", [128, 8, 16])
    wg_sb = sb("wg_sb", [8, 16], F32)
    wgsq = sb("wgsq", [8, 16], F32)
    wgn = sb("wgn", [8, 2], F32)
    wgnb = sb("wgnb", [8, 16])
    wgnT = sb("wgnT", [16, 8])
    xr_bf = sb("xr_bf", [16, 512])
    lg_sb = sb("lg_sb", [8, 512])
    lgt = sb("lgt", [128, 4, 8], F32)
    nm1 = sb("nm1", [128, 1], F32)
    eq8 = sb("eq8", [128, 8], F32)
    e8 = sb("e8", [128, 8], F32)
    ssum = sb("ssum", [128, 1], F32)
    rs = sb("rs", [128, 1], F32)
    mtok = sb("mtok", [128, 8])
    m8 = sb("m8", [8, TOK])
    mrows = sb("mrows", [1, 8, TOK])
    b1c = sb("b1c", [128, 16], F32)
    b2_sb = sb("b2_sb", [8, 8, 128])
    w1t = sb("w1t", [128, 2, 8, 128])
    w2t = sb("w2t", [128, 2, 8, 128])
    hhb = sb("hhb", [128, 8, 2048])
    bcm4 = sb("bcm4", [128, 4, 512])
    out32 = sb("out32", [128, 8, 2048], F32)

    cG, cY, cP, cT, cV, cA = env["ctr"]
    PB0, PB1 = env["PB"]
    PF = env["PF"]
    ALL = env["ALL"]
    ENG = env["ENG"]
    id128 = env["id128"]
    ones1, onescol_f = env["ones1"], env["onescol_f"]
    ysq = env["ysq"]
    MDL = dict(max_dma_last_dim=8192)

    cG.tick(G.dma_start(out=wred_sb[:],
                        in_=w_red.rearrange("(kc p) g -> p kc g", p=128)))
    cG.tick(G.dma_start(out=wg_sb[:], in_=wg[:, :]))
    cG.tick(G.dma_start(out=b2_sb[:],
                        in_=b2.rearrange("e (dc m) -> e dc m", m=128)))
    fence(ENG, ALL)

    cA.tick(A.activation(wgsq[:], wg_sb[:], AF.Square))
    cA.wait(V)
    cV.tick(V.reduce_sum(wgn[:, 0:1], wgsq[:], axis=AX.X))
    cV.wait(A)
    cA.tick(A.activation(wgn[:, 1:2], wgn[:, 0:1], AF.Sqrt))
    cA.wait(V)
    cV.tick(V.tensor_scalar_max(wgn[:, 1:2], wgn[:, 1:2], 1e-4))
    cV.wait(V)
    cV.tick(V.reciprocal(wgn[:, 0:1], wgn[:, 1:2]))
    cV.wait(V)
    cV.tick(V.tensor_scalar(wgnb[:], wg_sb[:], wgn[:, 0:1], None, ALU.mult))
    cV.wait(T)
    cT.tick(T.transpose(PB0[0:16, 0:8], wgnb[:], id128[0:8, 0:8]))
    cT.wait(V)
    cV.tick(V.tensor_copy(wgnT[:], PB0[0:16, 0:8]))
    fence(ENG, ALL)

    if debug:
        fence(ENG, ALL)
        cG.tick(G.dma_start(out=dbg_r[0:8, 0:2], in_=wgn[:]))
        cG.tick(G.dma_start(out=dbg_r[0:8, 2:18], in_=wgnb[:]))
        cG.tick(G.dma_start(out=dbg_r[0:16, 18:26], in_=wgnT[:]))
        cG.tick(G.dma_start(out=dbg_r[0:8, 26:42], in_=wg_sb[:]))
    for g in range(4):
        off = 512 * g
        fence([T], [cA, cV])
        for kc in range(8):
            ins = T.matmul(PF[0][0:16, :], wred_sb[:, kc, :], ysrc[:, kc, off:off + 512],
                           start=(kc == 0), stop=(kc == 7))
        t_xr = cT.tick(ins)
        A.wait_ge(cT.sem, t_xr)
        a_xr = cA.tick(A.activation(xr_bf[:], PF[0][0:16, :], AF.Copy))
        T.wait_ge(cA.sem, a_xr)
        t_lg = cT.tick(T.matmul(PF[1][0:8, :], wgnT[:], xr_bf[:], start=True, stop=True))
        V.wait_ge(cT.sem, t_lg)
        cV.tick(V.tensor_copy(lg_sb[:], PF[1][0:8, :]))
        if debug and g == 0:
            v_d = cV.n
            G.wait_ge(cV.sem, v_d)
            cG.tick(G.dma_start(out=dbg_r[0:16, 42:42 + 128], in_=xr_bf[:, 0:128]))
            cG.tick(G.dma_start(out=dbg_r[0:8, 256:256 + 256], in_=lg_sb[:, 0:256]))
        cV.wait(T)
        for t4 in range(4):
            ins = T.transpose(PB0[0:128, 8 * t4:8 * (t4 + 1)],
                              lg_sb[:, 128 * t4:128 * (t4 + 1)], id128[0:8, 0:8])
        t_tr = cT.tick(ins)
        V.wait_ge(cT.sem, t_tr)
        cV.tick(V.tensor_copy(lgt[:], PB0[0:128, 0:32].rearrange("p (a b) -> p a b", b=8)))
        for t4 in range(4):
            cV.wait(V)
            cV.tick(V.reduce_max(nm1[:], lgt[:, t4, :], axis=AX.X, negate=True))
            cV.wait(V)
            cV.tick(V.tensor_scalar(eq8[:], lgt[:, t4, :], nm1[:, 0:1], 0.0,
                                    ALU.add, ALU.is_equal))
            cV.wait(A)
            cA.tick(A.activation(e8[:], lgt[:, t4, :], AF.Exp, bias=nm1[:, 0:1],
                                 accum_out=ssum[:, 0:1]))
            cA.wait(V)
            cV.tick(V.reciprocal(rs[:], ssum[:]))
            cV.wait(V)
            cV.tick(V.tensor_scalar(mtok[:], eq8[:], rs[:, 0:1], None, ALU.mult))
            cV.wait(T)
            t_f = cT.tick(T.transpose(PB1[0:8, 128 * t4:128 * (t4 + 1)], mtok[:], id128[:]))
            for e in range(8):
                ins = T.transpose(PB0[0:1, 128 * e:128 * (e + 1)],
                                  mtok[:, e:e + 1], id128[:])
            t_r = cT.tick(ins)
            V.wait_ge(cT.sem, t_r)
            cV.tick(V.tensor_copy(mrows[0:1, :, off + 128 * t4:off + 128 * (t4 + 1)],
                                  PB0[0:1, 0:1024].rearrange("p (a b) -> p a b", b=128)))
        V.wait_ge(cT.sem, cT.n)
        cV.tick(V.tensor_copy(m8[:, off:off + 512], PB1[0:8, 0:512]))
        fence([T], [cV])
        if debug and g == 0:
            cY.tick(Y.dma_start(out=dbg_lgt[:, :, :], in_=lgt[:]))
    fence(ENG, ALL)
    if debug:
        cG.tick(G.dma_start(out=dbg_m8[:, :], in_=m8[:]))
        for e in range(8):
            cG.tick(G.dma_start(out=dbg_mrows[e:e + 1, :], in_=mrows[0:1, e, :]))

    P = dict(W1e=[W1[e] for e in range(8)], W2e=[W2[e] for e in range(8)],
             b1s=b1s, b2_sb=b2_sb, m8=m8, mrows=mrows,
             w1t=w1t, w2t=w2t, hhb=hhb, bcm4=bcm4, b1c=b1c, out32=out32,
             ysrc=ysrc, act=AF.Gelu, HC=16)
    emit_moe_experts(nc, env, P, 1.0, env["stream"], y2, True)


def _stage_B(inputs, y1_shards):
    maps = []
    b1s = np.ascontiguousarray(
        inputs["L0_ffn_b1"].reshape(8, 16, 128).transpose(0, 2, 1))
    for c in range(8):
        maps.append({
            "y1": None if y1_shards[c] is None else np.asarray(y1_shards[c]),
            "w_red": inputs["L0_ffn_w_red"],
            "wg": inputs["L0_ffn_wg"],
            "W1": inputs["L0_ffn_W1"],
            "b1s": b1s,
            "W2": inputs["L0_ffn_W2"],
            "b2": inputs["L0_ffn_b2"],
        })
    return maps






def emit_C(nc, stack, env):
    G, T, V, A, Y = env["eng"]
    S = 4096
    kvT = nc.dram_tensor("kvT", [D, S], BF16, kind="ExternalInput").ap()
    qinT = nc.dram_tensor("qinT", [D, TOK], BF16, kind="ExternalInput").ap()
    resT = nc.dram_tensor("resT", [D, TOK], BF16, kind="ExternalInput").ap()
    h32 = nc.dram_tensor("h32", [128, 1], F32, kind="ExternalInput").ap()
    Wqkv = nc.dram_tensor("Wqkv", [NH, D, 192], F32, kind="ExternalInput").ap()
    Wout = nc.dram_tensor("Wout", [D, D], F32, kind="ExternalInput").ap()
    o_dram = nc.dram_tensor("o_dram", [NH, TOK, HD], BF16, kind="Internal").ap()

    sb = lambda name, shape, dt=BF16: stack.enter_context(nc.sbuf_tensor(name, shape, dt))

    kv_sb = sb("kv_sb", [128, 8, S])
    qin_sb = sb("qin_sb", [128, 8, TOK])
    wqkv_sb = sb("wqkv_sb", [128, 8, 2, 192])
    KT = sb("KT", [128, S])
    vstage = sb("vstage", [128, S])
    vt = sb("vt", [128, 2, 32, 65])
    sq_v = vstage
    QT = sb("QT", [128, TOK])
    ptile = sb("ptile", [128, 2, 512])
    h32c = sb("h32c", [128, 1], F32)
    m1t = sb("m1t", [128, 128])
    m2t = sb("m2t", [128, 128])
    nmax = sb("nmax", [1, 48], F32)
    bnd = sb("bnd", [1, 8], F32)
    bndbf = sb("bndbf", [1, 1])
    boundc = sb("boundc", [128, 2], F32)
    lr = sb("lr", [1, 2, 512], F32)
    r_bf = sb("r_bf", [1, 2, 512])
    rcol = sb("rcol", [128, 2, 4], F32)
    oevac = vstage[:, 1024:1536]
    osb = vstage[:, 1536:2048].rearrange("p (a b c) -> p a b c", b=4, c=64)
    wout_c = sb("wout_c", [128, 2, D])
    wsum_sb = sb("wsum_sb", [1, D], F32)
    wsum_bf = sb("wsum_bf", [1, D])
    wsumT = sb("wsumT", [128, 8], F32)
    attd = sb("attd", [128, 2, 512])
    resc = sb("resc", [128, 2, 512])

    cG, cY, cP, cT, cV, cA = env["ctr"]
    PB0, PB1 = env["PB"]
    PF = env["PF"]
    ALL = env["ALL"]
    ENG = env["ENG"]
    id128, id64d = env["id128"], env["id64d"]
    ones1, onescol_f, onescol_b = env["ones1"], env["onescol_f"], env["onescol_b"]
    ysb = env["ysb"] = kv_sb[:, 0:2, :].rearrange("p a b -> p (a b)").bitcast(F32).rearrange(
        "p (a b) -> p a b", b=512)
    ysq = env["ysq"]
    io2 = env["iot"]

    cY.tick(Y.dma_start(out=h32c[:], in_=h32[:, :]))
    for kc in range(8):
        cY.tick(Y.dma_start(out=kv_sb[:, kc, :], in_=kvT[128 * kc:128 * (kc + 1), :]))
        cY.tick(Y.dma_start(out=qin_sb[:, kc, :], in_=qinT[128 * kc:128 * (kc + 1), :]))
    cP.tick(G.iota(io2[:], pattern=[[64, 4], [1, 32]], base=0, channel_multiplier=-1,
                   allow_small_or_imprecise_dtypes=True))
    cP.tick(G.memset(vt[:, :, :, 64], 1.0))
    cP.wait(V)
    cY.wait(V)
    cV.tick(V.tensor_scalar(m1t[:], io2[:], h32c[:, 0:1], 0.0, ALU.add, ALU.is_ge))
    cV.tick(V.tensor_scalar(m2t[:], io2[:], h32c[:, 0:1], 128.0, ALU.add, ALU.is_ge))
    for kc in range(8):
        buf = kc % 2
        fence([G], [cT])
        g_w = cG.tick(G.dma_start(out=wout_c[:, buf, :],
                                  in_=Wout[128 * kc:128 * (kc + 1), :], max_dma_last_dim=8192))
        T.wait_ge(cG.sem, g_w)
        for half in range(2):
            ins = T.matmul(PF[half][0:1, :], onescol_b[:, 0:1],
                           wout_c[:, buf, 512 * half:512 * (half + 1)],
                           start=(kc == 0), stop=(kc == 7))
        cT.tick(ins)
    cT.wait(V)
    cV.tick(V.tensor_copy(wsum_sb[0:1, 0:512], PF[0][0:1, :]))
    cV.tick(V.tensor_copy(wsum_sb[0:1, 512:1024], PF[1][0:1, :]))
    cV.wait(V)
    cV.tick(V.tensor_copy(wsum_bf[0:1, :], wsum_sb[0:1, :]))
    cV.wait(T)
    for kc in range(8):
        ins = T.transpose(PB0[0:128, 2 * kc:2 * kc + 1], wsum_bf[0:1, 128 * kc:128 * (kc + 1)],
                          id128[0:1, 0:1])
    t_ws = cT.tick(ins)
    A.wait_ge(cT.sem, t_ws)
    cA.tick(A.activation(wsumT[:], PB0[0:128, 0:16].rearrange("p (a b) -> p a b", b=2)[:, :, 0],
                         AF.Copy))
    fence(ENG, ALL)

    for p in range(NPAIR):
        fence([G], [cT])
        for kc in range(8):
            g_w = cG.tick(G.dma_start(
                out=wqkv_sb[:, kc, :, :],
                in_=Wqkv[2 * p:2 * p + 2, 128 * kc:128 * (kc + 1), :].rearrange("h p w -> p h w")))
        T.wait_ge(cG.sem, cG.n)
        kev = {}
        for g in range(8):
            if 0 in kev:
                T.wait_ge(cA.sem, kev[0])
                T.wait_ge(cV.sem, kev[1])
            for kc in range(8):
                for h2 in range(2):
                    T.matmul(PF[0][64 * h2:64 * (h2 + 1), :], wqkv_sb[:, kc, h2, 64:128],
                             kv_sb[:, kc, 512 * g:512 * (g + 1)], start=(kc == 0), stop=(kc == 7),
                             tile_position=(0, 64 * h2), skip_group_check=True)
                    ins = T.matmul(PF[1][64 * h2:64 * (h2 + 1), :], wqkv_sb[:, kc, h2, 128:192],
                                   kv_sb[:, kc, 512 * g:512 * (g + 1)], start=(kc == 0), stop=(kc == 7),
                                   tile_position=(0, 64 * h2), skip_group_check=True)
                cT.tick(ins)
            cT.wait(A)
            cT.wait(V)
            kev[0] = cA.tick(A.activation(KT[:, 512 * g:512 * (g + 1)], PF[0][:], AF.Copy))
            kev[1] = cV.tick(V.tensor_copy(vstage[:, 512 * g:512 * (g + 1)], PF[1][:]))
        for g in range(4):
            fence([T], [cA])
            for kc in range(8):
                for h2 in range(2):
                    ins = T.matmul(PF[0][64 * h2:64 * (h2 + 1), :], wqkv_sb[:, kc, h2, 0:64],
                                   qin_sb[:, kc, 512 * g:512 * (g + 1)], start=(kc == 0), stop=(kc == 7),
                                   tile_position=(0, 64 * h2), skip_group_check=True)
                cT.tick(ins)
            cT.wait(A)
            cA.tick(A.activation(QT[:, 512 * g:512 * (g + 1)], PF[0][:], AF.Copy))
        fence(ENG, [cT, cV, cA])
        for h2 in range(2):
            for c0 in range(0, 32, 4):
                pb = (PB0, PB1)[(c0 // 4) % 2]
                fence([T], [cV])
                for cc in range(4):
                    ins = T.transpose(pb[:, 64 * cc:64 * (cc + 1)],
                                      vstage[64 * h2:64 * (h2 + 1), 128 * (c0 + cc):128 * (c0 + cc + 1)],
                                      id64d[64 * h2:64 * (h2 + 1), 0:64])
                cT.tick(ins)
                cT.wait(V)
                cV.tick(V.tensor_copy(vt[:, h2, c0:c0 + 4, 0:64],
                                      pb[:, 0:256].rearrange("p (a b) -> p a b", b=64)))
        fence([A], [cT, cV])
        for bsrc, width, ng, gbase in ((QT, TOK, 4, 0), (KT, S, 8, 16)):
            a_sq = cA.tick(A.activation(sq_v[:, 0:width], bsrc[:, 0:width], AF.Square))
            T.wait_ge(cA.sem, a_sq)
            for h2 in range(2):
                for gi in range(ng):
                    cV.wait(T)
                    ins = T.matmul(PF[5][0:1, 0:512],
                                   onescol_b[64 * h2:64 * (h2 + 1), 0:1],
                                   sq_v[64 * h2:64 * (h2 + 1), 512 * gi:512 * (gi + 1)],
                                   start=True, stop=True)
                    t_mm = cT.tick(ins)
                    V.wait_ge(cT.sem, t_mm)
                    cV.tick(V.reduce_max(nmax[0:1, gbase + 8 * h2 + gi:gbase + 8 * h2 + gi + 1],
                                         PF[5][0:1, 0:512], axis=AX.X))
            fence([A], [cT])
        for h2 in range(2):
            cV.wait(V)
            cV.tick(V.reduce_max(bnd[0:1, 0:1], nmax[0:1, 8 * h2:8 * h2 + 4], axis=AX.X))
            cV.tick(V.reduce_max(bnd[0:1, 1:2], nmax[0:1, 16 + 8 * h2:16 + 8 * h2 + 8], axis=AX.X))
            cV.wait(V)
            cV.tick(V.tensor_mul(bnd[0:1, 2:3], bnd[0:1, 0:1], bnd[0:1, 1:2]))
            cV.wait(A)
            cA.tick(A.activation(bnd[0:1, 3:4], bnd[0:1, 2:3], AF.Sqrt))
            cA.wait(V)
            cV.tick(V.tensor_single_scalar(bnd[0:1, 4:5], bnd[0:1, 3:4], -0.125, ALU.mult))
            cV.wait(V)
            cV.tick(V.tensor_copy(bndbf[0:1, 0:1], bnd[0:1, 4:5]))
            cV.wait(T)
            cA.wait(T)
            t_mm = cT.tick(T.matmul(PF[5][:, 0:1], ones1[0:1, :], bndbf[0:1, 0:1],
                                    start=True, stop=True))
            A.wait_ge(cT.sem, t_mm)
            cA.tick(A.activation(boundc[:, h2:h2 + 1], PF[5][:, 0:1], AF.Copy))
        fence(ENG, [cT, cV, cA])
        for g4 in range(4):
            nchunks = 8 * g4 + 8
            for h2 in range(2):
                ltgt = PF[5] if h2 == 0 else PF[0]

                def geom(c):
                    qoff = 128 * max(0, c // 2 - 4 * g4)
                    return qoff, 512 - qoff

                def emit_S(c, wait_ae):
                    qoff, aw = geom(c)
                    if wait_ae is not None:
                        T.wait_ge(cA.sem, wait_ae)
                    return cT.tick(T.matmul(PF[2 + c % 2][:, 0:aw],
                                            KT[64 * h2:64 * (h2 + 1), 128 * c:128 * (c + 1)],
                                            QT[64 * h2:64 * (h2 + 1), 512 * g4 + qoff:512 * (g4 + 1)],
                                            start=True, stop=True))

                ae = {}
                ts = {}
                pv = {}
                prev_ae = cA.n if (h2 or g4) else None
                prev_pv = cT.n if (h2 or g4) else None
                ts[0] = emit_S(0, prev_ae)
                if nchunks > 1:
                    ts[1] = emit_S(1, prev_ae)
                for c in range(nchunks):
                    qoff, aw = geom(c)
                    pb = c % 2
                    if c >= 2:
                        A.wait_ge(cT.sem, pv[c - 2])
                    elif prev_pv is not None:
                        A.wait_ge(cT.sem, prev_pv)
                    A.wait_ge(cT.sem, ts[c])
                    ae[c] = cA.tick(A.activation(ptile[:, pb, 0:aw], PF[2 + pb][:, 0:aw],
                                                 AF.Exp, bias=boundc[:, h2:h2 + 1],
                                                 scale=0.125))
                    masked = c >= 8 * g4
                    if masked:
                        V.wait_ge(cA.sem, ae[c])
                        vm = cV.tick(V.tensor_mul(ptile[:, pb, 0:128], ptile[:, pb, 0:128],
                                                  (m1t if c % 2 == 0 else m2t)[:]))
                    if c + 2 < nchunks:
                        ts[c + 2] = emit_S(c + 2, ae[c])
                    if masked:
                        T.wait_ge(cV.sem, vm)
                    else:
                        T.wait_ge(cA.sem, ae[c])
                    T.matmul(PF[4][64 * h2:64 * (h2 + 1), qoff:512], vt[:, h2, c, 0:64],
                             ptile[:, pb, 0:aw], start=(c == 0), stop=(c == nchunks - 1),
                             tile_position=(0, 64 * h2), skip_group_check=True)
                    ins = T.matmul(ltgt[0:1, qoff:512], vt[:, h2, c, 64:65],
                                   ptile[:, pb, 0:aw], start=(c == 0), stop=(c == nchunks - 1),
                                   skip_group_check=True)
                    pv[c] = cT.tick(ins)
            V.wait_ge(cT.sem, cT.n)
            cV.tick(V.tensor_copy(lr[0:1, 0, :], PF[5][0:1, 0:512]))
            cV.tick(V.tensor_copy(lr[0:1, 1, :], PF[0][0:1, 0:512]))
            cV.wait(A)
            cA.tick(A.activation(lr[0:1, :, :], lr[0:1, :, :], AF.Ln))
            cA.wait(A)
            a_r = cA.tick(A.activation(r_bf[0:1, :, :], lr[0:1, :, :], AF.Exp, scale=-1.0))
            A.wait_ge(cA.sem, a_r)
            a_o = cA.tick(A.activation(oevac[:], PF[4][:], AF.Copy))
            T.wait_ge(cA.sem, a_o)
            for h2 in range(2):
                fence([T], [cA])
                for t4 in range(4):
                    ins = T.transpose(PB0[0:128, 2 * t4:2 * t4 + 1],
                                      r_bf[0:1, h2, 128 * t4:128 * (t4 + 1)], id128[0:1, 0:1])
                cT.tick(ins)
                cT.wait(A)
                cA.tick(A.activation(rcol[:, h2, :],
                                     PB0[0:128, 0:8].rearrange("p (a b) -> p a b", b=2)[:, :, 0],
                                     AF.Copy))
            for rnd in range(2):
                h2 = rnd
                fence([T], [cV])
                for t4 in range(4):
                    ins = T.transpose(PB1[:, 64 * t4:64 * (t4 + 1)],
                                      oevac[64 * h2:64 * (h2 + 1), 128 * t4:128 * (t4 + 1)],
                                      id64d[64 * h2:64 * (h2 + 1), 0:64])
                cT.tick(ins)
                cT.wait(V)
                cA.wait(V)
                for t4 in range(4):
                    cV.tick(V.tensor_scalar(osb[:, h2, t4, :], PB1[:, 64 * t4:64 * (t4 + 1)],
                                            rcol[:, h2, t4:t4 + 1], None, ALU.mult))
            v_done = cV.n
            Y.wait_ge(cV.sem, v_done)
            for h2 in range(2):
                cY.tick(Y.dma_start(
                    out=o_dram[2 * p + h2, 512 * g4:512 * (g4 + 1), :].rearrange(
                        "(t4 r) e -> r t4 e", t4=4),
                    in_=osb[:, h2, :, :]))
            fence([T, V], [cA, cV, cY])
        fence(ENG, ALL)

    o_flat = o_dram.rearrange("h q e -> (h q e)")
    for g in range(4):
        off = 512 * g
        vdone = {}
        for dc in range(8):
            buf = dc % 2
            if buf in vdone:
                Y.wait_ge(cV.sem, vdone[buf])
            y_r = cY.tick(Y.dma_start(out=resc[:, buf, :],
                                      in_=resT[128 * dc:128 * (dc + 1), off:off + 512]))
            y_a = cY.tick(Y.dma_start(
                out=attd[:, buf, :],
                in_=o_flat[dc * 262144:(dc + 1) * 262144].rearrange(
                    "(r c) -> r c", r=128)[:, off:off + 512]))
            V.wait_ge(cY.sem, y_a)
            vdone[buf] = cV.tick(V.scalar_tensor_tensor(ysb[:, dc, :], attd[:, buf, :],
                                                        wsumT[:, dc:dc + 1], resc[:, buf, :],
                                                        ALU.mult, ALU.add))
        emit_ln_group(env, ysb, ysq, env["stream"], None, off, sb_off=off)
    fence(ENG, ALL)


def _stage_C(inputs, y2_shards):
    wqkv = inputs["L1_attn_W_qkv"]
    wout = np.ascontiguousarray(inputs["L1_attn_W_out"].reshape(D, D))
    maps = []
    y2f = []
    for b in range(4):
        y2f.append(np.concatenate([np.asarray(y2_shards[2 * b]),
                                   np.asarray(y2_shards[2 * b + 1])], axis=1))
    for c in range(8):
        b, h = c // 2, c % 2
        full = y2f[b]
        qcols = (np.arange(4096).reshape(64, 64)[:, 32 * h:32 * h + 32]).reshape(-1)
        maps.append({
            "kvT": full,
            "qinT": np.ascontiguousarray(full[:, qcols]),
            "resT": np.ascontiguousarray(full[:, 2048 * h:2048 * (h + 1)]),
            "h32": np.full((128, 1), 32.0 * h, np.float32),
            "Wqkv": wqkv,
            "Wout": wout,
        })
    return maps






def emit_D(nc, stack, env):
    G, T, V, A, Y = env["eng"]
    H1 = 4096
    rw1 = nc.dram_tensor("rw1", [D, D], F32, kind="ExternalInput").ap()
    rb1s = nc.dram_tensor("rb1s", [128, 8], F32, kind="ExternalInput").ap()
    rw2 = nc.dram_tensor("rw2", [D, 8], F32, kind="ExternalInput").ap()
    rb2 = nc.dram_tensor("rb2", [8, 1], F32, kind="ExternalInput").ap()
    we1 = nc.dram_tensor("we1", [8, D, H1], F32, kind="ExternalInput").ap()
    be1s = nc.dram_tensor("be1s", [8, 128, 32], F32, kind="ExternalInput").ap()
    we2 = nc.dram_tensor("we2", [8, H1, D], F32, kind="ExternalInput").ap()
    be2 = nc.dram_tensor("be2", [8, D], F32, kind="ExternalInput").ap()
    yout = nc.dram_tensor("yout", [D, TOK], F32, kind="ExternalOutput").ap()

    sb = lambda name, shape, dt=BF16: stack.enter_context(nc.sbuf_tensor(name, shape, dt))

    ysrc = env["stream"]
    big = sb("big", [128, 16384])
    t_v = big[:].rearrange("p (a b) -> p a b", b=TOK)
    hhb = t_v
    rw1t = sb("rw1t", [128, 2, 8, 128])
    rw2_sb = sb("rw2_sb", [128, 8, 8])
    rb1c = sb("rb1c", [128, 8], F32)
    rb2c = sb("rb2c", [8, 1], F32)
    lg_sb = sb("lg_sb", [8, 512])
    lgt = sb("lgt", [128, 4, 8], F32)
    nm1 = sb("nm1", [128, 1], F32)
    nm2 = sb("nm2", [128, 1], F32)
    eq8 = sb("eq8", [128, 8], F32)
    l2 = sb("l2", [128, 8], F32)
    msk8 = sb("msk8", [128, 8], F32)
    e8 = sb("e8", [128, 8], F32)
    wr8 = sb("wr8", [128, 8], F32)
    ssum = sb("ssum", [128, 1], F32)
    rs = sb("rs", [128, 1], F32)
    wtok = sb("wtok", [128, 8])
    w8 = sb("w8", [8, TOK])
    wrows = sb("wrows", [1, 8, TOK])
    be1c = sb("be1c", [128, 32], F32)
    be2_sb = sb("be2_sb", [8, 8, 128])
    w1t = sb("w1t", [128, 2, 8, 128])
    w2t = sb("w2t", [128, 2, 8, 128])
    bcm4 = sb("bcm4", [128, 4, 512])
    out32 = sb("out32", [128, 8, 2048], F32)
    y4sb = big[:, 0:8192].bitcast(F32).rearrange("p (a b) -> p a b", b=512)

    cG, cY, cP, cT, cV, cA = env["ctr"]
    PB0, PB1 = env["PB"]
    PF = env["PF"]
    ALL = env["ALL"]
    ENG = env["ENG"]
    id128 = env["id128"]
    ones1, onescol_f = env["ones1"], env["onescol_f"]
    ysq = env["ysq"]
    MDL = dict(max_dma_last_dim=8192)

    cG.tick(G.dma_start(out=rw2_sb[:],
                        in_=rw2.rearrange("(kc p) e -> p kc e", p=128)))
    cG.tick(G.dma_start(out=rb1c[:], in_=rb1s[:, :]))
    cG.tick(G.dma_start(out=rb2c[:], in_=rb2[:, :]))
    cG.tick(G.dma_start(out=be2_sb[:],
                        in_=be2.rearrange("e (dc m) -> e dc m", m=128)))
    fence(ENG, ALL)

    for mc in range(8):
        buf = mc % 2
        fence([G], [cT])
        g_w = cG.tick(G.dma_start(
            out=rw1t[:, buf, :, :],
            in_=rw1.rearrange("(kc p) d -> p kc d", p=128)[:, :, 128 * mc:128 * (mc + 1)],
            **MDL))
        T.wait_ge(cG.sem, g_w)
        for g in range(4):
            pf = PF[(mc * 4 + g) % 2]
            fence([T], [cA])
            for kc in range(8):
                ins = T.matmul(pf[:], rw1t[:, buf, kc, :],
                               ysrc[:, kc, 512 * g:512 * (g + 1)],
                               start=(kc == 0), stop=(kc == 7))
            t_h = cT.tick(ins)
            A.wait_ge(cT.sem, t_h)
            cA.tick(A.activation(t_v[:, mc, 512 * g:512 * (g + 1)], pf[:],
                                 AF.Silu, bias=rb1c[:, mc:mc + 1]))
    fence(ENG, ALL)
    for g in range(4):
        off = 512 * g
        fence([T], [cA, cV])
        for kc in range(8):
            ins = T.matmul(PF[1][0:8, :], rw2_sb[:, kc, :], t_v[:, kc, off:off + 512],
                           start=(kc == 0), stop=(kc == 7))
        t_lg = cT.tick(ins)
        V.wait_ge(cT.sem, t_lg)
        cV.tick(V.tensor_scalar(lg_sb[:], PF[1][0:8, :], rb2c[:, 0:1], None, ALU.add))
        cV.wait(T)
        for t4 in range(4):
            ins = T.transpose(PB0[0:128, 8 * t4:8 * (t4 + 1)],
                              lg_sb[:, 128 * t4:128 * (t4 + 1)], id128[0:8, 0:8])
        t_tr = cT.tick(ins)
        V.wait_ge(cT.sem, t_tr)
        cV.tick(V.tensor_copy(lgt[:], PB0[0:128, 0:32].rearrange("p (a b) -> p a b", b=8)))
        for t4 in range(4):
            cV.wait(V)
            cV.tick(V.reduce_max(nm1[:], lgt[:, t4, :], axis=AX.X, negate=True))
            cV.wait(V)
            cV.tick(V.tensor_scalar(eq8[:], lgt[:, t4, :], nm1[:, 0:1], 0.0,
                                    ALU.add, ALU.is_equal))
            cV.wait(V)
            cV.tick(V.scalar_tensor_tensor(l2[:], eq8[:], -1e9, lgt[:, t4, :],
                                           ALU.mult, ALU.add))
            cV.wait(V)
            cV.tick(V.reduce_max(nm2[:], l2[:], axis=AX.X, negate=True))
            cV.wait(V)
            cV.tick(V.tensor_scalar(msk8[:], lgt[:, t4, :], nm2[:, 0:1], 0.0,
                                    ALU.add, ALU.is_ge))
            cV.wait(A)
            cA.tick(A.activation(e8[:], lgt[:, t4, :], AF.Exp, bias=nm1[:, 0:1]))
            cA.wait(V)
            cV.tick(V.tensor_mul(wr8[:], e8[:], msk8[:]))
            cV.wait(V)
            cV.tick(V.reduce_sum(ssum[:], wr8[:], axis=AX.X))
            cV.wait(V)
            cV.tick(V.reciprocal(rs[:], ssum[:]))
            cV.wait(V)
            cV.tick(V.tensor_scalar(wtok[:], wr8[:], rs[:, 0:1], None, ALU.mult))
            cV.wait(T)
            t_f = cT.tick(T.transpose(PB1[0:8, 128 * t4:128 * (t4 + 1)], wtok[:], id128[:]))
            for e in range(8):
                ins = T.transpose(PB0[0:1, 128 * e:128 * (e + 1)],
                                  wtok[:, e:e + 1], id128[:])
            t_r = cT.tick(ins)
            V.wait_ge(cT.sem, t_r)
            cV.tick(V.tensor_copy(wrows[0:1, :, off + 128 * t4:off + 128 * (t4 + 1)],
                                  PB0[0:1, 0:1024].rearrange("p (a b) -> p a b", b=128)))
        V.wait_ge(cT.sem, cT.n)
        cV.tick(V.tensor_copy(w8[:, off:off + 512], PB1[0:8, 0:512]))
        fence([T], [cV])
    fence(ENG, ALL)

    P = dict(W1e=[we1[e] for e in range(8)], W2e=[we2[e] for e in range(8)],
             b1s=be1s, b2_sb=be2_sb, m8=w8, mrows=wrows,
             w1t=w1t, w2t=w2t, hhb=hhb, bcm4=bcm4, b1c=be1c, out32=out32,
             ysrc=ysrc, act=AF.Silu, HC=32)
    emit_moe_experts(nc, env, P, 2.0, y4sb, yout, False)


def _stage_D(inputs, y3_shards):
    rb1s = np.ascontiguousarray(inputs["L1_ffn_rb1"].reshape(8, 128).T)
    be1s = np.ascontiguousarray(
        inputs["L1_ffn_be1"].reshape(8, 32, 128).transpose(0, 2, 1))
    maps = []
    for c in range(8):
        maps.append({
            "y3": None if y3_shards[c] is None else np.asarray(y3_shards[c]),
            "rw1": inputs["L1_ffn_rw1"],
            "rb1s": rb1s,
            "rw2": inputs["L1_ffn_rw2"],
            "rb2": inputs["L1_ffn_rb2"].reshape(8, 1),
            "we1": inputs["L1_ffn_we1"],
            "be1s": be1s,
            "we2": inputs["L1_ffn_we2"],
            "be2": inputs["L1_ffn_be2"],
        })
    return maps






def _stage_A(inputs):
    x0 = inputs["L0_attn_x"]
    maps = []
    for c in range(8):
        b, h = c // 2, c % 2
        start = 2048 * h
        seg = np.zeros((TOKH, D), np.float32)
        lo = start - 128
        off = 128 if h == 0 else 0
        seg[off:] = x0[b, max(lo, 0):start + 2048]
        valid = np.zeros(TOKH, np.float32)
        valid[off:] = 1.0
        maps.append({
            "x0T": np.ascontiguousarray(seg.T),
            "validT": np.ascontiguousarray(valid.reshape(17, 128).T),
            "Wq": inputs["L0_attn_Wq"],
            "Wk": inputs["L0_attn_Wk"],
            "Wv": inputs["L0_attn_Wv"],
            "Wo": inputs["L0_attn_Wo"],
        })
    return maps





def build_launch_AB(nc):
    from contextlib import ExitStack
    stack = ExitStack()
    with stack:
        env = mk_env(nc, stack)
        with ExitStack() as sA:
            emit_A(nc, sA, env)
        with ExitStack() as sB:
            emit_B(nc, sB, env)
        cY = env["ctr"][1]
        env["eng"][4].wait_ge(cY.sem, cY.n)
    return nc


def build_launch_CD(nc):
    from contextlib import ExitStack
    stack = ExitStack()
    with stack:
        env = mk_env(nc, stack)
        with ExitStack() as sC:
            emit_C(nc, sC, env)
        with ExitStack() as sD:
            emit_D(nc, sD, env)
        cY = env["ctr"][1]
        env["eng"][4].wait_ge(cY.sem, cY.n)
    return nc


def run_launch_AB(inputs):
    nc = bass.Bass("TRN2", target_bir_lowering=False)
    build_launch_AB(nc)
    mapsA = _stage_A(inputs)
    mapsB = _stage_B(inputs, [None] * 8)
    maps = []
    for c in range(8):
        m = dict(mapsA[c])
        mB = dict(mapsB[c]); mB.pop("y1")
        m.update(mB)
        maps.append(m)
    res = run_bass_kernel_spmd(nc, maps, core_ids=list(range(8)))
    return [r["y2"] for r in res.results]


def run_launch_CD(inputs, y2_shards):
    nc = bass.Bass("TRN2", target_bir_lowering=False)
    build_launch_CD(nc)
    mapsC = _stage_C(inputs, y2_shards)
    mapsD = _stage_D(inputs, [None] * 8)
    maps = []
    for c in range(8):
        m = dict(mapsC[c])
        mD = dict(mapsD[c]); mD.pop("y3")
        m.update(mD)
        maps.append(m)
    res = run_bass_kernel_spmd(nc, maps, core_ids=list(range(8)))
    return [r["yout"] for r in res.results]


def kernel(**inputs):
    y2 = run_launch_AB(inputs)
    y4 = run_launch_CD(inputs, y2)
    out = np.empty((4, 4096, 1024), np.float32)
    for c in range(8):
        out[c // 2, 2048 * (c % 2):2048 * (c % 2 + 1), :] = np.asarray(y4[c]).T
    return out


if __name__ == "__main__":
    pass
